# version 17
# baseline (speedup 1.0000x reference)
import contextlib
import numpy as np

import concourse.bass as bass
import concourse.mybir as mybir
from concourse.ap import AP
from concourse.bass_utils import run_bass_kernel_spmd

F32 = mybir.dt.float32
BF16 = mybir.dt.bfloat16
I32 = mybir.dt.int32
AF = mybir.ActivationFunctionType
ALU = mybir.AluOpType

NCORES = 8
D = 1024
H = 16
HD = 64
SLAB = 512
NE = 64
NL = 8
PSL = 32
NP = NE * PSL

DEBUG_OUTS = []
PHASES = 'all'


class Sem:
    def __init__(self, ctx, nc, name):
        self.h = ctx.enter_context(nc.semaphore(name))
        self.n = 0

    def inc(self, inst, k=1):
        inst.then_inc(self.h, k)
        self.n += k
        return inst

    def wait(self, eng, n=None):
        eng.wait_ge(self.h, self.n if n is None else n)


def _rep(tile_ap, offset, dims, pcount=128):
    t = tile_ap.tensor
    pitch = int(t.shape[-1])
    return AP(t, offset, [[pitch, pcount]] + [list(d) for d in dims])


def build_nc(N=4096, debug=()):
    nc = bass.Bass("TRN2", target_bir_lowering=False,
                   detect_race_conditions=False)
    NT = N // 128
    NSLAB = N // SLAB
    NCH = N // 128

    x_in = nc.dram_tensor("x", [N, D], F32, kind="ExternalInput")
    wq_in = nc.dram_tensor("wq", [D, D], F32, kind="ExternalInput")
    wk_in = nc.dram_tensor("wk", [D, D], F32, kind="ExternalInput")
    wv_in = nc.dram_tensor("wv", [D, D], F32, kind="ExternalInput")
    wo_in = nc.dram_tensor("wo", [D, D], F32, kind="ExternalInput")
    wf_in = nc.dram_tensor("wf", [H, D], F32, kind="ExternalInput")
    wg1_in = nc.dram_tensor("wg1", [64, D], F32, kind="ExternalInput")
    wg2_in = nc.dram_tensor("wg2", [D, 64], F32, kind="ExternalInput")
    lnw_in = nc.dram_tensor("lnw", [D], F32, kind="ExternalInput")
    phi_in = nc.dram_tensor("phi", [D, NP], F32, kind="ExternalInput")
    wffn_in = nc.dram_tensor("wffn", [NL, D, D], F32, kind="ExternalInput")
    bffn_in = nc.dram_tensor("bffn", [NL, D], F32, kind="ExternalInput")
    dynoffs_in = nc.dram_tensor("dynoffs", [1, 2], I32, kind="ExternalInput")
    out_t = nc.dram_tensor("out", [N, D], F32, kind="ExternalOutput")

    h_xn = nc.dram_tensor("h_xn", [N, D], BF16)
    h_q = nc.dram_tensor("h_q", [N, H * 128], BF16)
    h_k = nc.dram_tensor("h_k", [N, H * 128], BF16)
    h_g1 = nc.dram_tensor("h_g1", [N, 128], BF16)
    h_oln = nc.dram_tensor("h_oln", [N, D], BF16)
    h_s = nc.dram_tensor("h_s", [N, D], BF16)
    h_E = nc.dram_tensor("h_E", [N, NP], BF16)
    h_Z = nc.dram_tensor("h_Z", [N, NP], BF16, addr_space="Shared")
    h_slots = nc.dram_tensor("h_slots", [D, NP], BF16)
    h_slots_all = nc.dram_tensor("h_slots_all", [NCORES * D, NP], BF16,
                                 addr_space="Shared")
    h_y = nc.dram_tensor("h_y", [NL * 256, D], BF16)
    h_y_all = nc.dram_tensor("h_y_all", [NCORES * NL * 256, D], BF16,
                             addr_space="Shared")
    h_smine = nc.dram_tensor("h_smine", [NCORES * D, NL * PSL], BF16)
    h_ymine = nc.dram_tensor("h_ymine", [NP, D], BF16)
    h_wqb = nc.dram_tensor("h_wqb", [D, D], BF16)
    h_wkb = nc.dram_tensor("h_wkb", [D, D], BF16)
    h_wvb = nc.dram_tensor("h_wvb", [D, D], BF16)
    h_wob = nc.dram_tensor("h_wob", [D, D], BF16)
    h_wfb = nc.dram_tensor("h_wfb", [H, D], BF16)
    h_wg1b = nc.dram_tensor("h_wg1b", [64, D], BF16)
    h_wg2b = nc.dram_tensor("h_wg2b", [D, 128], BF16)

    dbg = {}
    dbg_src = {"xn_early": h_xn, "xn": h_xn, "q": h_q, "k": h_k, "g1": h_g1, "oln": h_oln,
               "s": h_s, "E": h_E, "Z": h_Z, "slots": h_slots, "y": h_y,
               "smine": h_smine, "ymine": h_ymine, "yall": h_y_all}
    for name in debug:
        hs = dbg_src[name]
        dbg[name] = nc.dram_tensor("dbg_" + name, [int(hs.shape[0]), int(hs.shape[1])],
                                   F32, kind="ExternalOutput")

    with contextlib.ExitStack() as top:
        SBt = lambda *a: top.enter_context(nc.sbuf_tensor(*a))
        T0, V0, S0_, G0, Y0 = nc.tensor, nc.vector, nc.scalar, nc.gpsimd, nc.sync

        class EngTrack:
            def __init__(self, eng):
                self.eng = eng
                self.last = None

            def __getattr__(self, name):
                fn = getattr(self.eng, name)
                if callable(fn) and name not in ("wait_ge",):
                    def wrap(*a, **k):
                        inst = fn(*a, **k)
                        self.last = inst
                        return inst
                    return wrap
                return fn

            def wait_ge(self, sem, n):
                return self.eng.wait_ge(sem, n)

        T, V, S, G, Y = (EngTrack(T0), EngTrack(V0), EngTrack(S0_), EngTrack(G0),
                         EngTrack(Y0))

        u_bf = SBt("u_bf", [128, 128], BF16)
        lst_bf = SBt("lst_bf", [128, 128], BF16)
        lstpos = SBt("lstpos", [128, 128], BF16)
        id_bf = SBt("id_bf", [128, 128], BF16)
        ones_bf = SBt("ones_bf", [1, 128], BF16)
        lnw_sb = SBt("lnw_sb", [128, 8], F32)
        rs_all = SBt("rs_all", [128, NT], F32)
        rc_all = SBt("rc_all", [128, NT], F32)
        rc_tmp = SBt("rc_tmp", [128, NT], F32)
        eps_col = SBt("eps_col", [128, 1], F32)
        eps24_col = SBt("eps24_col", [128, 1], F32)
        vj1 = SBt("vj1", [128, 16], F32)
        vj2 = SBt("vj2", [128, 16], F32)
        wfT = SBt("wfT", [128, 8 * H], BF16)
        wg1T = SBt("wg1T", [128, 8 * 64], BF16)
        wg2T = SBt("wg2T", [128, D], BF16)
        watt = SBt("watt", [128, 4 * 8 * D], BF16)
        wqT = lambda ic: watt[:, (0 * 8 + ic) * D:(0 * 8 + ic + 1) * D]
        wkT = lambda ic: watt[:, (1 * 8 + ic) * D:(1 * 8 + ic + 1) * D]
        wvT = lambda ic: watt[:, (2 * 8 + ic) * D:(2 * 8 + ic + 1) * D]
        woT = lambda ic: watt[:, (3 * 8 + ic) * D:(3 * 8 + ic + 1) * D]
        xnf = SBt("xnf", [128, 8 * SLAB], BF16)
        xnf2 = SBt("xnf2", [128, 8 * SLAB], BF16)

        s_g = Sem(top, nc, "s_g")
        s_ld = Sem(top, nc, "s_ld")
        s_st = Sem(top, nc, "s_st")
        s_tr = Sem(top, nc, "s_tr")
        s_pe = Sem(top, nc, "s_pe")
        s_ve = Sem(top, nc, "s_ve")
        s_sc = Sem(top, nc, "s_sc")
        s_cc = Sem(top, nc, "s_cc")
        s_out = Sem(top, nc, "s_out")
        s_x = Sem(top, nc, "s_x")

        def gstep(inst):
            s_g.inc(inst)
            G.wait_ge(s_g.h, s_g.n)

        gstep(G.memset(u_bf[:, :], 0.0))
        gstep(G.affine_select(out=u_bf[:, :], in_=u_bf[:, :], compare_op=ALU.is_gt,
                              fill=1.0, base=0, pattern=[[-1, 128]],
                              channel_multiplier=1))
        gstep(G.memset(lst_bf[:, :], 1.0))
        gstep(G.affine_select(out=lst_bf[:, :], in_=lst_bf[:, :], compare_op=ALU.is_gt,
                              fill=0.0, base=0, pattern=[[-1, 128]],
                              channel_multiplier=1))
        gstep(G.memset(lstpos[:, :], 1e30))
        gstep(G.affine_select(out=lstpos[:, :], in_=lstpos[:, :], compare_op=ALU.is_gt,
                              fill=0.0, base=0, pattern=[[-1, 128]],
                              channel_multiplier=1))
        gstep(G.memset(id_bf[:, :], 0.0))
        gstep(G.affine_select(out=id_bf[:, :], in_=id_bf[:, :], compare_op=ALU.not_equal,
                              fill=1.0, base=0, pattern=[[-1, 128]],
                              channel_multiplier=1))
        gstep(G.memset(ones_bf[0:1, :], 1.0))
        gstep(G.memset(eps_col[:, :], 1e-5))
        gstep(G.memset(eps24_col[:, :], 1e-24))
        gstep(G.memset(vj1[:, :], 0.0))
        gstep(G.memset(vj2[:, :], 0.0))

        s_st.inc(Y.dma_start(out=lnw_sb[:, :],
                             in_=AP(lnw_in.ap().tensor, 0, [[1, 128], [128, 8]]),
                             allow_slow_non_contiguous=True), 16)
        with contextlib.ExitStack() as p0:
            SB0 = lambda *a: p0.enter_context(nc.sbuf_tensor(*a))
            wnat = SB0("wnat", [128, 8 * D], BF16)
            Xb = [SB0(f"xld{i}", [128, D], F32) for i in range(4)]
            XO = [SB0(f"xnt{i}", [128, D], BF16) for i in range(2)]
            musum = [SB0(f"musum{i}", [128, 1], F32) for i in range(2)]
            ssc = [SB0(f"ssc{i}", [128, 1], F32) for i in range(2)]
            ln_t1 = [SB0(f"ln_t1{i}", [128, 1], F32) for i in range(2)]
            ln_t2 = [SB0(f"ln_t2{i}", [128, 1], F32) for i in range(2)]
            ln_r = [SB0(f"ln_r{i}", [128, 1], F32) for i in range(2)]
            ln_nm = [SB0(f"ln_nm{i}", [128, 1], F32) for i in range(2)]
            ln_v1 = [SB0(f"ln_v1{i}", [128, 1], F32) for i in range(2)]
            ln_v2 = [SB0(f"ln_v2{i}", [128, 1], F32) for i in range(2)]
            ln_v3 = SB0("ln_v30", [128, 1], F32)
            sq_junk = [SB0(f"sq_junk{i}", [128, D], BF16) for i in range(2)]
            phi_sb = SB0("phi_sb", [128, 8 * NP], BF16)
            e_sb = [SB0("e_sb", [128, NP], BF16), SB0("e_sb2", [128, NP], BF16)]
            pl = [p0.enter_context(nc.psum_tensor("pl", [128, NP], F32)),
                  p0.enter_context(nc.psum_tensor("pl2", [128, NP], F32))]

            s_ld.inc(G.dma_start(
                out=phi_sb[:, :].rearrange("p (ic n) -> p ic n", ic=8),
                in_=AP(phi_in.ap().tensor, 0, [[NP, 128], [128 * NP, 8], [1, NP]])), 16)
            cap_phi = s_ld.n

            lnw_row = SB0("lnw_row", [128, D], F32)
            wosc = SB0("wosc", [128, D], BF16)
            s_ld.inc(G.dma_start(out=lnw_row[:, :],
                                 in_=AP(lnw_in.ap().tensor, 0, [[0, 128], [1, D]])), 16)
            for wi, (src, hb) in enumerate([(wo_in, h_wob), (wq_in, h_wqb),
                                            (wk_in, h_wkb), (wv_in, h_wvb)]):
                s_ld.inc(G.dma_start(
                    out=wnat[:, :].rearrange("p (jc i) -> p jc i", jc=8),
                    in_=AP(src.ap().tensor, 0, [[D, 128], [128 * D, 8], [1, D]])), 16)
                Y.wait_ge(s_ld.h, s_ld.n)
                if wi == 0:
                    V.wait_ge(s_ld.h, s_ld.n)
                    for jc in range(8):
                        if jc > 0:
                            V.wait_ge(s_st.h, s_st.n)
                        s_ve.inc(V.tensor_mul(wosc[:, :], wnat[:, jc * D:(jc + 1) * D],
                                              lnw_row[:, :]))
                        Y.wait_ge(s_ve.h, s_ve.n)
                        s_st.inc(Y.dma_start(out=hb[jc * 128:(jc + 1) * 128, :],
                                             in_=wosc[:, :]), 16)
                else:
                    for jc in range(8):
                        s_st.inc(Y.dma_start(out=hb[jc * 128:(jc + 1) * 128, :],
                                             in_=wnat[:, jc * D:(jc + 1) * D]), 16)
                Y.wait_ge(s_st.h, s_st.n)
                for ic in range(8):
                    dst = [woT, wqT, wkT, wvT][wi](ic)
                    s_tr.inc(Y.dma_start(out=dst, in_=hb[:, ic * 128:(ic + 1) * 128],
                                         transpose=True), 16)
                G.wait_ge(s_st.h, s_st.n)
            s_ld.inc(G.dma_start(out=wnat[0:H, 0:D], in_=wf_in[:, :]), 16)
            Y.wait_ge(s_ld.h, s_ld.n)
            s_st.inc(Y.dma_start(out=h_wfb[:, :], in_=wnat[0:H, 0:D]), 16)
            Y.wait_ge(s_st.h, s_st.n)
            for ic in range(8):
                s_tr.inc(Y.dma_start(out=wfT[:, ic * H:(ic + 1) * H],
                                     in_=h_wfb[:, ic * 128:(ic + 1) * 128],
                                     transpose=True), 16)
            G.wait_ge(s_st.h, s_st.n)
            s_ld.inc(G.dma_start(out=wnat[0:64, 0:D], in_=wg1_in[:, :]), 16)
            Y.wait_ge(s_ld.h, s_ld.n)
            s_st.inc(Y.dma_start(out=h_wg1b[:, :], in_=wnat[0:64, 0:D]), 16)
            Y.wait_ge(s_st.h, s_st.n)
            for ic in range(8):
                s_tr.inc(Y.dma_start(out=wg1T[:, ic * 64:(ic + 1) * 64],
                                     in_=h_wg1b[:, ic * 128:(ic + 1) * 128],
                                     transpose=True), 16)
            G.wait_ge(s_st.h, s_st.n)
            wg2pad = SB0("wg2pad", [128, 8 * 128], BF16)
            gstep(G.memset(wg2pad[:, :], 0.0))
            s_ld.inc(G.dma_start(
                out=wg2pad[:, :].rearrange("p (jc i) -> p jc i", jc=8)[:, :, 0:64],
                in_=AP(wg2_in.ap().tensor, 0, [[64, 128], [128 * 64, 8], [1, 64]])), 16)
            Y.wait_ge(s_ld.h, s_ld.n)
            Y.wait_ge(s_g.h, s_g.n)
            for jc in range(8):
                s_st.inc(Y.dma_start(out=h_wg2b[jc * 128:(jc + 1) * 128, :],
                                     in_=wg2pad[:, jc * 128:(jc + 1) * 128]), 16)
            Y.wait_ge(s_st.h, s_st.n)
            s_tr.inc(Y.dma_start(out=wg2T[:, :], in_=h_wg2b[:, :], transpose=True), 16)

            cap_x = {}; cap_sq = {}; cap_sscv = {}; cap_chain = {}; cap_xo = {}
            cap_xnst = {}; cap_Est = {}; cap_rs = {}; cap_f1pe = {}
            sc_marks = {}
            cap_xnf = {}
            cap_f1mm = {}

            def f1_evac(m):
                pp = pl[m % 2]
                ee = e_sb[m % 2]
                S.wait_ge(s_pe.h, cap_f1pe[m])
                if m >= 2:
                    S.wait_ge(s_st.h, cap_Est[m - 2])
                s_sc.inc(S.activation(ee[:, :], pp[:, :], AF.Exp,
                                      accum_out=rs_all[:, m:m + 1]))
                sc_marks[m] = s_sc.n
                Y.wait_ge(s_sc.h, sc_marks[m])
                s_st.inc(Y.dma_start(out=h_E[m * 128:(m + 1) * 128, :], in_=ee[:, :]), 16)
                cap_Est[m] = s_st.n

            for mt in range(NT):
                sl, r = mt // 4, mt % 4
                p = mt % 2
                xt = Xb[mt % 4]
                xo = XO[p]
                if mt >= 4:
                    G.wait_ge(s_ve.h, cap_xo[mt - 4])
                s_x.inc(G.dma_start(out=xt[:, :], in_=x_in[mt * 128:(mt + 1) * 128, :]), 16)
                cap_x[mt] = s_x.n
                V.wait_ge(s_x.h, cap_x[mt])
                if mt >= 2:
                    V.wait_ge(s_sc.h, cap_chain[mt - 2])
                s_ve.inc(V.tensor_reduce(musum[p][:, :], xt[:, :],
                                         mybir.AxisListType.X, ALU.add))
                cap_mu = s_ve.n
                S.wait_ge(s_x.h, cap_x[mt])
                if mt >= 2:
                    S.wait_ge(s_ve.h, cap_sscv[mt - 2])
                s_sc.inc(S.activation(sq_junk[p][:, :], xt[:, :], AF.Square))
                cap_sq[mt] = s_sc.n
                V.wait_ge(s_sc.h, cap_sq[mt])
                s_ve.inc(V.tensor_reduce(ssc[p][:, :], sq_junk[p][:, :],
                                         mybir.AxisListType.X, ALU.add))
                cap_sscv[mt] = s_ve.n
                S.wait_ge(s_ve.h, cap_sscv[mt])
                if mt >= 2:
                    S.wait_ge(s_ve.h, cap_xo[mt - 2])
                S.activation(ln_t1[p][:, :], musum[p][:, :], AF.Identity, scale=1.0 / D)
                S.activation(ln_t2[p][:, :], ln_t1[p][:, :], AF.Square)
                S.activation(ln_v2[p][:, :], ln_t2[p][:, :], AF.Identity, scale=-1.0)
                S.activation(ln_nm[p][:, :], ln_t1[p][:, :], AF.Identity, scale=-1.0)
                S.activation(ln_v3[:, :], eps_col[:, :], AF.Identity)
                S.activation(ln_v1[p][:, :], ssc[p][:, :], AF.Identity, scale=1.0 / D,
                             bias=ln_v2[p][:, 0:1])
                S.activation(ln_t2[p][:, :], ln_v1[p][:, :], AF.Ln, bias=eps_col[:, 0:1])
                s_sc.inc(S.activation(ln_r[p][:, :], ln_t2[p][:, :], AF.Exp, scale=-0.5))
                cap_chain[mt] = s_sc.n
                V.wait_ge(s_sc.h, cap_chain[mt])
                if mt >= 2:
                    V.wait_ge(s_st.h, cap_xnst[mt - 2])
                s_ve.inc(V.tensor_scalar(xo[:, :], xt[:, :], ln_nm[p][:, 0:1],
                                         ln_r[p][:, 0:1], ALU.add, ALU.mult))
                cap_xo[mt] = s_ve.n
                Y.wait_ge(s_ve.h, cap_xo[mt])
                s_st.inc(Y.dma_start(out=h_xn[mt * 128:(mt + 1) * 128, :], in_=xo[:, :]), 16)
                cap_xnst[mt] = s_st.n

                if r == 3:
                    xnb = xnf if sl % 2 == 0 else xnf2
                    Y.wait_ge(s_st.h, cap_xnst[mt])
                    if sl > 1:
                        Y.wait_ge(s_pe.h, cap_f1mm[sl - 2])
                    for ic in range(8):
                        s_tr.inc(Y.dma_start(
                            out=xnb[:, ic * SLAB:(ic + 1) * SLAB],
                            in_=h_xn[sl * SLAB:(sl + 1) * SLAB, ic * 128:(ic + 1) * 128],
                            transpose=True), 16)
                    cap_xnf[sl] = s_tr.n
                    for r4 in range(4):
                        m2 = sl * 4 + r4
                        pp = pl[m2 % 2]
                        if r4 >= 2:
                            f1_evac(m2 - 2)
                        T.wait_ge(s_tr.h, cap_xnf[sl])
                        T.wait_ge(s_ld.h, cap_phi)
                        if m2 >= 2:
                            T.wait_ge(s_sc.h, sc_marks[m2 - 2])
                        for ic in range(8):
                            lhs = xnb[:, ic * SLAB + r4 * 128: ic * SLAB + (r4 + 1) * 128]
                            for nb in range(4):
                                T.matmul(pp[:, nb * 512:(nb + 1) * 512], lhs,
                                         phi_sb[:, ic * NP + nb * 512: ic * NP + (nb + 1) * 512],
                                         start=(ic == 0), stop=(ic == 7))
                        s_pe.inc(T.last)
                        cap_f1pe[m2] = s_pe.n
                    cap_f1mm[sl] = s_pe.n
                    f1_evac(sl * 4 + 2)
                    f1_evac(sl * 4 + 3)

            S.activation(ln_v3[:, :], eps_col[:, :], AF.Identity)
            S.activation(ln_v3[:, :], eps_col[:, :], AF.Identity)
            S.activation(rc_tmp[:, :], rs_all[:, :], AF.Ln)
            s_sc.inc(S.activation(rc_all[:, :], rc_tmp[:, :], AF.Exp, scale=-1.0))
            G.wait_ge(s_st.h, cap_Est[NT - 1])
            s_cc.inc(G.collective_compute(
                "AllReduce", ALU.add, replica_groups=[list(range(NCORES))],
                ins=[h_E.ap()], outs=[h_Z.ap()]))

        cap_f1_sc = s_sc.n
        cap_f1_pe = s_pe.n
        cap_f1_st = s_st.n
        cap_f1_ve = s_ve.n

        with contextlib.ExitStack() as aa:
            SBA = lambda *a: aa.enter_context(nc.sbuf_tensor(*a))
            k_bf = [SBA("k_bf0", [128, 4 * D], BF16), SBA("k_bf1", [128, 4 * D], BF16)]
            v_sl = [SBA("v_sl0", [128, 4 * D], BF16), SBA("v_sl1", [128, 4 * D], BF16)]
            gate_sl = SBA("gate_sl", [128, 4 * D], BF16)
            q_fm = SBA("q_fm", [128, H * SLAB], BF16)
            k_fm = SBA("k_fm", [128, H * SLAB], BF16)
            g1f = SBA("g1f", [128, SLAB], BF16)
            q_bf = SBA("q_bf", [128, H * 128], BF16)
            k_pad = SBA("k_pad", [128, H * 128], BF16)
            g1_bf = SBA("g1_bf", [128, 128], BF16)
            gt_raw = SBA("gt_raw", [128, 512], BF16)
            B1 = SBA("B1", [128, 2048], BF16)
            B2 = SBA("B2", [128, 2048], BF16)
            B3 = SBA("B3", [128, 2048], BF16)
            e1f = SBA("e1f", [128, 64], F32)
            e1g = SBA("e1g", [128, 64], F32)
            phif = SBA("phif", [128, 64], F32)
            phih = SBA("phih", [128, 64], BF16)
            phil = SBA("phil", [128, 64], BF16)
            ssn = SBA("ssn", [128, 64], F32)
            ssnb = SBA("ssnb", [128, 64], F32)
            rn_sl = SBA("rn_sl", [128, 64], F32)
            expF = SBA("expF", [128, H], F32)
            expK = SBA("expK", [128, H], F32)
            ekrn = SBA("ekrn", [128, H], F32)
            vpp = SBA("vpp", [128, D], BF16)
            dec_col = SBA("dec_col", [128, H], F32)
            SA_ = SBA("SA_", [64, H * HD], F32)
            SBf = SBA("SBf", [64, H * HD], F32)
            S_bf = SBA("S_bf", [64, H * HD], BF16)
            og = SBA("og", [128, D], BF16)
            ogg = SBA("ogg", [128, D], BF16)
            oln_bf = SBA("oln_bf", [128, D], BF16)
            olnf = SBA("olnf", [128, 8 * SLAB], BF16)
            musum = SBA("musumA", [128, 1], F32)
            ssc = SBA("sscA", [128, 1], F32)
            ln_t1 = SBA("ln_t1A", [128, 1], F32)
            ln_t2 = SBA("ln_t2A", [128, 1], F32)
            ln_r = SBA("ln_rA", [128, 1], F32)
            ln_nm = SBA("ln_nmA", [128, 1], F32)
            ln_v1 = SBA("ln_v1A", [128, 1], F32)
            ln_v2 = SBA("ln_v2A", [128, 1], F32)
            ln_v3 = SBA("ln_v3A", [128, 1], F32)
            pS = [aa.enter_context(nc.psum_tensor(f"pS{i}", [128, 512], F32))
                  for i in range(4)]
            pBig = aa.enter_context(nc.psum_tensor("pBigA", [128, 2048], F32))

            G.wait_ge(s_st.h, cap_f1_st)
            G.wait_ge(s_pe.h, cap_f1_pe)
            G.wait_ge(s_ve.h, cap_f1_ve)
            G.wait_ge(s_sc.h, cap_f1_sc)
            gstep(G.memset(SA_[:, :], 0.0))
            gstep(G.memset(S_bf[:, :], 0.0))
            gstep(G.memset(q_bf[:, :], 0.0))
            gstep(G.memset(k_pad[:, :], 0.0))
            gstep(G.memset(g1_bf[:, :], 0.0))
            T.wait_ge(s_sc.h, cap_f1_sc)
            S.wait_ge(s_g.h, s_g.n)
            V.wait_ge(s_g.h, s_g.n)

            bank_free = [cap_f1_sc] * 4
            cap_state = 0
            cap_expdv = 0
            cap_pmul = 0
            cap_combh0 = 0
            cap_vpp = 0
            cap_oh1 = 0
            cap_kv = 0
            cap_phiev = 0
            cap_flphi = 0
            cap_phimm = 0
            cap_e1g = 0
            cap_rn = 0
            cap_phcp = 0
            cap_s3ev2 = 0
            cap_s3st_last = 0
            cap_s3mm_last = 0
            cap_olnst = 0
            cap_gla_last = 0
            cap_gatemm = 0
            cap_trqk = 0
            cap_trg1 = 0
            qst = {}
            mmcap = {}
            s1last = {}
            evlast = {}
            xnfcap = {}
            gidx = [0]

            def issue_xnf(sl2):
                Y.wait_ge(s_pe.h, s1last.get(sl2 - 1, cap_f1_pe))
                for ic in range(8):
                    s_tr.inc(Y.dma_start(
                        out=xnf[:, ic * SLAB:(ic + 1) * SLAB],
                        in_=h_xn[sl2 * SLAB:(sl2 + 1) * SLAB, ic * 128:(ic + 1) * 128],
                        transpose=True), 16)
                xnfcap[sl2] = s_tr.n

            def s1_mm(sl2, c, kind):
                g = gidx[0]
                gidx[0] += 1
                bank = pS[g % 4]
                T.wait_ge(s_tr.h, xnfcap[sl2])
                T.wait_ge(s_sc.h, bank_free[g % 4])
                for ic in range(8):
                    lhs = xnf[:, ic * SLAB + c * 128: ic * SLAB + (c + 1) * 128]
                    st, sp = (ic == 0), (ic == 7)
                    if kind == "qlo":
                        T.matmul(bank[:, 0:512], lhs, wqT(ic)[:, 0:512], start=st, stop=sp)
                    elif kind == "qhi":
                        T.matmul(bank[:, 0:512], lhs, wqT(ic)[:, 512:1024], start=st, stop=sp)
                    elif kind == "klo":
                        T.matmul(bank[:, 0:512], lhs, wkT(ic)[:, 0:512], start=st, stop=sp)
                    elif kind == "khi":
                        T.matmul(bank[:, 0:512], lhs, wkT(ic)[:, 512:1024], start=st, stop=sp)
                    elif kind == "vlo":
                        T.matmul(bank[:, 0:512], lhs, wvT(ic)[:, 0:512], start=st, stop=sp)
                    elif kind == "vhi":
                        T.matmul(bank[:, 0:512], lhs, wvT(ic)[:, 512:1024], start=st, stop=sp)
                    elif kind == "f":
                        T.matmul(bank[:, 0:16], lhs, wfT[:, ic * H:(ic + 1) * H],
                                 start=st, stop=sp)
                    else:
                        T.matmul(bank[:, 0:64], lhs, wg1T[:, ic * 64:(ic + 1) * 64],
                                 start=st, stop=sp)
                s_pe.inc(T.last)
                mmcap[(sl2, c, kind)] = s_pe.n
                s1last[sl2] = s_pe.n
                return bank

            def s1_evac(sl2, c, kind, bank):
                par = sl2 % 2
                S.wait_ge(s_pe.h, mmcap[(sl2, c, kind)])
                if kind == "qlo" and (sl2, c - 1) in qst:
                    S.wait_ge(s_st.h, qst[(sl2, c - 1)])
                if kind == "qlo":
                    S.activation(_rep(q_bf[:, :], 0, [[128, 8], [1, 64]]),
                                 bank[:, 0:512], AF.Silu)
                elif kind == "qhi":
                    S.activation(_rep(q_bf[:, :], 8 * 128, [[128, 8], [1, 64]]),
                                 bank[:, 0:512], AF.Silu)
                elif kind == "klo":
                    S.activation(_rep(k_pad[:, :], 0, [[128, 8], [1, 64]]),
                                 bank[:, 0:512], AF.Silu)
                elif kind == "khi":
                    S.activation(_rep(k_pad[:, :], 8 * 128, [[128, 8], [1, 64]]),
                                 bank[:, 0:512], AF.Silu)
                    S.activation(k_bf[par][:, c * D:(c + 1) * D],
                                 _rep(k_pad[:, :], 0, [[128, H], [1, 64]]), AF.Copy)
                elif kind == "vlo":
                    S.activation(v_sl[par][:, c * D: c * D + 512],
                                 bank[:, 0:512], AF.Silu)
                elif kind == "vhi":
                    S.activation(v_sl[par][:, c * D + 512:(c + 1) * D],
                                 bank[:, 0:512], AF.Silu)
                elif kind == "f":
                    S.wait_ge(s_ve.h, cap_e1g)
                    S.activation(e1f[:, c * 16:(c + 1) * 16], bank[:, 0:16],
                                 AF.Exp, scale=-1.0)
                else:
                    S.copy(g1_bf[:, 0:64], bank[:, 0:64])
                s_sc.inc(S.last)
                b = None
                for i in range(4):
                    if pS[i] is bank:
                        b = i
                bank_free[b] = s_sc.n
                evlast[sl2] = s_sc.n

            def s1_stores(sl2, c):
                tt = sl2 * SLAB + c * 128
                G.wait_ge(s_sc.h, evlast[sl2])
                s_st.inc(G.dma_start(out=h_q[tt:tt + 128, :], in_=q_bf[:, :]), 16)
                s_st.inc(G.dma_start(out=h_k[tt:tt + 128, :], in_=k_pad[:, :]), 16)
                s_st.inc(G.dma_start(out=h_g1[tt:tt + 128, :], in_=g1_bf[:, :]), 16)
                qst[(sl2, c)] = s_st.n

            KA = ["qlo", "qhi", "klo", "khi"]
            KB = ["vlo", "vhi", "f", "g1"]

            def s1_batchA(sl2, c):
                banks = [s1_mm(sl2, c, k) for k in KA]
                for k, b in zip(KA, banks):
                    s1_evac(sl2, c, k, b)

            def s1_batchB(sl2, c):
                banks = [s1_mm(sl2, c, k) for k in KB]
                for k, b in zip(KB, banks):
                    s1_evac(sl2, c, k, b)
                s1_stores(sl2, c)

            def issue_trans(sl2):
                Y.wait_ge(s_st.h, qst[(sl2, 3)])
                Y.wait_ge(s_pe.h, cap_gla_last)
                Y.wait_ge(s_pe.h, cap_gatemm)
                s_tr.inc(Y.dma_start(out=g1f[:, :], in_=h_g1[sl2 * SLAB:(sl2 + 1) * SLAB, :],
                                     transpose=True), 16)
                trg1 = s_tr.n
                for h in range(H):
                    s_tr.inc(Y.dma_start(out=q_fm[:, h * SLAB:(h + 1) * SLAB],
                                         in_=h_q[sl2 * SLAB:(sl2 + 1) * SLAB,
                                                 h * 128:(h + 1) * 128],
                                         transpose=True), 16)
                    s_tr.inc(Y.dma_start(out=k_fm[:, h * SLAB:(h + 1) * SLAB],
                                         in_=h_k[sl2 * SLAB:(sl2 + 1) * SLAB,
                                                 h * 128:(h + 1) * 128],
                                         transpose=True), 16)
                return trg1, s_tr.n

            def issue_gate(sl2):
                nonlocal cap_gatemm
                T.wait_ge(s_tr.h, cap_trg1)
                T.wait_ge(s_sc.h, evlast[sl2])
                for r in range(4):
                    lhs = g1f[0:64, r * 128:(r + 1) * 128]
                    T.matmul(pS[r][:, 0:512], lhs, wg2T[0:64, 0:512],
                             start=True, stop=True)
                s_pe.inc(T.last)
                S.wait_ge(s_pe.h, s_pe.n)
                cap_gts = s_ve.n
                for r in range(4):
                    if r > 0:
                        S.wait_ge(s_ve.h, cap_gts)
                    s_sc.inc(S.activation(gt_raw[:, 0:512], pS[r][:, 0:512],
                                          AF.Tanh, scale=0.5))
                    V.wait_ge(s_sc.h, s_sc.n)
                    s_ve.inc(V.tensor_scalar(gate_sl[:, r * D:r * D + 512],
                                             gt_raw[:, 0:512],
                                             0.5, 0.5, ALU.mult, ALU.add))
                    cap_gts = s_ve.n
                T.wait_ge(s_sc.h, s_sc.n)
                T.wait_ge(s_ve.h, cap_gts)
                for r in range(4):
                    lhs = g1f[0:64, r * 128:(r + 1) * 128]
                    T.matmul(pS[r][:, 0:512], lhs, wg2T[0:64, 512:1024],
                             start=True, stop=True)
                s_pe.inc(T.last)
                cap_gatemm = s_pe.n
                S.wait_ge(s_pe.h, s_pe.n)
                for r in range(4):
                    if r > 0:
                        S.wait_ge(s_ve.h, cap_gts)
                    s_sc.inc(S.activation(gt_raw[:, 0:512], pS[r][:, 0:512],
                                          AF.Tanh, scale=0.5))
                    V.wait_ge(s_sc.h, s_sc.n)
                    s_ve.inc(V.tensor_scalar(gate_sl[:, r * D + 512:(r + 1) * D],
                                             gt_raw[:, 0:512],
                                             0.5, 0.5, ALU.mult, ALU.add))
                    cap_gts = s_ve.n
                for i in range(4):
                    bank_free[i] = s_sc.n

            def issue_fknorm(sl2):
                nonlocal cap_e1g, cap_rn, cap_phcp
                par = sl2 % 2
                V.wait_ge(s_sc.h, evlast[sl2])
                s_ve.inc(V.tensor_scalar_add(e1g[:, :], e1f[:, :], 1.0))
                cap_e1g = s_ve.n
                S.wait_ge(s_ve.h, cap_e1g)
                S.wait_ge(s_ve.h, cap_flphi)
                s_sc.inc(S.activation(phif[:, :], e1g[:, :], AF.Ln))
                cap_phif = s_sc.n
                V.wait_ge(s_sc.h, cap_rn)
                for c in range(4):
                    V.tensor_mul(B1[:, 0:1024], k_bf[par][:, c * D:(c + 1) * D],
                                 k_bf[par][:, c * D:(c + 1) * D])
                    V.tensor_reduce(ssn[:, c * 16:(c + 1) * 16],
                                    B1[:, 0:1024].rearrange("p (h w) -> p h w", h=H),
                                    mybir.AxisListType.X, ALU.add)
                s_ve.inc(V.last)
                cap_ssn = s_ve.n
                S.wait_ge(s_ve.h, cap_ssn)
                S.wait_ge(s_ve.h, cap_pmul)
                S.activation(ssnb[:, :], ssn[:, :], AF.Ln, bias=eps24_col[:, 0:1])
                s_sc.inc(S.activation(rn_sl[:, :], ssnb[:, :], AF.Exp, scale=-0.5))
                cap_rn = s_sc.n
                V.wait_ge(s_sc.h, cap_rn)
                V.wait_ge(s_pe.h, cap_phimm)
                V.tensor_copy(phih[:, :], phif[:, :])
                V.tensor_copy(vj1[:, :], vj2[:, :])
                s_ve.inc(V.tensor_sub(phil[:, :], phif[:, :], phih[:, :]))
                cap_phcp = s_ve.n

            trqk_caps = {}
            issue_xnf(0)
            for c in range(4):
                s1_batchA(0, c)
                s1_batchB(0, c)
            cap_trg1, cap_trqk = issue_trans(0)
            for c in range(4):
                trqk_caps[(0, c)] = cap_trqk
            issue_gate(0)
            issue_fknorm(0)

            for sl in range(NSLAB):
                t0s = sl * SLAB
                par = sl % 2
                fill = sl + 1 < NSLAB
                if fill:
                    issue_xnf(sl + 1)
                for c in range(4):
                    if c == 0:
                        T.wait_ge(s_ve.h, cap_s3ev2)
                        T.wait_ge(s_ve.h, cap_phcp)
                    T.wait_ge(s_ve.h, cap_state)
                    T.wait_ge(s_sc.h, cap_expdv)
                    T.matmul(pBig[:, 0:16], u_bf[:, :], phih[:, c * 16:(c + 1) * 16],
                             start=True, stop=False)
                    T.matmul(pBig[:, 0:16], u_bf[:, :], phil[:, c * 16:(c + 1) * 16],
                             start=False, stop=True)
                    T.matmul(pBig[:, 16:32], lst_bf[:, :], phih[:, c * 16:(c + 1) * 16],
                             start=True, stop=False)
                    T.matmul(pBig[:, 16:32], lst_bf[:, :], phil[:, c * 16:(c + 1) * 16],
                             start=False, stop=True)
                    s_pe.inc(T.last)
                    cap_phimm = s_pe.n
                    S.wait_ge(s_pe.h, cap_phimm)
                    S.activation(expF[:, :], pBig[:, 0:16], AF.Exp, scale=-1.0)
                    s_sc.inc(S.activation(expK[:, :], pBig[:, 16:32], AF.Exp,
                                          scale=-1.0))
                    cap_phiev = s_sc.n
                    if fill:
                        banksA = [s1_mm(sl + 1, c, k) for k in KA]
                    V.wait_ge(s_sc.h, cap_phiev)
                    V.tensor_mul(dec_col[:, :], expF[:, :], expK[:, :])
                    V.tensor_mul(ekrn[:, :], expK[:, :], rn_sl[:, c * 16:(c + 1) * 16])
                    V.tensor_copy(vj1[:, :], vj2[:, :])
                    s_ve.inc(V.tensor_tensor(
                        vpp[:, :].rearrange("p (h w) -> p h w", h=H),
                        v_sl[par][:, c * D:(c + 1) * D].rearrange("p (h w) -> p h w", h=H),
                        _rep(ekrn[:, :], 0, [[1, H], [0, HD]]), ALU.mult))
                    cap_vpp = s_ve.n
                    V.wait_ge(s_pe.h, cap_oh1)
                    for h in range(H):
                        V.tensor_scalar_mul(B3[:, h * 128:(h + 1) * 128], lst_bf[:, :],
                                            phif[:, c * 16 + h:c * 16 + h + 1])
                    s_ve.inc(V.last)
                    cap_flphi = s_ve.n
                    if fill:
                        for k, b in zip(KA, banksA):
                            s1_evac(sl + 1, c, k, b)
                        banksB = [s1_mm(sl + 1, c, k) for k in KB]
                    T.wait_ge(s_ve.h, cap_flphi)
                    T.wait_ge(s_sc.h, cap_phiev)
                    for h in range(H):
                        T.matmul(pBig[:, h * 128:(h + 1) * 128],
                                 B3[:, h * 128:(h + 1) * 128], u_bf[:, :],
                                 start=True, stop=False)
                        T.matmul(pBig[:, h * 128:(h + 1) * 128],
                                 id_bf[:, :], lstpos[:, :], start=False, stop=True)
                    s_pe.inc(T.last)
                    cap_dmm = s_pe.n
                    if fill:
                        for k, b in zip(KB, banksB):
                            s1_evac(sl + 1, c, k, b)
                        s1_stores(sl + 1, c)
                    S.wait_ge(s_pe.h, cap_dmm)
                    S.wait_ge(s_ve.h, cap_pmul)
                    s_sc.inc(S.activation(B2[:, :], pBig[:, :], AF.Exp, scale=-1.0))
                    cap_expdv = s_sc.n
                    T.wait_ge(s_sc.h, cap_expdv)
                    T.wait_ge(s_tr.h, trqk_caps[(sl, c)])
                    for h in range(H):
                        T.matmul(pBig[:, h * 128:(h + 1) * 128],
                                 k_fm[0:64, h * SLAB + c * 128: h * SLAB + (c + 1) * 128],
                                 q_fm[0:64, h * SLAB + c * 128: h * SLAB + (c + 1) * 128],
                                 start=True, stop=True)
                    s_pe.inc(T.last)
                    cap_gmm = s_pe.n
                    V.wait_ge(s_pe.h, cap_gmm)
                    V.tensor_tensor(
                        B1[:, :].rearrange("p (h w) -> p h w", h=H),
                        pBig[:, :].rearrange("p (h w) -> p h w", h=H),
                        _rep(rn_sl[:, :], c * 16, [[1, H], [0, 128]]), ALU.mult)
                    s_ve.inc(V.tensor_mul(B3[:, :], B1[:, :], B2[:, :]))
                    cap_pmul = s_ve.n
                    T.wait_ge(s_ve.h, cap_pmul)
                    for hp in range(8):
                        T.matmul(pBig[:, hp * 64:(hp + 1) * 64],
                                 B3[:, hp * 128:(hp + 1) * 128],
                                 v_sl[par][:, c * D + hp * 64: c * D + (hp + 1) * 64],
                                 start=True, stop=True)
                        T.matmul(pBig[:, 512 + hp * 64: 512 + (hp + 1) * 64],
                                 q_fm[0:64, hp * SLAB + c * 128: hp * SLAB + (c + 1) * 128],
                                 S_bf[0:64, hp * 64:(hp + 1) * 64],
                                 start=True, stop=True)
                    s_pe.inc(T.last)
                    cap_oh0 = s_pe.n
                    V.wait_ge(s_pe.h, cap_oh0)
                    V.wait_ge(s_sc.h, cap_expdv)
                    if c == 0:
                        V.wait_ge(s_st.h, cap_s3st_last)
                    V.tensor_tensor(
                        B1[:, 0:512].rearrange("p (h w) -> p h w", h=8),
                        pBig[:, 512:1024].rearrange("p (h w) -> p h w", h=8),
                        _rep(expF[:, :], 0, [[1, 8], [0, 64]]), ALU.mult)
                    V.tensor_add(og[:, 0:512], B1[:, 0:512], pBig[:, 0:512])
                    s_ve.inc(V.tensor_mul(ogg[:, 0:512], og[:, 0:512],
                                          gate_sl[:, c * D: c * D + 512]))
                    cap_combh0 = s_ve.n
                    for hp in range(8):
                        h = 8 + hp
                        T.matmul(pBig[:, 1024 + hp * 64: 1024 + (hp + 1) * 64],
                                 B3[:, h * 128:(h + 1) * 128],
                                 v_sl[par][:, c * D + h * 64: c * D + (h + 1) * 64],
                                 start=True, stop=True)
                        T.matmul(pBig[:, 1536 + hp * 64: 1536 + (hp + 1) * 64],
                                 q_fm[0:64, h * SLAB + c * 128: h * SLAB + (c + 1) * 128],
                                 S_bf[0:64, h * 64:(h + 1) * 64],
                                 start=True, stop=True)
                    s_pe.inc(T.last)
                    cap_oh1 = s_pe.n
                    cap_gla_last = s_pe.n
                    T.wait_ge(s_ve.h, cap_combh0)
                    T.wait_ge(s_ve.h, cap_vpp)
                    for h in range(H):
                        T.matmul(pBig[0:64, h * 64:(h + 1) * 64],
                                 k_bf[par][:, c * D + h * 64: c * D + (h + 1) * 64],
                                 vpp[:, h * 64:(h + 1) * 64], start=True, stop=True)
                    s_pe.inc(T.last)
                    cap_kv = s_pe.n
                    if fill:
                        Y.wait_ge(s_pe.h, cap_oh1)
                        Y.wait_ge(s_st.h, qst[(sl + 1, c)])
                        for h in range(H):
                            s_tr.inc(Y.dma_start(
                                out=q_fm[:, h * SLAB + c * 128: h * SLAB + (c + 1) * 128],
                                in_=h_q[(sl + 1) * SLAB + c * 128:
                                        (sl + 1) * SLAB + (c + 1) * 128,
                                        h * 128:(h + 1) * 128],
                                transpose=True), 16)
                            s_tr.inc(Y.dma_start(
                                out=k_fm[:, h * SLAB + c * 128: h * SLAB + (c + 1) * 128],
                                in_=h_k[(sl + 1) * SLAB + c * 128:
                                        (sl + 1) * SLAB + (c + 1) * 128,
                                        h * 128:(h + 1) * 128],
                                transpose=True), 16)
                        if c == 3:
                            Y.wait_ge(s_pe.h, cap_gatemm)
                            s_tr.inc(Y.dma_start(
                                out=g1f[:, :],
                                in_=h_g1[(sl + 1) * SLAB:(sl + 2) * SLAB, :],
                                transpose=True), 16)
                            cap_trg1 = s_tr.n
                        trqk_caps[(sl + 1, c)] = s_tr.n
                    V.wait_ge(s_pe.h, cap_oh1)
                    V.tensor_tensor(
                        B1[:, 0:512].rearrange("p (h w) -> p h w", h=8),
                        pBig[:, 1536:2048].rearrange("p (h w) -> p h w", h=8),
                        _rep(expF[:, :], 8, [[1, 8], [0, 64]]), ALU.mult)
                    V.tensor_add(og[:, 512:1024], B1[:, 0:512], pBig[:, 1024:1536])
                    s_ve.inc(V.tensor_mul(ogg[:, 512:1024], og[:, 512:1024],
                                          gate_sl[:, c * D + 512:(c + 1) * D]))
                    s_ve.inc(V.tensor_reduce(musum[:, :], ogg[:, :],
                                             mybir.AxisListType.X, ALU.add))
                    cap_mu = s_ve.n
                    S.wait_ge(s_ve.h, cap_mu)
                    s_sc.inc(S.activation(B2[:, 0:1024], ogg[:, :], AF.Square))
                    cap_sq = s_sc.n
                    V.wait_ge(s_sc.h, cap_sq)
                    s_ve.inc(V.tensor_reduce(ssc[:, :], B2[:, 0:1024],
                                             mybir.AxisListType.X, ALU.add))
                    cap_ssc = s_ve.n
                    S.wait_ge(s_ve.h, cap_ssc)
                    S.activation(ln_t1[:, :], musum[:, :], AF.Identity, scale=1.0 / D)
                    S.activation(ln_t2[:, :], ln_t1[:, :], AF.Square)
                    S.activation(ln_v2[:, :], ln_t2[:, :], AF.Identity, scale=-1.0)
                    S.activation(ln_nm[:, :], ln_t1[:, :], AF.Identity, scale=-1.0)
                    S.activation(ln_v3[:, :], eps_col[:, :], AF.Identity)
                    S.activation(ln_v1[:, :], ssc[:, :], AF.Identity, scale=1.0 / D,
                                 bias=ln_v2[:, 0:1])
                    S.activation(ln_t2[:, :], ln_v1[:, :], AF.Ln, bias=eps_col[:, 0:1])
                    s_sc.inc(S.activation(ln_r[:, :], ln_t2[:, :], AF.Exp, scale=-0.5))
                    cap_lnr = s_sc.n
                    V.wait_ge(s_sc.h, cap_lnr)
                    V.wait_ge(s_st.h, cap_olnst)
                    s_ve.inc(V.tensor_scalar(oln_bf[:, :], ogg[:, :], ln_nm[:, 0:1],
                                             ln_r[:, 0:1], ALU.add, ALU.mult))
                    cap_oln = s_ve.n
                    G.wait_ge(s_ve.h, cap_oln)
                    s_st.inc(G.dma_start(
                        out=h_oln[t0s + c * 128: t0s + (c + 1) * 128, :],
                        in_=oln_bf[:, :]), 16)
                    cap_olnst = s_st.n
                    V.wait_ge(s_pe.h, cap_kv)
                    V.tensor_tensor(
                        SBf[0:64, :].rearrange("p (h w) -> p h w", h=H),
                        SA_[0:64, :].rearrange("p (h w) -> p h w", h=H),
                        _rep(dec_col[:, :], 0, [[1, H], [0, HD]], pcount=64),
                        ALU.mult)
                    V.tensor_add(SA_[0:64, :], SBf[0:64, :], pBig[0:64, 0:1024])
                    s_ve.inc(V.tensor_copy(S_bf[0:64, :], SA_[0:64, :]))
                    cap_state = s_ve.n

                Y.wait_ge(s_st.h, cap_olnst)
                Y.wait_ge(s_pe.h, cap_s3mm_last)
                for ic in range(8):
                    s_tr.inc(Y.dma_start(
                        out=olnf[:, ic * SLAB:(ic + 1) * SLAB],
                        in_=h_oln[t0s:t0s + SLAB, ic * 128:(ic + 1) * 128],
                        transpose=True), 16)
                cap_trolnf = s_tr.n

                s3ev = {}
                s3st = {}
                for r in range(4):
                    off = (r % 2) * 1024
                    T.wait_ge(s_tr.h, cap_trolnf)
                    if r >= 2:
                        T.wait_ge(s_ve.h, s3ev[r - 2])
                    else:
                        T.wait_ge(s_ve.h, cap_state)
                    for ic in range(8):
                        lhs = olnf[:, ic * SLAB + r * 128: ic * SLAB + (r + 1) * 128]
                        T.matmul(pBig[:, off:off + 512], lhs, woT(ic)[:, 0:512],
                                 start=(ic == 0), stop=(ic == 7))
                        T.matmul(pBig[:, off + 512:off + 1024], lhs,
                                 woT(ic)[:, 512:1024], start=(ic == 0), stop=(ic == 7))
                    s_pe.inc(T.last)
                    cap_s3mm = s_pe.n
                    V.wait_ge(s_pe.h, cap_s3mm)
                    if r >= 1:
                        V.wait_ge(s_st.h, s3st[r - 1])
                    s_ve.inc(V.tensor_copy(og[:, :], pBig[:, off:off + 1024]))
                    s3ev[r] = s_ve.n
                    G.wait_ge(s_ve.h, s3ev[r])
                    s_st.inc(G.dma_start(
                        out=h_s[t0s + r * 128: t0s + (r + 1) * 128, :],
                        in_=og[:, :]), 16)
                    s3st[r] = s_st.n
                cap_s3mm_last = s_pe.n
                cap_s3ev2 = s3ev[2]
                cap_s3st_last = s3st[3]

                if fill:
                    issue_gate(sl + 1)
                    issue_fknorm(sl + 1)
        nc.all_engine_barrier()

        G.wait_ge(s_cc.h, s_cc.n)
        nc.all_engine_barrier()
        with contextlib.ExitStack() as f2:
            SB2 = lambda *a: f2.enter_context(nc.sbuf_tensor(*a))
            e_stream = SB2("e_stream", [128, NT * 512], BF16)
            z_stream = SB2("z_stream", [128, NT * 512], BF16)
            xn_t = [SB2("xn_t1", [128, D], BF16), SB2("xn_t2", [128, D], BF16)]
            rz_f = SB2("rz_f", [128, 512], F32)
            rz_g = [SB2("rz_g1", [128, 512], F32), SB2("rz_g2", [128, 512], F32)]
            disp = [SB2("disp1", [128, 512], BF16), SB2("disp2", [128, 512], BF16)]
            slots_fm = SB2("slots_fm", [128, 8 * NP], BF16)
            psl = [f2.enter_context(nc.psum_tensor(f"psl{i}", [128, 512], F32))
                   for i in range(8)]
            cap_rz = {}
            cap_dp = {}
            cap_xt = {}
            cap_mm = {}
            cap_ev = {}
            it = 0
            for sb in range(4):
                if sb > 0:
                    G.wait_ge(s_ve.h, cap_dp[it - 1])
                    G.wait_ge(s_sc.h, cap_rz[it - 1])
                s_ld.inc(G.dma_start(
                    out=e_stream[:, :].rearrange("p (m w) -> p m w", m=NT),
                    in_=AP(h_E.ap().tensor, sb * 512,
                           [[NP, 128], [128 * NP, NT], [1, 512]])), 16)
                s_ld.inc(G.dma_start(
                    out=z_stream[:, :].rearrange("p (m w) -> p m w", m=NT),
                    in_=AP(h_Z.ap().tensor, sb * 512,
                           [[NP, 128], [128 * NP, NT], [1, 512]])), 16)
                cap_eld = s_ld.n
                for mt in range(NT):
                    par = it % 2
                    xt = xn_t[par]
                    dp = disp[par]
                    rzg = rz_g[par]
                    if it >= 2:
                        Y.wait_ge(s_pe.h, cap_mm[it - 2])
                    s_st.inc(Y.dma_start(out=xt[:, :],
                                         in_=h_xn[mt * 128:(mt + 1) * 128, :]), 16)
                    cap_xt[it] = s_st.n
                    S.wait_ge(s_ld.h, cap_eld)
                    if it >= 2:
                        S.wait_ge(s_ve.h, cap_dp[it - 2])
                    S.activation(rz_f[:, :], z_stream[:, mt * 512:(mt + 1) * 512], AF.Ln)
                    s_sc.inc(S.activation(rzg[:, :], rz_f[:, :], AF.Exp, scale=-1.0))
                    cap_rz[it] = s_sc.n
                    V.wait_ge(s_sc.h, cap_rz[it])
                    if it >= 2:
                        V.wait_ge(s_pe.h, cap_mm[it - 2])
                    s_ve.inc(V.tensor_mul(dp[:, :],
                                          e_stream[:, mt * 512:(mt + 1) * 512], rzg[:, :]))
                    cap_dp[it] = s_ve.n
                    T.wait_ge(s_ve.h, cap_dp[it])
                    T.wait_ge(s_st.h, cap_xt[it])
                    if mt == 0 and sb > 0:
                        T.wait_ge(s_ve.h, cap_ev[sb - 1])
                    for dc in range(8):
                        T.matmul(psl[dc][:, :], xt[:, dc * 128:(dc + 1) * 128], dp[:, :],
                                 start=(mt == 0), stop=(mt == NT - 1))
                    s_pe.inc(T.last)
                    cap_mm[it] = s_pe.n
                    it += 1
                V.wait_ge(s_pe.h, cap_mm[it - 1])
                for dc in range(8):
                    V.tensor_copy(slots_fm[:, dc * NP + sb * 512: dc * NP + (sb + 1) * 512],
                                  psl[dc][:, :])
                s_ve.inc(V.last)
                cap_ev[sb] = s_ve.n
            Y.wait_ge(s_ve.h, s_ve.n)
            for dc in range(8):
                s_st.inc(Y.dma_start(out=h_slots[dc * 128:(dc + 1) * 128, :],
                                     in_=slots_fm[:, dc * NP:(dc + 1) * NP]), 16)
            G.wait_ge(s_st.h, s_st.n)
            s_cc.inc(G.collective_compute(
                "AllGather", ALU.bypass, replica_groups=[list(range(NCORES))],
                ins=[h_slots.ap()], outs=[h_slots_all.ap()]))

        nc.all_engine_barrier()

        with contextlib.ExitStack() as f3:
            SB3 = lambda *a: f3.enter_context(nc.sbuf_tensor(*a))
            sa = SB3("sa", [128, 8 * 2 * NL * 128], BF16)
            b_rows = SB3("b_rows", [1, NL * D], BF16)
            s_ld.inc(G.dma_start(out=b_rows[0:1, :],
                                 in_=AP(bffn_in.ap().tensor, 0,
                                        [[NL * D, 1], [1, NL * D]])), 16)
            wfa = SB3("wfa", [128, 8 * D], BF16)
            wfb = SB3("wfb", [128, 8 * D], BF16)
            y_ev = SB3("y_ev", [128, D], BF16)
            py1 = f3.enter_context(nc.psum_tensor("py1", [128, D], F32))
            py2 = f3.enter_context(nc.psum_tensor("py2", [128, D], F32))

            def load_wffn(nl):
                wb = wfa if nl % 2 == 0 else wfb
                s_ld.inc(G.dma_start(
                    out=wb[:, :].rearrange("p (dc e) -> p dc e", dc=8),
                    in_=AP(wffn_in.ap().tensor, nl * D * D,
                           [[D, 128], [128 * D, 8], [1, D]])), 16)
                ld_marks.append(s_ld.n)

            ld_marks = []
            load_wffn(0)
            load_wffn(1)
            G.wait_ge(s_cc.h, s_cc.n)
            with G.register("sar") as sareg:
                G.reg_load(sareg, dynoffs_in[0:1, 0:1])
                off = G.snap(sareg)
                s_ld.inc(G.dma_start(
                    out=h_smine.ap(),
                    in_=AP(h_slots_all.ap().tensor, off,
                           [[NP, NCORES * D], [1, 256]]),
                    bounds_check=None), 16)
            G.wait_ge(s_ld.h, s_ld.n)
            for b in range(NCORES):
                mh, rb = b // 4, b % 4
                for dc in range(8):
                    src_rows = h_smine[b * D + dc * 128: b * D + (dc + 1) * 128, :]
                    src3 = src_rows.rearrange("p (n c) -> p n c", c=32)
                    s_ld.inc(G.dma_start(
                        out=AP(sa[:, :].tensor,
                               (dc * 2 + mh) * NL * 128 + rb * 32,
                               [[8 * 2 * NL * 128, 128], [128, NL], [1, 32]]),
                        in_=src3), 16)
            cap_sa = s_ld.n
            T.wait_ge(s_ld.h, cap_sa)
            pe_f3 = s_pe.n
            for nl in range(NL):
                wb = wfa if nl % 2 == 0 else wfb
                if nl + 2 < NL:
                    G.wait_ge(s_pe.h, pe_f3 + 2 * nl + 2)
                    load_wffn(nl + 2)
                for mh in range(2):
                    py = py1 if mh == 0 else py2
                    T.wait_ge(s_ld.h, ld_marks[nl])
                    T.wait_ge(s_st.h, s_st.n)
                    if not (nl == 0 and mh == 0):
                        T.wait_ge(s_ve.h, s_ve.n)
                    lbase = (0 * 2 + mh) * NL + nl
                    for dc in range(8):
                        lhs = sa[:, ((dc * 2 + mh) * NL + nl) * 128:
                                 ((dc * 2 + mh) * NL + nl) * 128 + 128]
                        T.matmul(py[:, 0:512], lhs, wb[:, dc * D:dc * D + 512],
                                 start=(dc == 0), stop=False)
                        T.matmul(py[:, 512:1024], lhs, wb[:, dc * D + 512:(dc + 1) * D],
                                 start=(dc == 0), stop=False)
                    T.matmul(py[:, 0:512], ones_bf[0:1, :],
                             b_rows[0:1, nl * D:nl * D + 512], start=False, stop=True)
                    T.matmul(py[:, 512:1024], ones_bf[0:1, :],
                             b_rows[0:1, nl * D + 512:(nl + 1) * D], start=False, stop=True)
                    s_pe.inc(T.last)
                    V.wait_ge(s_pe.h, s_pe.n)
                    if not (nl == 0 and mh == 0):
                        V.wait_ge(s_st.h, s_st.n)
                    V.tensor_copy(y_ev[:, :], py[:, :])
                    s_ve.inc(V.last)
                    Y.wait_ge(s_ve.h, s_ve.n)
                    s_st.inc(Y.dma_start(
                        out=h_y[nl * 256 + mh * 128: nl * 256 + (mh + 1) * 128, :],
                        in_=y_ev[:, :]), 16)
            G.wait_ge(s_st.h, s_st.n)
            s_cc.inc(G.collective_compute(
                "AllGather", ALU.bypass, replica_groups=[list(range(NCORES))],
                ins=[h_y.ap()], outs=[h_y_all.ap()]))

        nc.all_engine_barrier()

        with contextlib.ExitStack() as f4:
            SB4 = lambda *a: f4.enter_context(nc.sbuf_tensor(*a))
            y_b = SB4("y_b", [128, 16 * D], BF16)
            et = SB4("et", [128, 16 * 1024], BF16)
            g_bf = SB4("g_bf", [128, D], BF16)
            s_t = SB4("s_t", [128, D], BF16)
            x_t = SB4("x_t", [128, D], F32)
            t2 = SB4("t2", [128, D], BF16)
            outf = SB4("outf", [128, D], F32)
            po1 = f4.enter_context(nc.psum_tensor("po1", [128, D], F32))
            po2 = f4.enter_context(nc.psum_tensor("po2", [128, D], F32))
            MS = min(1024, N)
            for npc in range(16):
                s_tr.inc(Y.dma_start(
                    out=et[:, npc * 1024:npc * 1024 + MS],
                    in_=h_E[0:MS, npc * 128:(npc + 1) * 128],
                    transpose=True), 16)
            G.wait_ge(s_cc.h, s_cc.n)
            with G.register("ybr") as ybreg:
                G.reg_load(ybreg, dynoffs_in[0:1, 1:2])
                bboff = G.snap(ybreg)
                for i in range(NCORES):
                    view = h_y_all[i * 2048:(i + 1) * 2048, :].rearrange(
                        "(nl bb p) e -> nl bb (p e)", nl=NL, bb=NCORES)
                    s_ld.inc(G.dma_start(
                        out=h_ymine[i * 256:(i + 1) * 256, :].rearrange(
                            "(nl p) e -> nl (p e)", nl=NL).unsqueeze(1),
                        in_=view[:, bass.ds(bboff, 1), :],
                        bounds_check=None), 16)
            G.wait_ge(s_ld.h, s_ld.n)
            for npc in range(16):
                s_ld.inc(G.dma_start(
                    out=y_b[:, npc * D:(npc + 1) * D],
                    in_=h_ymine[npc * 128:(npc + 1) * 128, :]), 16)
            for ms in range(N // MS):
                if ms > 0:
                    Y.wait_ge(s_pe.h, s_pe.n)
                    for npc in range(16):
                        s_tr.inc(Y.dma_start(
                            out=et[:, npc * 1024:npc * 1024 + MS],
                            in_=h_E[ms * MS:(ms + 1) * MS, npc * 128:(npc + 1) * 128],
                            transpose=True), 16)
                for rr in range(MS // 128):
                    mt = ms * (MS // 128) + rr
                    po = po1 if mt % 2 == 0 else po2
                    T.wait_ge(s_tr.h, s_tr.n)
                    T.wait_ge(s_ld.h, s_ld.n)
                    if mt >= 2:
                        T.wait_ge(s_sc.h, s_sc.n)
                    for npc in range(16):
                        lhs = et[:, npc * 1024 + rr * 128: npc * 1024 + (rr + 1) * 128]
                        T.matmul(po[:, 0:512], lhs, y_b[:, npc * D:npc * D + 512],
                                 start=(npc == 0), stop=(npc == 15))
                        T.matmul(po[:, 512:1024], lhs, y_b[:, npc * D + 512:(npc + 1) * D],
                                 start=(npc == 0), stop=(npc == 15))
                    s_pe.inc(T.last)
                    S.wait_ge(s_pe.h, s_pe.n)
                    if mt >= 1:
                        S.wait_ge(s_ve.h, s_ve.n)
                    s_sc.inc(S.activation(g_bf[:, :], po[:, :], AF.Silu,
                                          scale=rc_all[:, mt:mt + 1]))
                    s_st.inc(Y.dma_start(out=s_t[:, :],
                                         in_=h_s[mt * 128:(mt + 1) * 128, :]), 16)
                    s_st.inc(Y.dma_start(out=x_t[:, :],
                                         in_=x_in[mt * 128:(mt + 1) * 128, :]), 16)
                    V.wait_ge(s_sc.h, s_sc.n)
                    V.wait_ge(s_st.h, s_st.n)
                    V.tensor_mul(t2[:, :], s_t[:, :], g_bf[:, :])
                    s_ve.inc(V.tensor_add(outf[:, :], x_t[:, :], t2[:, :]))
                    Y.wait_ge(s_ve.h, s_ve.n)
                    s_out.inc(Y.dma_start(out=out_t[mt * 128:(mt + 1) * 128, :],
                                          in_=outf[:, :]), 16)
            Y.wait_ge(s_out.h, s_out.n)

        if dbg:
            nc.all_engine_barrier()
            with contextlib.ExitStack() as dd:
                cp = dd.enter_context(nc.sbuf_tensor("dbgcp", [128, NP], BF16))
                cpf = dd.enter_context(nc.sbuf_tensor("dbgcf", [128, NP], F32))
                for name, dst in dbg.items():
                    hs = dbg_src[name]
                    wid = int(hs.shape[1])
                    for mt in range(int(hs.shape[0]) // 128):
                        Y.wait_ge(s_ve.h, s_ve.n)
                        s_st.inc(Y.dma_start(out=cp[:, 0:wid],
                                             in_=hs[mt * 128:(mt + 1) * 128, :]), 16)
                        V.wait_ge(s_st.h, s_st.n)
                        s_ve.inc(V.tensor_copy(cpf[:, 0:wid], cp[:, 0:wid]))
                        Y.wait_ge(s_ve.h, s_ve.n)
                        s_out.inc(Y.dma_start(out=dst[mt * 128:(mt + 1) * 128, :],
                                              in_=cpf[:, 0:wid]), 16)
                Y.wait_ge(s_out.h, s_out.n)

        nc.all_engine_barrier()
        Y.wait_ge(s_out.h, s_out.n)

    return nc


_NC_CACHE = {}


def kernel(L0_attn_x, L0_attn_Wq, L0_attn_Wk, L0_attn_Wv, L0_attn_Wf,
           L0_attn_Wg1, L0_attn_Wg2, L0_attn_ln_w, L0_attn_Wo,
           L0_ffn_phi, L0_ffn_W, L0_ffn_b):
    N = int(np.asarray(L0_attn_x).shape[1])
    key = (N, tuple(DEBUG_OUTS))
    if key not in _NC_CACHE:
        _NC_CACHE[key] = build_nc(N=N, debug=tuple(DEBUG_OUTS))
    nc = _NC_CACHE[key]
    x = np.ascontiguousarray(np.asarray(L0_attn_x, dtype=np.float32))
    com = {
        "wq": np.ascontiguousarray(np.asarray(L0_attn_Wq, np.float32)),
        "wk": np.ascontiguousarray(np.asarray(L0_attn_Wk, np.float32)),
        "wv": np.ascontiguousarray(np.asarray(L0_attn_Wv, np.float32)),
        "wo": np.ascontiguousarray(np.asarray(L0_attn_Wo, np.float32)),
        "wf": np.ascontiguousarray(np.asarray(L0_attn_Wf, np.float32)),
        "wg1": np.ascontiguousarray(np.asarray(L0_attn_Wg1, np.float32)),
        "wg2": np.ascontiguousarray(np.asarray(L0_attn_Wg2, np.float32)),
        "lnw": np.ascontiguousarray(np.asarray(L0_attn_ln_w, np.float32)),
        "phi": np.ascontiguousarray(np.asarray(L0_ffn_phi, np.float32).reshape(D, NP)),
    }
    W = np.asarray(L0_ffn_W, np.float32)
    B = np.asarray(L0_ffn_b, np.float32)
    in_maps = []
    for c in range(NCORES):
        m = dict(com)
        m["x"] = np.ascontiguousarray(x[c])
        m["wffn"] = np.ascontiguousarray(W[c * NL:(c + 1) * NL])
        m["bffn"] = np.ascontiguousarray(B[c * NL:(c + 1) * NL])
        m["dynoffs"] = np.array([[256 * c, c]], dtype=np.int32)
        in_maps.append(m)
    res = run_bass_kernel_spmd(nc, in_maps, core_ids=list(range(NCORES)))
    out = np.stack([np.asarray(res.results[c]["out"]) for c in range(NCORES)], axis=0)
    return out


# version 19 (final)
# speedup vs baseline: 1.0286x; 1.0286x over previous
import contextlib
import numpy as np

import concourse.bass as bass
import concourse.mybir as mybir
from concourse.ap import AP
from concourse.bass_utils import run_bass_kernel_spmd

F32 = mybir.dt.float32
BF16 = mybir.dt.bfloat16
I32 = mybir.dt.int32
AF = mybir.ActivationFunctionType
ALU = mybir.AluOpType

NCORES = 8
D = 1024
H = 16
HD = 64
SLAB = 512
NE = 64
NL = 8
PSL = 32
NP = NE * PSL

DEBUG_OUTS = []
PHASES = 'all'


class Sem:
    def __init__(self, ctx, nc, name):
        self.h = ctx.enter_context(nc.semaphore(name))
        self.n = 0

    def inc(self, inst, k=1):
        inst.then_inc(self.h, k)
        self.n += k
        return inst

    def wait(self, eng, n=None):
        eng.wait_ge(self.h, self.n if n is None else n)


def _rep(tile_ap, offset, dims, pcount=128):
    t = tile_ap.tensor
    pitch = int(t.shape[-1])
    return AP(t, offset, [[pitch, pcount]] + [list(d) for d in dims])


def build_nc(N=4096, debug=()):
    nc = bass.Bass("TRN2", target_bir_lowering=False,
                   detect_race_conditions=False)
    NT = N // 128
    NSLAB = N // SLAB
    NCH = N // 128

    x_in = nc.dram_tensor("x", [N, D], F32, kind="ExternalInput")
    wq_in = nc.dram_tensor("wq", [D, D], F32, kind="ExternalInput")
    wk_in = nc.dram_tensor("wk", [D, D], F32, kind="ExternalInput")
    wv_in = nc.dram_tensor("wv", [D, D], F32, kind="ExternalInput")
    wo_in = nc.dram_tensor("wo", [D, D], F32, kind="ExternalInput")
    wf_in = nc.dram_tensor("wf", [H, D], F32, kind="ExternalInput")
    wg1_in = nc.dram_tensor("wg1", [64, D], F32, kind="ExternalInput")
    wg2_in = nc.dram_tensor("wg2", [D, 64], F32, kind="ExternalInput")
    lnw_in = nc.dram_tensor("lnw", [D], F32, kind="ExternalInput")
    phi_in = nc.dram_tensor("phi", [D, NP], F32, kind="ExternalInput")
    wffn_in = nc.dram_tensor("wffn", [NL, D, D], F32, kind="ExternalInput")
    bffn_in = nc.dram_tensor("bffn", [NL, D], F32, kind="ExternalInput")
    dynoffs_in = nc.dram_tensor("dynoffs", [1, 2], I32, kind="ExternalInput")
    out_t = nc.dram_tensor("out", [N, D], F32, kind="ExternalOutput")

    h_xn = nc.dram_tensor("h_xn", [N, D], BF16)
    h_q = nc.dram_tensor("h_q", [N, H * 128], BF16)
    h_k = nc.dram_tensor("h_k", [N, H * 128], BF16)
    h_g1 = nc.dram_tensor("h_g1", [N, 128], BF16)
    h_oln = nc.dram_tensor("h_oln", [N, D], BF16)
    h_s = nc.dram_tensor("h_s", [N, D], BF16)
    h_E = nc.dram_tensor("h_E", [N, NP], BF16)
    h_Z = nc.dram_tensor("h_Z", [N, NP], BF16, addr_space="Shared")
    h_slots = nc.dram_tensor("h_slots", [D, NP], BF16)
    h_slots_all = nc.dram_tensor("h_slots_all", [NCORES * D, NP], BF16,
                                 addr_space="Shared")
    h_y = nc.dram_tensor("h_y", [NL * 256, D], BF16)
    h_y_all = nc.dram_tensor("h_y_all", [NCORES * NL * 256, D], BF16,
                             addr_space="Shared")
    h_smine = nc.dram_tensor("h_smine", [NCORES * D, NL * PSL], BF16)
    h_ymine = nc.dram_tensor("h_ymine", [NP, D], BF16)
    h_wqb = nc.dram_tensor("h_wqb", [D, D], BF16)
    h_wkb = nc.dram_tensor("h_wkb", [D, D], BF16)
    h_wvb = nc.dram_tensor("h_wvb", [D, D], BF16)
    h_wob = nc.dram_tensor("h_wob", [D, D], BF16)
    h_wfb = nc.dram_tensor("h_wfb", [H, D], BF16)
    h_wg1b = nc.dram_tensor("h_wg1b", [64, D], BF16)
    h_wg2b = nc.dram_tensor("h_wg2b", [D, 128], BF16)

    dbg = {}
    dbg_src = {"xn_early": h_xn, "xn": h_xn, "q": h_q, "k": h_k, "g1": h_g1, "oln": h_oln,
               "s": h_s, "E": h_E, "Z": h_Z, "slots": h_slots, "y": h_y,
               "smine": h_smine, "ymine": h_ymine, "yall": h_y_all}
    for name in debug:
        hs = dbg_src[name]
        dbg[name] = nc.dram_tensor("dbg_" + name, [int(hs.shape[0]), int(hs.shape[1])],
                                   F32, kind="ExternalOutput")

    with contextlib.ExitStack() as top:
        SBt = lambda *a: top.enter_context(nc.sbuf_tensor(*a))
        T0, V0, S0_, G0, Y0 = nc.tensor, nc.vector, nc.scalar, nc.gpsimd, nc.sync

        class EngTrack:
            def __init__(self, eng):
                self.eng = eng
                self.last = None

            def __getattr__(self, name):
                fn = getattr(self.eng, name)
                if callable(fn) and name not in ("wait_ge",):
                    def wrap(*a, **k):
                        inst = fn(*a, **k)
                        self.last = inst
                        return inst
                    return wrap
                return fn

            def wait_ge(self, sem, n):
                return self.eng.wait_ge(sem, n)

        T, V, S, G, Y = (EngTrack(T0), EngTrack(V0), EngTrack(S0_), EngTrack(G0),
                         EngTrack(Y0))

        u_bf = SBt("u_bf", [128, 128], BF16)
        lst_bf = SBt("lst_bf", [128, 128], BF16)
        lstpos = SBt("lstpos", [128, 128], BF16)
        id_bf = SBt("id_bf", [128, 128], BF16)
        ones_bf = SBt("ones_bf", [1, 128], BF16)
        lnw_sb = SBt("lnw_sb", [128, 8], F32)
        rs_all = SBt("rs_all", [128, NT], F32)
        rc_all = SBt("rc_all", [128, NT], F32)
        rc_tmp = SBt("rc_tmp", [128, NT], F32)
        eps_col = SBt("eps_col", [128, 1], F32)
        eps24_col = SBt("eps24_col", [128, 1], F32)
        vj1 = SBt("vj1", [128, 16], F32)
        vj2 = SBt("vj2", [128, 16], F32)
        wfT = SBt("wfT", [128, 8 * H], BF16)
        wg1T = SBt("wg1T", [128, 8 * 64], BF16)
        wg2T = SBt("wg2T", [128, D], BF16)
        watt = SBt("watt", [128, 4 * 8 * D], BF16)
        wqT = lambda ic: watt[:, (0 * 8 + ic) * D:(0 * 8 + ic + 1) * D]
        wkT = lambda ic: watt[:, (1 * 8 + ic) * D:(1 * 8 + ic + 1) * D]
        wvT = lambda ic: watt[:, (2 * 8 + ic) * D:(2 * 8 + ic + 1) * D]
        woT = lambda ic: watt[:, (3 * 8 + ic) * D:(3 * 8 + ic + 1) * D]
        xnf = SBt("xnf", [128, 8 * SLAB], BF16)
        xnf2 = SBt("xnf2", [128, 8 * SLAB], BF16)

        s_g = Sem(top, nc, "s_g")
        s_ld = Sem(top, nc, "s_ld")
        s_st = Sem(top, nc, "s_st")
        s_tr = Sem(top, nc, "s_tr")
        s_pe = Sem(top, nc, "s_pe")
        s_ve = Sem(top, nc, "s_ve")
        s_sc = Sem(top, nc, "s_sc")
        s_cc = Sem(top, nc, "s_cc")
        s_out = Sem(top, nc, "s_out")
        s_x = Sem(top, nc, "s_x")

        def gstep(inst):
            s_g.inc(inst)
            G.wait_ge(s_g.h, s_g.n)

        gstep(G.memset(u_bf[:, :], 0.0))
        gstep(G.affine_select(out=u_bf[:, :], in_=u_bf[:, :], compare_op=ALU.is_gt,
                              fill=1.0, base=0, pattern=[[-1, 128]],
                              channel_multiplier=1))
        gstep(G.memset(lst_bf[:, :], 1.0))
        gstep(G.affine_select(out=lst_bf[:, :], in_=lst_bf[:, :], compare_op=ALU.is_gt,
                              fill=0.0, base=0, pattern=[[-1, 128]],
                              channel_multiplier=1))
        gstep(G.memset(lstpos[:, :], 1e30))
        gstep(G.affine_select(out=lstpos[:, :], in_=lstpos[:, :], compare_op=ALU.is_gt,
                              fill=0.0, base=0, pattern=[[-1, 128]],
                              channel_multiplier=1))
        gstep(G.memset(id_bf[:, :], 0.0))
        gstep(G.affine_select(out=id_bf[:, :], in_=id_bf[:, :], compare_op=ALU.not_equal,
                              fill=1.0, base=0, pattern=[[-1, 128]],
                              channel_multiplier=1))
        gstep(G.memset(ones_bf[0:1, :], 1.0))
        gstep(G.memset(eps_col[:, :], 1e-5))
        gstep(G.memset(eps24_col[:, :], 1e-24))
        gstep(G.memset(vj1[:, :], 0.0))
        gstep(G.memset(vj2[:, :], 0.0))

        s_st.inc(Y.dma_start(out=lnw_sb[:, :],
                             in_=AP(lnw_in.ap().tensor, 0, [[1, 128], [128, 8]]),
                             allow_slow_non_contiguous=True), 16)
        with contextlib.ExitStack() as p0:
            SB0 = lambda *a: p0.enter_context(nc.sbuf_tensor(*a))
            wnat = SB0("wnat", [128, 8 * D], BF16)
            Xb = [SB0(f"xld{i}", [128, D], F32) for i in range(4)]
            XO = [SB0(f"xnt{i}", [128, D], BF16) for i in range(2)]
            musum = [SB0(f"musum{i}", [128, 1], F32) for i in range(2)]
            ssc = [SB0(f"ssc{i}", [128, 1], F32) for i in range(2)]
            ln_t1 = [SB0(f"ln_t1{i}", [128, 1], F32) for i in range(2)]
            ln_t2 = [SB0(f"ln_t2{i}", [128, 1], F32) for i in range(2)]
            ln_r = [SB0(f"ln_r{i}", [128, 1], F32) for i in range(2)]
            ln_nm = [SB0(f"ln_nm{i}", [128, 1], F32) for i in range(2)]
            ln_v1 = [SB0(f"ln_v1{i}", [128, 1], F32) for i in range(2)]
            ln_v2 = [SB0(f"ln_v2{i}", [128, 1], F32) for i in range(2)]
            ln_v3 = SB0("ln_v30", [128, 1], F32)
            sq_junk = [SB0(f"sq_junk{i}", [128, D], BF16) for i in range(2)]
            phi_sb = SB0("phi_sb", [128, 8 * NP], BF16)
            e_sb = [SB0("e_sb", [128, NP], BF16), SB0("e_sb2", [128, NP], BF16)]
            pl = [p0.enter_context(nc.psum_tensor("pl", [128, NP], F32)),
                  p0.enter_context(nc.psum_tensor("pl2", [128, NP], F32))]

            s_ld.inc(G.dma_start(
                out=phi_sb[:, :].rearrange("p (ic n) -> p ic n", ic=8),
                in_=AP(phi_in.ap().tensor, 0, [[NP, 128], [128 * NP, 8], [1, NP]])), 16)
            cap_phi = s_ld.n

            lnw_row = SB0("lnw_row", [128, D], F32)
            wosc = SB0("wosc", [128, D], BF16)
            s_ld.inc(G.dma_start(out=lnw_row[:, :],
                                 in_=AP(lnw_in.ap().tensor, 0, [[0, 128], [1, D]])), 16)
            for wi, (src, hb) in enumerate([(wo_in, h_wob), (wq_in, h_wqb),
                                            (wk_in, h_wkb), (wv_in, h_wvb)]):
                s_ld.inc(G.dma_start(
                    out=wnat[:, :].rearrange("p (jc i) -> p jc i", jc=8),
                    in_=AP(src.ap().tensor, 0, [[D, 128], [128 * D, 8], [1, D]])), 16)
                Y.wait_ge(s_ld.h, s_ld.n)
                if wi == 0:
                    V.wait_ge(s_ld.h, s_ld.n)
                    for jc in range(8):
                        if jc > 0:
                            V.wait_ge(s_st.h, s_st.n)
                        s_ve.inc(V.tensor_mul(wosc[:, :], wnat[:, jc * D:(jc + 1) * D],
                                              lnw_row[:, :]))
                        Y.wait_ge(s_ve.h, s_ve.n)
                        s_st.inc(Y.dma_start(out=hb[jc * 128:(jc + 1) * 128, :],
                                             in_=wosc[:, :]), 16)
                else:
                    for jc in range(8):
                        s_st.inc(Y.dma_start(out=hb[jc * 128:(jc + 1) * 128, :],
                                             in_=wnat[:, jc * D:(jc + 1) * D]), 16)
                Y.wait_ge(s_st.h, s_st.n)
                for ic in range(8):
                    dst = [woT, wqT, wkT, wvT][wi](ic)
                    s_tr.inc(Y.dma_start(out=dst, in_=hb[:, ic * 128:(ic + 1) * 128],
                                         transpose=True), 16)
                G.wait_ge(s_st.h, s_st.n)
            s_ld.inc(G.dma_start(out=wnat[0:H, 0:D], in_=wf_in[:, :]), 16)
            Y.wait_ge(s_ld.h, s_ld.n)
            s_st.inc(Y.dma_start(out=h_wfb[:, :], in_=wnat[0:H, 0:D]), 16)
            Y.wait_ge(s_st.h, s_st.n)
            for ic in range(8):
                s_tr.inc(Y.dma_start(out=wfT[:, ic * H:(ic + 1) * H],
                                     in_=h_wfb[:, ic * 128:(ic + 1) * 128],
                                     transpose=True), 16)
            G.wait_ge(s_st.h, s_st.n)
            s_ld.inc(G.dma_start(out=wnat[0:64, 0:D], in_=wg1_in[:, :]), 16)
            Y.wait_ge(s_ld.h, s_ld.n)
            s_st.inc(Y.dma_start(out=h_wg1b[:, :], in_=wnat[0:64, 0:D]), 16)
            Y.wait_ge(s_st.h, s_st.n)
            for ic in range(8):
                s_tr.inc(Y.dma_start(out=wg1T[:, ic * 64:(ic + 1) * 64],
                                     in_=h_wg1b[:, ic * 128:(ic + 1) * 128],
                                     transpose=True), 16)
            G.wait_ge(s_st.h, s_st.n)
            wg2pad = SB0("wg2pad", [128, 8 * 128], BF16)
            gstep(G.memset(wg2pad[:, :], 0.0))
            s_ld.inc(G.dma_start(
                out=wg2pad[:, :].rearrange("p (jc i) -> p jc i", jc=8)[:, :, 0:64],
                in_=AP(wg2_in.ap().tensor, 0, [[64, 128], [128 * 64, 8], [1, 64]])), 16)
            Y.wait_ge(s_ld.h, s_ld.n)
            Y.wait_ge(s_g.h, s_g.n)
            for jc in range(8):
                s_st.inc(Y.dma_start(out=h_wg2b[jc * 128:(jc + 1) * 128, :],
                                     in_=wg2pad[:, jc * 128:(jc + 1) * 128]), 16)
            Y.wait_ge(s_st.h, s_st.n)
            s_tr.inc(Y.dma_start(out=wg2T[:, :], in_=h_wg2b[:, :], transpose=True), 16)

            cap_x = {}; cap_sq = {}; cap_sscv = {}; cap_chain = {}; cap_xo = {}
            cap_xnst = {}; cap_Est = {}; cap_rs = {}; cap_f1pe = {}
            sc_marks = {}
            cap_xnf = {}
            cap_f1mm = {}

            def f1_evac(m):
                pp = pl[m % 2]
                ee = e_sb[m % 2]
                S.wait_ge(s_pe.h, cap_f1pe[m])
                if m >= 2:
                    S.wait_ge(s_st.h, cap_Est[m - 2])
                s_sc.inc(S.activation(ee[:, :], pp[:, :], AF.Exp,
                                      accum_out=rs_all[:, m:m + 1]))
                sc_marks[m] = s_sc.n
                Y.wait_ge(s_sc.h, sc_marks[m])
                s_st.inc(Y.dma_start(out=h_E[m * 128:(m + 1) * 128, :], in_=ee[:, :]), 16)
                cap_Est[m] = s_st.n

            for mt in range(NT):
                sl, r = mt // 4, mt % 4
                p = mt % 2
                xt = Xb[mt % 4]
                xo = XO[p]
                if mt >= 4:
                    G.wait_ge(s_ve.h, cap_xo[mt - 4])
                s_x.inc(G.dma_start(out=xt[:, :], in_=x_in[mt * 128:(mt + 1) * 128, :]), 16)
                cap_x[mt] = s_x.n
                V.wait_ge(s_x.h, cap_x[mt])
                if mt >= 2:
                    V.wait_ge(s_sc.h, cap_chain[mt - 2])
                s_ve.inc(V.tensor_reduce(musum[p][:, :], xt[:, :],
                                         mybir.AxisListType.X, ALU.add))
                cap_mu = s_ve.n
                S.wait_ge(s_x.h, cap_x[mt])
                if mt >= 2:
                    S.wait_ge(s_ve.h, cap_sscv[mt - 2])
                s_sc.inc(S.activation(sq_junk[p][:, :], xt[:, :], AF.Square))
                cap_sq[mt] = s_sc.n
                V.wait_ge(s_sc.h, cap_sq[mt])
                s_ve.inc(V.tensor_reduce(ssc[p][:, :], sq_junk[p][:, :],
                                         mybir.AxisListType.X, ALU.add))
                cap_sscv[mt] = s_ve.n
                S.wait_ge(s_ve.h, cap_sscv[mt])
                if mt >= 2:
                    S.wait_ge(s_ve.h, cap_xo[mt - 2])
                S.activation(ln_t1[p][:, :], musum[p][:, :], AF.Identity, scale=1.0 / D)
                S.activation(ln_t2[p][:, :], ln_t1[p][:, :], AF.Square)
                S.activation(ln_v2[p][:, :], ln_t2[p][:, :], AF.Identity, scale=-1.0)
                S.activation(ln_nm[p][:, :], ln_t1[p][:, :], AF.Identity, scale=-1.0)
                S.activation(ln_v3[:, :], eps_col[:, :], AF.Identity)
                S.activation(ln_v1[p][:, :], ssc[p][:, :], AF.Identity, scale=1.0 / D,
                             bias=ln_v2[p][:, 0:1])
                S.activation(ln_t2[p][:, :], ln_v1[p][:, :], AF.Ln, bias=eps_col[:, 0:1])
                s_sc.inc(S.activation(ln_r[p][:, :], ln_t2[p][:, :], AF.Exp, scale=-0.5))
                cap_chain[mt] = s_sc.n
                V.wait_ge(s_sc.h, cap_chain[mt])
                if mt >= 2:
                    V.wait_ge(s_st.h, cap_xnst[mt - 2])
                s_ve.inc(V.tensor_scalar(xo[:, :], xt[:, :], ln_nm[p][:, 0:1],
                                         ln_r[p][:, 0:1], ALU.add, ALU.mult))
                cap_xo[mt] = s_ve.n
                Y.wait_ge(s_ve.h, cap_xo[mt])
                s_st.inc(Y.dma_start(out=h_xn[mt * 128:(mt + 1) * 128, :], in_=xo[:, :]), 16)
                cap_xnst[mt] = s_st.n

                if r == 3:
                    xnb = xnf if sl % 2 == 0 else xnf2
                    Y.wait_ge(s_st.h, cap_xnst[mt])
                    if sl > 1:
                        Y.wait_ge(s_pe.h, cap_f1mm[sl - 2])
                    for ic in range(8):
                        s_tr.inc(Y.dma_start(
                            out=xnb[:, ic * SLAB:(ic + 1) * SLAB],
                            in_=h_xn[sl * SLAB:(sl + 1) * SLAB, ic * 128:(ic + 1) * 128],
                            transpose=True), 16)
                    cap_xnf[sl] = s_tr.n
                    for r4 in range(4):
                        m2 = sl * 4 + r4
                        pp = pl[m2 % 2]
                        if r4 >= 2:
                            f1_evac(m2 - 2)
                        T.wait_ge(s_tr.h, cap_xnf[sl])
                        T.wait_ge(s_ld.h, cap_phi)
                        if m2 >= 2:
                            T.wait_ge(s_sc.h, sc_marks[m2 - 2])
                        for ic in range(8):
                            lhs = xnb[:, ic * SLAB + r4 * 128: ic * SLAB + (r4 + 1) * 128]
                            for nb in range(4):
                                T.matmul(pp[:, nb * 512:(nb + 1) * 512], lhs,
                                         phi_sb[:, ic * NP + nb * 512: ic * NP + (nb + 1) * 512],
                                         start=(ic == 0), stop=(ic == 7))
                        s_pe.inc(T.last)
                        cap_f1pe[m2] = s_pe.n
                    cap_f1mm[sl] = s_pe.n
                    f1_evac(sl * 4 + 2)
                    f1_evac(sl * 4 + 3)

            S.activation(ln_v3[:, :], eps_col[:, :], AF.Identity)
            S.activation(ln_v3[:, :], eps_col[:, :], AF.Identity)
            S.activation(rc_tmp[:, :], rs_all[:, :], AF.Ln)
            s_sc.inc(S.activation(rc_all[:, :], rc_tmp[:, :], AF.Exp, scale=-1.0))
            G.wait_ge(s_st.h, cap_Est[NT - 1])
            s_cc.inc(G.collective_compute(
                "AllReduce", ALU.add, replica_groups=[list(range(NCORES))],
                ins=[h_E.ap()], outs=[h_Z.ap()]))

        cap_f1_sc = s_sc.n
        cap_f1_pe = s_pe.n
        cap_f1_st = s_st.n
        cap_f1_ve = s_ve.n

        with contextlib.ExitStack() as aa:
            SBA = lambda *a: aa.enter_context(nc.sbuf_tensor(*a))
            k_bf = [SBA("k_bf0", [128, 4 * D], BF16), SBA("k_bf1", [128, 4 * D], BF16)]
            v_sl = [SBA("v_sl0", [128, 4 * D], BF16), SBA("v_sl1", [128, 4 * D], BF16)]
            gate_sl = SBA("gate_sl", [128, 4 * D], BF16)
            q_fm = SBA("q_fm", [128, H * SLAB], BF16)
            k_fm = SBA("k_fm", [128, H * SLAB], BF16)
            g1f = SBA("g1f", [128, SLAB], BF16)
            q_bf = SBA("q_bf", [128, H * 128], BF16)
            k_pad = SBA("k_pad", [128, H * 128], BF16)
            g1_bf = SBA("g1_bf", [128, 128], BF16)
            gt_raw = SBA("gt_raw", [128, 512], BF16)
            B1 = SBA("B1", [128, 2048], BF16)
            B2 = SBA("B2", [128, 2048], BF16)
            B3 = SBA("B3", [128, 2048], BF16)
            e1f = SBA("e1f", [128, 64], F32)
            e1g = SBA("e1g", [128, 64], F32)
            phif = SBA("phif", [128, 64], F32)
            phih = SBA("phih", [128, 64], BF16)
            phil = SBA("phil", [128, 64], BF16)
            ssn = SBA("ssn", [128, 64], F32)
            ssnb = SBA("ssnb", [128, 64], F32)
            rn_sl = SBA("rn_sl", [128, 64], F32)
            expF = SBA("expF", [128, H], F32)
            expK = SBA("expK", [128, H], F32)
            ekrn = SBA("ekrn", [128, H], F32)
            vpp = SBA("vpp", [128, D], BF16)
            dec_col = SBA("dec_col", [128, H], F32)
            SA_ = SBA("SA_", [64, H * HD], F32)
            SBf = SBA("SBf", [64, H * HD], F32)
            S_bf = SBA("S_bf", [64, H * HD], BF16)
            og = SBA("og", [128, D], BF16)
            ogg = SBA("ogg", [128, D], BF16)
            oln_bf = SBA("oln_bf", [128, D], BF16)
            olnf = SBA("olnf", [128, 8 * SLAB], BF16)
            musum = SBA("musumA", [128, 1], F32)
            ssc = SBA("sscA", [128, 1], F32)
            ln_t1 = SBA("ln_t1A", [128, 1], F32)
            ln_t2 = SBA("ln_t2A", [128, 1], F32)
            ln_r = SBA("ln_rA", [128, 1], F32)
            ln_nm = SBA("ln_nmA", [128, 1], F32)
            ln_v1 = SBA("ln_v1A", [128, 1], F32)
            ln_v2 = SBA("ln_v2A", [128, 1], F32)
            ln_v3 = SBA("ln_v3A", [128, 1], F32)
            pS = [aa.enter_context(nc.psum_tensor(f"pS{i}", [128, 512], F32))
                  for i in range(4)]
            pBig = aa.enter_context(nc.psum_tensor("pBigA", [128, 2048], F32))

            G.wait_ge(s_st.h, cap_f1_st)
            G.wait_ge(s_pe.h, cap_f1_pe)
            G.wait_ge(s_ve.h, cap_f1_ve)
            G.wait_ge(s_sc.h, cap_f1_sc)
            gstep(G.memset(SA_[:, :], 0.0))
            gstep(G.memset(S_bf[:, :], 0.0))
            gstep(G.memset(q_bf[:, :], 0.0))
            gstep(G.memset(k_pad[:, :], 0.0))
            gstep(G.memset(g1_bf[:, :], 0.0))
            T.wait_ge(s_sc.h, cap_f1_sc)
            S.wait_ge(s_g.h, s_g.n)
            V.wait_ge(s_g.h, s_g.n)

            bank_free = [cap_f1_sc] * 4
            cap_state = 0
            cap_expdv = 0
            cap_pmul = 0
            cap_combh0 = 0
            cap_vpp = 0
            cap_oh1 = 0
            cap_kv = 0
            cap_phiev = 0
            cap_flphi = 0
            cap_phimm = 0
            cap_e1g = 0
            cap_rn = 0
            cap_phcp = 0
            cap_s3ev2 = 0
            cap_s3st_last = 0
            cap_s3mm_last = 0
            cap_olnst = 0
            cap_gla_last = 0
            cap_gatemm = 0
            cap_trqk = 0
            cap_trg1 = 0
            qst = {}
            mmcap = {}
            s1last = {}
            evlast = {}
            xnfcap = {}
            gidx = [0]

            def issue_xnf(sl2):
                Y.wait_ge(s_pe.h, s1last.get(sl2 - 1, cap_f1_pe))
                for ic in range(8):
                    s_tr.inc(Y.dma_start(
                        out=xnf[:, ic * SLAB:(ic + 1) * SLAB],
                        in_=h_xn[sl2 * SLAB:(sl2 + 1) * SLAB, ic * 128:(ic + 1) * 128],
                        transpose=True), 16)
                xnfcap[sl2] = s_tr.n

            def s1_mm(sl2, c, kind):
                g = gidx[0]
                gidx[0] += 1
                bank = pS[g % 4]
                T.wait_ge(s_tr.h, xnfcap[sl2])
                T.wait_ge(s_sc.h, bank_free[g % 4])
                for ic in range(8):
                    lhs = xnf[:, ic * SLAB + c * 128: ic * SLAB + (c + 1) * 128]
                    st, sp = (ic == 0), (ic == 7)
                    if kind == "qlo":
                        T.matmul(bank[:, 0:512], lhs, wqT(ic)[:, 0:512], start=st, stop=sp)
                    elif kind == "qhi":
                        T.matmul(bank[:, 0:512], lhs, wqT(ic)[:, 512:1024], start=st, stop=sp)
                    elif kind == "klo":
                        T.matmul(bank[:, 0:512], lhs, wkT(ic)[:, 0:512], start=st, stop=sp)
                    elif kind == "khi":
                        T.matmul(bank[:, 0:512], lhs, wkT(ic)[:, 512:1024], start=st, stop=sp)
                    elif kind == "vlo":
                        T.matmul(bank[:, 0:512], lhs, wvT(ic)[:, 0:512], start=st, stop=sp)
                    elif kind == "vhi":
                        T.matmul(bank[:, 0:512], lhs, wvT(ic)[:, 512:1024], start=st, stop=sp)
                    elif kind == "f":
                        T.matmul(bank[:, 0:16], lhs, wfT[:, ic * H:(ic + 1) * H],
                                 start=st, stop=sp)
                    else:
                        T.matmul(bank[:, 0:64], lhs, wg1T[:, ic * 64:(ic + 1) * 64],
                                 start=st, stop=sp)
                s_pe.inc(T.last)
                mmcap[(sl2, c, kind)] = s_pe.n
                s1last[sl2] = s_pe.n
                return bank

            def s1_evac(sl2, c, kind, bank):
                par = sl2 % 2
                S.wait_ge(s_pe.h, mmcap[(sl2, c, kind)])
                if kind == "qlo" and (sl2, c - 1) in qst:
                    S.wait_ge(s_st.h, qst[(sl2, c - 1)])
                if kind == "qlo":
                    S.activation(_rep(q_bf[:, :], 0, [[128, 8], [1, 64]]),
                                 bank[:, 0:512], AF.Silu)
                elif kind == "qhi":
                    S.activation(_rep(q_bf[:, :], 8 * 128, [[128, 8], [1, 64]]),
                                 bank[:, 0:512], AF.Silu)
                elif kind == "klo":
                    S.activation(_rep(k_pad[:, :], 0, [[128, 8], [1, 64]]),
                                 bank[:, 0:512], AF.Silu)
                elif kind == "khi":
                    S.activation(_rep(k_pad[:, :], 8 * 128, [[128, 8], [1, 64]]),
                                 bank[:, 0:512], AF.Silu)
                    S.activation(k_bf[par][:, c * D:(c + 1) * D],
                                 _rep(k_pad[:, :], 0, [[128, H], [1, 64]]), AF.Copy)
                elif kind == "vlo":
                    S.activation(v_sl[par][:, c * D: c * D + 512],
                                 bank[:, 0:512], AF.Silu)
                elif kind == "vhi":
                    S.activation(v_sl[par][:, c * D + 512:(c + 1) * D],
                                 bank[:, 0:512], AF.Silu)
                elif kind == "f":
                    S.wait_ge(s_ve.h, cap_e1g)
                    S.activation(e1f[:, c * 16:(c + 1) * 16], bank[:, 0:16],
                                 AF.Exp, scale=-1.0)
                else:
                    S.copy(g1_bf[:, 0:64], bank[:, 0:64])
                s_sc.inc(S.last)
                b = None
                for i in range(4):
                    if pS[i] is bank:
                        b = i
                bank_free[b] = s_sc.n
                evlast[sl2] = s_sc.n

            def s1_stores(sl2, c):
                tt = sl2 * SLAB + c * 128
                G.wait_ge(s_sc.h, evlast[sl2])
                s_st.inc(G.dma_start(out=h_q[tt:tt + 128, :], in_=q_bf[:, :]), 16)
                s_st.inc(G.dma_start(out=h_k[tt:tt + 128, :], in_=k_pad[:, :]), 16)
                s_st.inc(G.dma_start(out=h_g1[tt:tt + 128, :], in_=g1_bf[:, :]), 16)
                qst[(sl2, c)] = s_st.n

            KA = ["qlo", "qhi", "klo", "khi"]
            KB = ["vlo", "vhi", "f", "g1"]

            def s1_batchA(sl2, c):
                banks = [s1_mm(sl2, c, k) for k in KA]
                for k, b in zip(KA, banks):
                    s1_evac(sl2, c, k, b)

            def s1_batchB(sl2, c):
                banks = [s1_mm(sl2, c, k) for k in KB]
                for k, b in zip(KB, banks):
                    s1_evac(sl2, c, k, b)
                s1_stores(sl2, c)

            def issue_trans(sl2):
                Y.wait_ge(s_st.h, qst[(sl2, 3)])
                Y.wait_ge(s_pe.h, cap_gla_last)
                Y.wait_ge(s_pe.h, cap_gatemm)
                s_tr.inc(Y.dma_start(out=g1f[:, :], in_=h_g1[sl2 * SLAB:(sl2 + 1) * SLAB, :],
                                     transpose=True), 16)
                trg1 = s_tr.n
                for h in range(H):
                    s_tr.inc(Y.dma_start(out=q_fm[:, h * SLAB:(h + 1) * SLAB],
                                         in_=h_q[sl2 * SLAB:(sl2 + 1) * SLAB,
                                                 h * 128:(h + 1) * 128],
                                         transpose=True), 16)
                    s_tr.inc(Y.dma_start(out=k_fm[:, h * SLAB:(h + 1) * SLAB],
                                         in_=h_k[sl2 * SLAB:(sl2 + 1) * SLAB,
                                                 h * 128:(h + 1) * 128],
                                         transpose=True), 16)
                return trg1, s_tr.n

            def issue_gate(sl2):
                nonlocal cap_gatemm
                T.wait_ge(s_tr.h, cap_trg1)
                T.wait_ge(s_sc.h, evlast[sl2])
                for r in range(4):
                    lhs = g1f[0:64, r * 128:(r + 1) * 128]
                    T.matmul(pS[r][:, 0:512], lhs, wg2T[0:64, 0:512],
                             start=True, stop=True)
                s_pe.inc(T.last)
                S.wait_ge(s_pe.h, s_pe.n)
                cap_gts = s_ve.n
                for r in range(4):
                    if r > 0:
                        S.wait_ge(s_ve.h, cap_gts)
                    s_sc.inc(S.activation(gt_raw[:, 0:512], pS[r][:, 0:512],
                                          AF.Tanh, scale=0.5))
                    V.wait_ge(s_sc.h, s_sc.n)
                    s_ve.inc(V.tensor_scalar(gate_sl[:, r * D:r * D + 512],
                                             gt_raw[:, 0:512],
                                             0.5, 0.5, ALU.mult, ALU.add))
                    cap_gts = s_ve.n
                T.wait_ge(s_sc.h, s_sc.n)
                T.wait_ge(s_ve.h, cap_gts)
                for r in range(4):
                    lhs = g1f[0:64, r * 128:(r + 1) * 128]
                    T.matmul(pS[r][:, 0:512], lhs, wg2T[0:64, 512:1024],
                             start=True, stop=True)
                s_pe.inc(T.last)
                cap_gatemm = s_pe.n
                S.wait_ge(s_pe.h, s_pe.n)
                for r in range(4):
                    if r > 0:
                        S.wait_ge(s_ve.h, cap_gts)
                    s_sc.inc(S.activation(gt_raw[:, 0:512], pS[r][:, 0:512],
                                          AF.Tanh, scale=0.5))
                    V.wait_ge(s_sc.h, s_sc.n)
                    s_ve.inc(V.tensor_scalar(gate_sl[:, r * D + 512:(r + 1) * D],
                                             gt_raw[:, 0:512],
                                             0.5, 0.5, ALU.mult, ALU.add))
                    cap_gts = s_ve.n
                for i in range(4):
                    bank_free[i] = s_sc.n

            def issue_fknorm(sl2):
                nonlocal cap_e1g, cap_rn, cap_phcp
                par = sl2 % 2
                V.wait_ge(s_sc.h, evlast[sl2])
                s_ve.inc(V.tensor_scalar_add(e1g[:, :], e1f[:, :], 1.0))
                cap_e1g = s_ve.n
                S.wait_ge(s_ve.h, cap_e1g)
                S.wait_ge(s_ve.h, cap_flphi)
                s_sc.inc(S.activation(phif[:, :], e1g[:, :], AF.Ln))
                cap_phif = s_sc.n
                V.wait_ge(s_sc.h, cap_rn)
                for c in range(4):
                    V.tensor_mul(B1[:, 0:1024], k_bf[par][:, c * D:(c + 1) * D],
                                 k_bf[par][:, c * D:(c + 1) * D])
                    V.tensor_reduce(ssn[:, c * 16:(c + 1) * 16],
                                    B1[:, 0:1024].rearrange("p (h w) -> p h w", h=H),
                                    mybir.AxisListType.X, ALU.add)
                s_ve.inc(V.last)
                cap_ssn = s_ve.n
                S.wait_ge(s_ve.h, cap_ssn)
                S.wait_ge(s_ve.h, cap_pmul)
                S.activation(ssnb[:, :], ssn[:, :], AF.Ln, bias=eps24_col[:, 0:1])
                s_sc.inc(S.activation(rn_sl[:, :], ssnb[:, :], AF.Exp, scale=-0.5))
                cap_rn = s_sc.n
                V.wait_ge(s_sc.h, cap_rn)
                V.wait_ge(s_pe.h, cap_phimm)
                V.tensor_copy(phih[:, :], phif[:, :])
                V.tensor_copy(vj1[:, :], vj2[:, :])
                s_ve.inc(V.tensor_sub(phil[:, :], phif[:, :], phih[:, :]))
                cap_phcp = s_ve.n

            trqk_caps = {}
            issue_xnf(0)
            for c in range(4):
                s1_batchA(0, c)
                s1_batchB(0, c)
            cap_trg1, cap_trqk = issue_trans(0)
            for c in range(4):
                trqk_caps[(0, c)] = cap_trqk
            issue_gate(0)
            issue_fknorm(0)

            for sl in range(NSLAB):
                t0s = sl * SLAB
                par = sl % 2
                fill = sl + 1 < NSLAB
                if fill:
                    issue_xnf(sl + 1)
                for c in range(4):
                    if c == 0:
                        T.wait_ge(s_ve.h, cap_s3ev2)
                        T.wait_ge(s_ve.h, cap_phcp)
                    T.wait_ge(s_ve.h, cap_state)
                    T.wait_ge(s_sc.h, cap_expdv)
                    T.matmul(pBig[:, 0:16], u_bf[:, :], phih[:, c * 16:(c + 1) * 16],
                             start=True, stop=False)
                    T.matmul(pBig[:, 0:16], u_bf[:, :], phil[:, c * 16:(c + 1) * 16],
                             start=False, stop=True)
                    T.matmul(pBig[:, 16:32], lst_bf[:, :], phih[:, c * 16:(c + 1) * 16],
                             start=True, stop=False)
                    T.matmul(pBig[:, 16:32], lst_bf[:, :], phil[:, c * 16:(c + 1) * 16],
                             start=False, stop=True)
                    s_pe.inc(T.last)
                    cap_phimm = s_pe.n
                    S.wait_ge(s_pe.h, cap_phimm)
                    S.activation(expF[:, :], pBig[:, 0:16], AF.Exp, scale=-1.0)
                    s_sc.inc(S.activation(expK[:, :], pBig[:, 16:32], AF.Exp,
                                          scale=-1.0))
                    cap_phiev = s_sc.n
                    if fill:
                        banksA = [s1_mm(sl + 1, c, k) for k in KA]
                    V.wait_ge(s_sc.h, cap_phiev)
                    V.tensor_mul(dec_col[:, :], expF[:, :], expK[:, :])
                    V.tensor_mul(ekrn[:, :], expK[:, :], rn_sl[:, c * 16:(c + 1) * 16])
                    V.tensor_copy(vj1[:, :], vj2[:, :])
                    s_ve.inc(V.tensor_tensor(
                        vpp[:, :].rearrange("p (h w) -> p h w", h=H),
                        v_sl[par][:, c * D:(c + 1) * D].rearrange("p (h w) -> p h w", h=H),
                        _rep(ekrn[:, :], 0, [[1, H], [0, HD]]), ALU.mult))
                    cap_vpp = s_ve.n
                    V.wait_ge(s_pe.h, cap_oh1)
                    for h in range(H):
                        V.tensor_scalar_mul(B3[:, h * 128:(h + 1) * 128], lst_bf[:, :],
                                            phif[:, c * 16 + h:c * 16 + h + 1])
                    s_ve.inc(V.last)
                    cap_flphi = s_ve.n
                    if fill:
                        for k, b in zip(KA, banksA):
                            s1_evac(sl + 1, c, k, b)
                        banksB = [s1_mm(sl + 1, c, k) for k in KB]
                    T.wait_ge(s_ve.h, cap_flphi)
                    T.wait_ge(s_sc.h, cap_phiev)
                    for h in range(H):
                        T.matmul(pBig[:, h * 128:(h + 1) * 128],
                                 B3[:, h * 128:(h + 1) * 128], u_bf[:, :],
                                 start=True, stop=False)
                        T.matmul(pBig[:, h * 128:(h + 1) * 128],
                                 id_bf[:, :], lstpos[:, :], start=False, stop=True)
                    s_pe.inc(T.last)
                    cap_dmm = s_pe.n
                    if fill:
                        for k, b in zip(KB, banksB):
                            s1_evac(sl + 1, c, k, b)
                        s1_stores(sl + 1, c)
                    S.wait_ge(s_pe.h, cap_dmm)
                    S.wait_ge(s_ve.h, cap_pmul)
                    s_sc.inc(S.activation(B2[:, :], pBig[:, :], AF.Exp, scale=-1.0))
                    cap_expdv = s_sc.n
                    T.wait_ge(s_sc.h, cap_expdv)
                    T.wait_ge(s_tr.h, trqk_caps[(sl, c)])
                    for h in range(H):
                        T.matmul(pBig[:, h * 128:(h + 1) * 128],
                                 k_fm[0:64, h * SLAB + c * 128: h * SLAB + (c + 1) * 128],
                                 q_fm[0:64, h * SLAB + c * 128: h * SLAB + (c + 1) * 128],
                                 start=True, stop=True)
                    s_pe.inc(T.last)
                    cap_gmm = s_pe.n
                    V.wait_ge(s_pe.h, cap_gmm)
                    V.tensor_tensor(
                        B1[:, :].rearrange("p (h w) -> p h w", h=H),
                        pBig[:, :].rearrange("p (h w) -> p h w", h=H),
                        _rep(rn_sl[:, :], c * 16, [[1, H], [0, 128]]), ALU.mult)
                    s_ve.inc(V.tensor_mul(B3[:, :], B1[:, :], B2[:, :]))
                    cap_pmul = s_ve.n
                    T.wait_ge(s_ve.h, cap_pmul)
                    for hp in range(8):
                        T.matmul(pBig[:, hp * 64:(hp + 1) * 64],
                                 B3[:, hp * 128:(hp + 1) * 128],
                                 v_sl[par][:, c * D + hp * 64: c * D + (hp + 1) * 64],
                                 start=True, stop=True)
                        T.matmul(pBig[:, 512 + hp * 64: 512 + (hp + 1) * 64],
                                 q_fm[0:64, hp * SLAB + c * 128: hp * SLAB + (c + 1) * 128],
                                 S_bf[0:64, hp * 64:(hp + 1) * 64],
                                 start=True, stop=True)
                    s_pe.inc(T.last)
                    cap_oh0 = s_pe.n
                    V.wait_ge(s_pe.h, cap_oh0)
                    V.wait_ge(s_sc.h, cap_expdv)
                    if c == 0:
                        V.wait_ge(s_st.h, cap_s3st_last)
                    V.tensor_tensor(
                        B1[:, 0:512].rearrange("p (h w) -> p h w", h=8),
                        pBig[:, 512:1024].rearrange("p (h w) -> p h w", h=8),
                        _rep(expF[:, :], 0, [[1, 8], [0, 64]]), ALU.mult)
                    V.tensor_add(og[:, 0:512], B1[:, 0:512], pBig[:, 0:512])
                    s_ve.inc(V.tensor_mul(ogg[:, 0:512], og[:, 0:512],
                                          gate_sl[:, c * D: c * D + 512]))
                    cap_combh0 = s_ve.n
                    for hp in range(8):
                        h = 8 + hp
                        T.matmul(pBig[:, 1024 + hp * 64: 1024 + (hp + 1) * 64],
                                 B3[:, h * 128:(h + 1) * 128],
                                 v_sl[par][:, c * D + h * 64: c * D + (h + 1) * 64],
                                 start=True, stop=True)
                        T.matmul(pBig[:, 1536 + hp * 64: 1536 + (hp + 1) * 64],
                                 q_fm[0:64, h * SLAB + c * 128: h * SLAB + (c + 1) * 128],
                                 S_bf[0:64, h * 64:(h + 1) * 64],
                                 start=True, stop=True)
                    s_pe.inc(T.last)
                    cap_oh1 = s_pe.n
                    cap_gla_last = s_pe.n
                    T.wait_ge(s_ve.h, cap_combh0)
                    T.wait_ge(s_ve.h, cap_vpp)
                    for h in range(H):
                        T.matmul(pBig[0:64, h * 64:(h + 1) * 64],
                                 k_bf[par][:, c * D + h * 64: c * D + (h + 1) * 64],
                                 vpp[:, h * 64:(h + 1) * 64], start=True, stop=True)
                    s_pe.inc(T.last)
                    cap_kv = s_pe.n
                    V.wait_ge(s_pe.h, cap_oh1)
                    V.tensor_tensor(
                        B1[:, 0:512].rearrange("p (h w) -> p h w", h=8),
                        pBig[:, 1536:2048].rearrange("p (h w) -> p h w", h=8),
                        _rep(expF[:, :], 8, [[1, 8], [0, 64]]), ALU.mult)
                    V.tensor_add(og[:, 512:1024], B1[:, 0:512], pBig[:, 1024:1536])
                    s_ve.inc(V.tensor_mul(ogg[:, 512:1024], og[:, 512:1024],
                                          gate_sl[:, c * D + 512:(c + 1) * D]))
                    s_ve.inc(V.tensor_reduce(musum[:, :], ogg[:, :],
                                             mybir.AxisListType.X, ALU.add))
                    cap_mu = s_ve.n
                    S.wait_ge(s_ve.h, cap_mu)
                    s_sc.inc(S.activation(B2[:, 0:1024], ogg[:, :], AF.Square))
                    cap_sq = s_sc.n
                    V.wait_ge(s_sc.h, cap_sq)
                    s_ve.inc(V.tensor_reduce(ssc[:, :], B2[:, 0:1024],
                                             mybir.AxisListType.X, ALU.add))
                    cap_ssc = s_ve.n
                    S.wait_ge(s_ve.h, cap_ssc)
                    S.activation(ln_t1[:, :], musum[:, :], AF.Identity, scale=1.0 / D)
                    S.activation(ln_t2[:, :], ln_t1[:, :], AF.Square)
                    S.activation(ln_v2[:, :], ln_t2[:, :], AF.Identity, scale=-1.0)
                    S.activation(ln_nm[:, :], ln_t1[:, :], AF.Identity, scale=-1.0)
                    S.activation(ln_v3[:, :], eps_col[:, :], AF.Identity)
                    S.activation(ln_v1[:, :], ssc[:, :], AF.Identity, scale=1.0 / D,
                                 bias=ln_v2[:, 0:1])
                    S.activation(ln_t2[:, :], ln_v1[:, :], AF.Ln, bias=eps_col[:, 0:1])
                    s_sc.inc(S.activation(ln_r[:, :], ln_t2[:, :], AF.Exp, scale=-0.5))
                    cap_lnr = s_sc.n
                    V.wait_ge(s_sc.h, cap_lnr)
                    V.wait_ge(s_st.h, cap_olnst)
                    s_ve.inc(V.tensor_scalar(oln_bf[:, :], ogg[:, :], ln_nm[:, 0:1],
                                             ln_r[:, 0:1], ALU.add, ALU.mult))
                    cap_oln = s_ve.n
                    G.wait_ge(s_ve.h, cap_oln)
                    s_st.inc(G.dma_start(
                        out=h_oln[t0s + c * 128: t0s + (c + 1) * 128, :],
                        in_=oln_bf[:, :]), 16)
                    cap_olnst = s_st.n
                    V.wait_ge(s_pe.h, cap_kv)
                    V.tensor_tensor(
                        SBf[0:64, :].rearrange("p (h w) -> p h w", h=H),
                        SA_[0:64, :].rearrange("p (h w) -> p h w", h=H),
                        _rep(dec_col[:, :], 0, [[1, H], [0, HD]], pcount=64),
                        ALU.mult)
                    V.tensor_add(SA_[0:64, :], SBf[0:64, :], pBig[0:64, 0:1024])
                    s_ve.inc(V.tensor_copy(S_bf[0:64, :], SA_[0:64, :]))
                    cap_state = s_ve.n

                Y.wait_ge(s_st.h, cap_olnst)
                Y.wait_ge(s_pe.h, cap_s3mm_last)
                for ic in range(8):
                    s_tr.inc(Y.dma_start(
                        out=olnf[:, ic * SLAB:(ic + 1) * SLAB],
                        in_=h_oln[t0s:t0s + SLAB, ic * 128:(ic + 1) * 128],
                        transpose=True), 16)
                cap_trolnf = s_tr.n
                if fill:
                    cap_trg1, cap_trqk = issue_trans(sl + 1)
                    for c in range(4):
                        trqk_caps[(sl + 1, c)] = cap_trqk

                s3ev = {}
                s3st = {}
                for r in range(4):
                    off = (r % 2) * 1024
                    T.wait_ge(s_tr.h, cap_trolnf)
                    if r >= 2:
                        T.wait_ge(s_ve.h, s3ev[r - 2])
                    else:
                        T.wait_ge(s_ve.h, cap_state)
                    for ic in range(8):
                        lhs = olnf[:, ic * SLAB + r * 128: ic * SLAB + (r + 1) * 128]
                        T.matmul(pBig[:, off:off + 512], lhs, woT(ic)[:, 0:512],
                                 start=(ic == 0), stop=(ic == 7))
                        T.matmul(pBig[:, off + 512:off + 1024], lhs,
                                 woT(ic)[:, 512:1024], start=(ic == 0), stop=(ic == 7))
                    s_pe.inc(T.last)
                    cap_s3mm = s_pe.n
                    V.wait_ge(s_pe.h, cap_s3mm)
                    if r >= 1:
                        V.wait_ge(s_st.h, s3st[r - 1])
                    s_ve.inc(V.tensor_copy(og[:, :], pBig[:, off:off + 1024]))
                    s3ev[r] = s_ve.n
                    G.wait_ge(s_ve.h, s3ev[r])
                    s_st.inc(G.dma_start(
                        out=h_s[t0s + r * 128: t0s + (r + 1) * 128, :],
                        in_=og[:, :]), 16)
                    s3st[r] = s_st.n
                cap_s3mm_last = s_pe.n
                cap_s3ev2 = s3ev[2]
                cap_s3st_last = s3st[3]

                if fill:
                    issue_gate(sl + 1)
                    issue_fknorm(sl + 1)
        nc.all_engine_barrier()

        G.wait_ge(s_cc.h, s_cc.n)
        nc.all_engine_barrier()
        with contextlib.ExitStack() as f2:
            SB2 = lambda *a: f2.enter_context(nc.sbuf_tensor(*a))
            e_stream = SB2("e_stream", [128, NT * 512], BF16)
            z_stream = SB2("z_stream", [128, NT * 512], BF16)
            xn_t = [SB2("xn_t1", [128, D], BF16), SB2("xn_t2", [128, D], BF16)]
            rz_f = SB2("rz_f", [128, 512], F32)
            rz_g = [SB2("rz_g1", [128, 512], F32), SB2("rz_g2", [128, 512], F32)]
            disp = [SB2("disp1", [128, 512], BF16), SB2("disp2", [128, 512], BF16)]
            slots_fm = SB2("slots_fm", [128, 8 * NP], BF16)
            psl = [f2.enter_context(nc.psum_tensor(f"psl{i}", [128, 512], F32))
                   for i in range(8)]
            cap_rz = {}
            cap_dp = {}
            cap_xt = {}
            cap_mm = {}
            cap_ev = {}
            it = 0
            for sb in range(4):
                if sb > 0:
                    G.wait_ge(s_ve.h, cap_dp[it - 1])
                    G.wait_ge(s_sc.h, cap_rz[it - 1])
                s_ld.inc(G.dma_start(
                    out=z_stream[:, :].rearrange("p (m w) -> p m w", m=NT),
                    in_=AP(h_Z.ap().tensor, sb * 512,
                           [[NP, 128], [128 * NP, NT], [1, 512]])), 16)
                cap_zld = s_ld.n
                s_ld.inc(G.dma_start(
                    out=e_stream[:, :].rearrange("p (m w) -> p m w", m=NT),
                    in_=AP(h_E.ap().tensor, sb * 512,
                           [[NP, 128], [128 * NP, NT], [1, 512]])), 16)
                cap_eld = s_ld.n
                for mt in range(NT):
                    par = it % 2
                    xt = xn_t[par]
                    dp = disp[par]
                    rzg = rz_g[par]
                    if it >= 2:
                        Y.wait_ge(s_pe.h, cap_mm[it - 2])
                    s_st.inc(Y.dma_start(out=xt[:, :],
                                         in_=h_xn[mt * 128:(mt + 1) * 128, :]), 16)
                    cap_xt[it] = s_st.n
                    S.wait_ge(s_ld.h, cap_zld)
                    if it >= 2:
                        S.wait_ge(s_ve.h, cap_dp[it - 2])
                    S.activation(rz_f[:, :], z_stream[:, mt * 512:(mt + 1) * 512], AF.Ln)
                    s_sc.inc(S.activation(rzg[:, :], rz_f[:, :], AF.Exp, scale=-1.0))
                    cap_rz[it] = s_sc.n
                    V.wait_ge(s_sc.h, cap_rz[it])
                    V.wait_ge(s_ld.h, cap_eld)
                    if it >= 2:
                        V.wait_ge(s_pe.h, cap_mm[it - 2])
                    s_ve.inc(V.tensor_mul(dp[:, :],
                                          e_stream[:, mt * 512:(mt + 1) * 512], rzg[:, :]))
                    cap_dp[it] = s_ve.n
                    T.wait_ge(s_ve.h, cap_dp[it])
                    T.wait_ge(s_st.h, cap_xt[it])
                    if mt == 0 and sb > 0:
                        T.wait_ge(s_ve.h, cap_ev[sb - 1])
                    for dc in range(8):
                        T.matmul(psl[dc][:, :], xt[:, dc * 128:(dc + 1) * 128], dp[:, :],
                                 start=(mt == 0), stop=(mt == NT - 1))
                    s_pe.inc(T.last)
                    cap_mm[it] = s_pe.n
                    it += 1
                V.wait_ge(s_pe.h, cap_mm[it - 1])
                for dc in range(8):
                    V.tensor_copy(slots_fm[:, dc * NP + sb * 512: dc * NP + (sb + 1) * 512],
                                  psl[dc][:, :])
                s_ve.inc(V.last)
                cap_ev[sb] = s_ve.n
            Y.wait_ge(s_ve.h, s_ve.n)
            for dc in range(8):
                s_st.inc(Y.dma_start(out=h_slots[dc * 128:(dc + 1) * 128, :],
                                     in_=slots_fm[:, dc * NP:(dc + 1) * NP]), 16)
            G.wait_ge(s_st.h, s_st.n)
            s_cc.inc(G.collective_compute(
                "AllGather", ALU.bypass, replica_groups=[list(range(NCORES))],
                ins=[h_slots.ap()], outs=[h_slots_all.ap()]))

        nc.all_engine_barrier()

        with contextlib.ExitStack() as f3:
            SB3 = lambda *a: f3.enter_context(nc.sbuf_tensor(*a))
            sa = SB3("sa", [128, 8 * 2 * NL * 128], BF16)
            b_rows = SB3("b_rows", [1, NL * D], BF16)
            s_ld.inc(G.dma_start(out=b_rows[0:1, :],
                                 in_=AP(bffn_in.ap().tensor, 0,
                                        [[NL * D, 1], [1, NL * D]])), 16)
            wfa = SB3("wfa", [128, 8 * D], BF16)
            wfb = SB3("wfb", [128, 8 * D], BF16)
            y_ev = SB3("y_ev", [128, D], BF16)
            py1 = f3.enter_context(nc.psum_tensor("py1", [128, D], F32))
            py2 = f3.enter_context(nc.psum_tensor("py2", [128, D], F32))

            def load_wffn(nl):
                wb = wfa if nl % 2 == 0 else wfb
                s_ld.inc(G.dma_start(
                    out=wb[:, :].rearrange("p (dc e) -> p dc e", dc=8),
                    in_=AP(wffn_in.ap().tensor, nl * D * D,
                           [[D, 128], [128 * D, 8], [1, D]])), 16)
                ld_marks.append(s_ld.n)

            ld_marks = []
            load_wffn(0)
            load_wffn(1)
            G.wait_ge(s_cc.h, s_cc.n)
            with G.register("sar") as sareg:
                G.reg_load(sareg, dynoffs_in[0:1, 0:1])
                off = G.snap(sareg)
                s_ld.inc(G.dma_start(
                    out=h_smine.ap(),
                    in_=AP(h_slots_all.ap().tensor, off,
                           [[NP, NCORES * D], [1, 256]]),
                    bounds_check=None), 16)
            G.wait_ge(s_ld.h, s_ld.n)
            for b in range(NCORES):
                mh, rb = b // 4, b % 4
                for dc in range(8):
                    src_rows = h_smine[b * D + dc * 128: b * D + (dc + 1) * 128, :]
                    src3 = src_rows.rearrange("p (n c) -> p n c", c=32)
                    s_ld.inc(G.dma_start(
                        out=AP(sa[:, :].tensor,
                               (dc * 2 + mh) * NL * 128 + rb * 32,
                               [[8 * 2 * NL * 128, 128], [128, NL], [1, 32]]),
                        in_=src3), 16)
            cap_sa = s_ld.n
            T.wait_ge(s_ld.h, cap_sa)
            pe_f3 = s_pe.n
            for nl in range(NL):
                wb = wfa if nl % 2 == 0 else wfb
                if nl + 2 < NL:
                    G.wait_ge(s_pe.h, pe_f3 + 2 * nl + 2)
                    load_wffn(nl + 2)
                for mh in range(2):
                    py = py1 if mh == 0 else py2
                    T.wait_ge(s_ld.h, ld_marks[nl])
                    T.wait_ge(s_st.h, s_st.n)
                    if not (nl == 0 and mh == 0):
                        T.wait_ge(s_ve.h, s_ve.n)
                    lbase = (0 * 2 + mh) * NL + nl
                    for dc in range(8):
                        lhs = sa[:, ((dc * 2 + mh) * NL + nl) * 128:
                                 ((dc * 2 + mh) * NL + nl) * 128 + 128]
                        T.matmul(py[:, 0:512], lhs, wb[:, dc * D:dc * D + 512],
                                 start=(dc == 0), stop=False)
                        T.matmul(py[:, 512:1024], lhs, wb[:, dc * D + 512:(dc + 1) * D],
                                 start=(dc == 0), stop=False)
                    T.matmul(py[:, 0:512], ones_bf[0:1, :],
                             b_rows[0:1, nl * D:nl * D + 512], start=False, stop=True)
                    T.matmul(py[:, 512:1024], ones_bf[0:1, :],
                             b_rows[0:1, nl * D + 512:(nl + 1) * D], start=False, stop=True)
                    s_pe.inc(T.last)
                    V.wait_ge(s_pe.h, s_pe.n)
                    if not (nl == 0 and mh == 0):
                        V.wait_ge(s_st.h, s_st.n)
                    V.tensor_copy(y_ev[:, :], py[:, :])
                    s_ve.inc(V.last)
                    Y.wait_ge(s_ve.h, s_ve.n)
                    s_st.inc(Y.dma_start(
                        out=h_y[nl * 256 + mh * 128: nl * 256 + (mh + 1) * 128, :],
                        in_=y_ev[:, :]), 16)
            G.wait_ge(s_st.h, s_st.n)
            s_cc.inc(G.collective_compute(
                "AllGather", ALU.bypass, replica_groups=[list(range(NCORES))],
                ins=[h_y.ap()], outs=[h_y_all.ap()]))

        nc.all_engine_barrier()

        with contextlib.ExitStack() as f4:
            SB4 = lambda *a: f4.enter_context(nc.sbuf_tensor(*a))
            y_b = SB4("y_b", [128, 16 * D], BF16)
            et = SB4("et", [128, 16 * 1024], BF16)
            g_bf = SB4("g_bf", [128, D], BF16)
            s_t = SB4("s_t", [128, D], BF16)
            x_t = SB4("x_t", [128, D], F32)
            t2 = SB4("t2", [128, D], BF16)
            outf = SB4("outf", [128, D], F32)
            po1 = f4.enter_context(nc.psum_tensor("po1", [128, D], F32))
            po2 = f4.enter_context(nc.psum_tensor("po2", [128, D], F32))
            MS = min(1024, N)
            for npc in range(16):
                s_tr.inc(Y.dma_start(
                    out=et[:, npc * 1024:npc * 1024 + MS],
                    in_=h_E[0:MS, npc * 128:(npc + 1) * 128],
                    transpose=True), 16)
            G.wait_ge(s_cc.h, s_cc.n)
            with G.register("ybr") as ybreg:
                G.reg_load(ybreg, dynoffs_in[0:1, 1:2])
                bboff = G.snap(ybreg)
                for i in range(NCORES):
                    view = h_y_all[i * 2048:(i + 1) * 2048, :].rearrange(
                        "(nl bb p) e -> nl bb (p e)", nl=NL, bb=NCORES)
                    s_ld.inc(G.dma_start(
                        out=h_ymine[i * 256:(i + 1) * 256, :].rearrange(
                            "(nl p) e -> nl (p e)", nl=NL).unsqueeze(1),
                        in_=view[:, bass.ds(bboff, 1), :],
                        bounds_check=None), 16)
            G.wait_ge(s_ld.h, s_ld.n)
            for npc in range(16):
                s_ld.inc(G.dma_start(
                    out=y_b[:, npc * D:(npc + 1) * D],
                    in_=h_ymine[npc * 128:(npc + 1) * 128, :]), 16)
            for ms in range(N // MS):
                if ms > 0:
                    Y.wait_ge(s_pe.h, s_pe.n)
                    for npc in range(16):
                        s_tr.inc(Y.dma_start(
                            out=et[:, npc * 1024:npc * 1024 + MS],
                            in_=h_E[ms * MS:(ms + 1) * MS, npc * 128:(npc + 1) * 128],
                            transpose=True), 16)
                for rr in range(MS // 128):
                    mt = ms * (MS // 128) + rr
                    po = po1 if mt % 2 == 0 else po2
                    T.wait_ge(s_tr.h, s_tr.n)
                    T.wait_ge(s_ld.h, s_ld.n)
                    if mt >= 2:
                        T.wait_ge(s_sc.h, s_sc.n)
                    for npc in range(16):
                        lhs = et[:, npc * 1024 + rr * 128: npc * 1024 + (rr + 1) * 128]
                        T.matmul(po[:, 0:512], lhs, y_b[:, npc * D:npc * D + 512],
                                 start=(npc == 0), stop=(npc == 15))
                        T.matmul(po[:, 512:1024], lhs, y_b[:, npc * D + 512:(npc + 1) * D],
                                 start=(npc == 0), stop=(npc == 15))
                    s_pe.inc(T.last)
                    S.wait_ge(s_pe.h, s_pe.n)
                    if mt >= 1:
                        S.wait_ge(s_ve.h, s_ve.n)
                    s_sc.inc(S.activation(g_bf[:, :], po[:, :], AF.Silu,
                                          scale=rc_all[:, mt:mt + 1]))
                    s_st.inc(Y.dma_start(out=s_t[:, :],
                                         in_=h_s[mt * 128:(mt + 1) * 128, :]), 16)
                    s_st.inc(Y.dma_start(out=x_t[:, :],
                                         in_=x_in[mt * 128:(mt + 1) * 128, :]), 16)
                    V.wait_ge(s_sc.h, s_sc.n)
                    V.wait_ge(s_st.h, s_st.n)
                    V.tensor_mul(t2[:, :], s_t[:, :], g_bf[:, :])
                    s_ve.inc(V.tensor_add(outf[:, :], x_t[:, :], t2[:, :]))
                    Y.wait_ge(s_ve.h, s_ve.n)
                    s_out.inc(Y.dma_start(out=out_t[mt * 128:(mt + 1) * 128, :],
                                          in_=outf[:, :]), 16)
            Y.wait_ge(s_out.h, s_out.n)

        if dbg:
            nc.all_engine_barrier()
            with contextlib.ExitStack() as dd:
                cp = dd.enter_context(nc.sbuf_tensor("dbgcp", [128, NP], BF16))
                cpf = dd.enter_context(nc.sbuf_tensor("dbgcf", [128, NP], F32))
                for name, dst in dbg.items():
                    hs = dbg_src[name]
                    wid = int(hs.shape[1])
                    for mt in range(int(hs.shape[0]) // 128):
                        Y.wait_ge(s_ve.h, s_ve.n)
                        s_st.inc(Y.dma_start(out=cp[:, 0:wid],
                                             in_=hs[mt * 128:(mt + 1) * 128, :]), 16)
                        V.wait_ge(s_st.h, s_st.n)
                        s_ve.inc(V.tensor_copy(cpf[:, 0:wid], cp[:, 0:wid]))
                        Y.wait_ge(s_ve.h, s_ve.n)
                        s_out.inc(Y.dma_start(out=dst[mt * 128:(mt + 1) * 128, :],
                                              in_=cpf[:, 0:wid]), 16)
                Y.wait_ge(s_out.h, s_out.n)

        nc.all_engine_barrier()
        Y.wait_ge(s_out.h, s_out.n)

    return nc


_NC_CACHE = {}


def kernel(L0_attn_x, L0_attn_Wq, L0_attn_Wk, L0_attn_Wv, L0_attn_Wf,
           L0_attn_Wg1, L0_attn_Wg2, L0_attn_ln_w, L0_attn_Wo,
           L0_ffn_phi, L0_ffn_W, L0_ffn_b):
    N = int(np.asarray(L0_attn_x).shape[1])
    key = (N, tuple(DEBUG_OUTS))
    if key not in _NC_CACHE:
        _NC_CACHE[key] = build_nc(N=N, debug=tuple(DEBUG_OUTS))
    nc = _NC_CACHE[key]
    x = np.ascontiguousarray(np.asarray(L0_attn_x, dtype=np.float32))
    com = {
        "wq": np.ascontiguousarray(np.asarray(L0_attn_Wq, np.float32)),
        "wk": np.ascontiguousarray(np.asarray(L0_attn_Wk, np.float32)),
        "wv": np.ascontiguousarray(np.asarray(L0_attn_Wv, np.float32)),
        "wo": np.ascontiguousarray(np.asarray(L0_attn_Wo, np.float32)),
        "wf": np.ascontiguousarray(np.asarray(L0_attn_Wf, np.float32)),
        "wg1": np.ascontiguousarray(np.asarray(L0_attn_Wg1, np.float32)),
        "wg2": np.ascontiguousarray(np.asarray(L0_attn_Wg2, np.float32)),
        "lnw": np.ascontiguousarray(np.asarray(L0_attn_ln_w, np.float32)),
        "phi": np.ascontiguousarray(np.asarray(L0_ffn_phi, np.float32).reshape(D, NP)),
    }
    W = np.asarray(L0_ffn_W, np.float32)
    B = np.asarray(L0_ffn_b, np.float32)
    in_maps = []
    for c in range(NCORES):
        m = dict(com)
        m["x"] = np.ascontiguousarray(x[c])
        m["wffn"] = np.ascontiguousarray(W[c * NL:(c + 1) * NL])
        m["bffn"] = np.ascontiguousarray(B[c * NL:(c + 1) * NL])
        m["dynoffs"] = np.array([[256 * c, c]], dtype=np.int32)
        in_maps.append(m)
    res = run_bass_kernel_spmd(nc, in_maps, core_ids=list(range(NCORES)))
    out = np.stack([np.asarray(res.results[c]["out"]) for c in range(NCORES)], axis=0)
    return out


# version 20
# speedup vs baseline: 1.0355x; 1.0067x over previous
import contextlib
import numpy as np

import concourse.bass as bass
import concourse.mybir as mybir
from concourse.ap import AP
from concourse.bass_utils import run_bass_kernel_spmd

F32 = mybir.dt.float32
BF16 = mybir.dt.bfloat16
I32 = mybir.dt.int32
AF = mybir.ActivationFunctionType
ALU = mybir.AluOpType

NCORES = 8
D = 1024
H = 16
HD = 64
SLAB = 512
NE = 64
NL = 8
PSL = 32
NP = NE * PSL

DEBUG_OUTS = []
PHASES = 'all'


class Sem:
    def __init__(self, ctx, nc, name):
        self.h = ctx.enter_context(nc.semaphore(name))
        self.n = 0

    def inc(self, inst, k=1):
        inst.then_inc(self.h, k)
        self.n += k
        return inst

    def wait(self, eng, n=None):
        eng.wait_ge(self.h, self.n if n is None else n)


def _rep(tile_ap, offset, dims, pcount=128):
    t = tile_ap.tensor
    pitch = int(t.shape[-1])
    return AP(t, offset, [[pitch, pcount]] + [list(d) for d in dims])


def build_nc(N=4096, debug=()):
    nc = bass.Bass("TRN2", target_bir_lowering=False,
                   detect_race_conditions=False)
    NT = N // 128
    NSLAB = N // SLAB
    NCH = N // 128

    x_in = nc.dram_tensor("x", [N, D], F32, kind="ExternalInput")
    wq_in = nc.dram_tensor("wq", [D, D], F32, kind="ExternalInput")
    wk_in = nc.dram_tensor("wk", [D, D], F32, kind="ExternalInput")
    wv_in = nc.dram_tensor("wv", [D, D], F32, kind="ExternalInput")
    wo_in = nc.dram_tensor("wo", [D, D], F32, kind="ExternalInput")
    wf_in = nc.dram_tensor("wf", [H, D], F32, kind="ExternalInput")
    wg1_in = nc.dram_tensor("wg1", [64, D], F32, kind="ExternalInput")
    wg2_in = nc.dram_tensor("wg2", [D, 64], F32, kind="ExternalInput")
    lnw_in = nc.dram_tensor("lnw", [D], F32, kind="ExternalInput")
    phi_in = nc.dram_tensor("phi", [D, NP], F32, kind="ExternalInput")
    wffn_in = nc.dram_tensor("wffn", [NL, D, D], F32, kind="ExternalInput")
    bffn_in = nc.dram_tensor("bffn", [NL, D], F32, kind="ExternalInput")
    dynoffs_in = nc.dram_tensor("dynoffs", [1, 2], I32, kind="ExternalInput")
    out_t = nc.dram_tensor("out", [N, D], F32, kind="ExternalOutput")

    h_xn = nc.dram_tensor("h_xn", [N, D], BF16)
    h_q = nc.dram_tensor("h_q", [N, H * 128], BF16)
    h_k = nc.dram_tensor("h_k", [N, H * 128], BF16)
    h_g1 = nc.dram_tensor("h_g1", [N, 128], BF16)
    h_oln = nc.dram_tensor("h_oln", [N, D], BF16)
    h_s = nc.dram_tensor("h_s", [N, D], BF16)
    h_E = nc.dram_tensor("h_E", [N, NP], BF16)
    h_Z = nc.dram_tensor("h_Z", [N, NP], BF16, addr_space="Shared")
    h_slots = nc.dram_tensor("h_slots", [D, NP], BF16)
    h_slots_all = nc.dram_tensor("h_slots_all", [NCORES * D, NP], BF16,
                                 addr_space="Shared")
    h_y = nc.dram_tensor("h_y", [NL * 256, D], BF16)
    h_y_all = nc.dram_tensor("h_y_all", [NCORES * NL * 256, D], BF16,
                             addr_space="Shared")
    h_smine = nc.dram_tensor("h_smine", [NCORES * D, NL * PSL], BF16)
    h_ymine = nc.dram_tensor("h_ymine", [NP, D], BF16)
    h_wqb = nc.dram_tensor("h_wqb", [D, D], BF16)
    h_wkb = nc.dram_tensor("h_wkb", [D, D], BF16)
    h_wvb = nc.dram_tensor("h_wvb", [D, D], BF16)
    h_wob = nc.dram_tensor("h_wob", [D, D], BF16)
    h_wfb = nc.dram_tensor("h_wfb", [H, D], BF16)
    h_wg1b = nc.dram_tensor("h_wg1b", [64, D], BF16)
    h_wg2b = nc.dram_tensor("h_wg2b", [D, 128], BF16)

    dbg = {}
    dbg_src = {"xn_early": h_xn, "xn": h_xn, "q": h_q, "k": h_k, "g1": h_g1, "oln": h_oln,
               "s": h_s, "E": h_E, "Z": h_Z, "slots": h_slots, "y": h_y,
               "smine": h_smine, "ymine": h_ymine, "yall": h_y_all}
    for name in debug:
        hs = dbg_src[name]
        dbg[name] = nc.dram_tensor("dbg_" + name, [int(hs.shape[0]), int(hs.shape[1])],
                                   F32, kind="ExternalOutput")

    with contextlib.ExitStack() as top:
        SBt = lambda *a: top.enter_context(nc.sbuf_tensor(*a))
        T0, V0, S0_, G0, Y0 = nc.tensor, nc.vector, nc.scalar, nc.gpsimd, nc.sync

        class EngTrack:
            def __init__(self, eng):
                self.eng = eng
                self.last = None

            def __getattr__(self, name):
                fn = getattr(self.eng, name)
                if callable(fn) and name not in ("wait_ge",):
                    def wrap(*a, **k):
                        inst = fn(*a, **k)
                        self.last = inst
                        return inst
                    return wrap
                return fn

            def wait_ge(self, sem, n):
                return self.eng.wait_ge(sem, n)

        T, V, S, G, Y = (EngTrack(T0), EngTrack(V0), EngTrack(S0_), EngTrack(G0),
                         EngTrack(Y0))

        u_bf = SBt("u_bf", [128, 128], BF16)
        lst_bf = SBt("lst_bf", [128, 128], BF16)
        lstpos = SBt("lstpos", [128, 128], BF16)
        id_bf = SBt("id_bf", [128, 128], BF16)
        ones_bf = SBt("ones_bf", [1, 128], BF16)
        lnw_sb = SBt("lnw_sb", [128, 8], F32)
        rs_all = SBt("rs_all", [128, NT], F32)
        rc_all = SBt("rc_all", [128, NT], F32)
        rc_tmp = SBt("rc_tmp", [128, NT], F32)
        eps_col = SBt("eps_col", [128, 1], F32)
        eps24_col = SBt("eps24_col", [128, 1], F32)
        vj1 = SBt("vj1", [128, 16], F32)
        vj2 = SBt("vj2", [128, 16], F32)
        wfT = SBt("wfT", [128, 8 * H], BF16)
        wg1T = SBt("wg1T", [128, 8 * 64], BF16)
        wg2T = SBt("wg2T", [128, D], BF16)
        watt = SBt("watt", [128, 4 * 8 * D], BF16)
        wqT = lambda ic: watt[:, (0 * 8 + ic) * D:(0 * 8 + ic + 1) * D]
        wkT = lambda ic: watt[:, (1 * 8 + ic) * D:(1 * 8 + ic + 1) * D]
        wvT = lambda ic: watt[:, (2 * 8 + ic) * D:(2 * 8 + ic + 1) * D]
        woT = lambda ic: watt[:, (3 * 8 + ic) * D:(3 * 8 + ic + 1) * D]
        xnf = SBt("xnf", [128, 8 * SLAB], BF16)
        xnf2 = SBt("xnf2", [128, 8 * SLAB], BF16)

        s_g = Sem(top, nc, "s_g")
        s_ld = Sem(top, nc, "s_ld")
        s_st = Sem(top, nc, "s_st")
        s_tr = Sem(top, nc, "s_tr")
        s_pe = Sem(top, nc, "s_pe")
        s_ve = Sem(top, nc, "s_ve")
        s_sc = Sem(top, nc, "s_sc")
        s_cc = Sem(top, nc, "s_cc")
        s_out = Sem(top, nc, "s_out")
        s_x = Sem(top, nc, "s_x")

        def gstep(inst):
            s_g.inc(inst)
            G.wait_ge(s_g.h, s_g.n)

        gstep(G.memset(u_bf[:, :], 0.0))
        gstep(G.affine_select(out=u_bf[:, :], in_=u_bf[:, :], compare_op=ALU.is_gt,
                              fill=1.0, base=0, pattern=[[-1, 128]],
                              channel_multiplier=1))
        gstep(G.memset(lst_bf[:, :], 1.0))
        gstep(G.affine_select(out=lst_bf[:, :], in_=lst_bf[:, :], compare_op=ALU.is_gt,
                              fill=0.0, base=0, pattern=[[-1, 128]],
                              channel_multiplier=1))
        gstep(G.memset(lstpos[:, :], 1e30))
        gstep(G.affine_select(out=lstpos[:, :], in_=lstpos[:, :], compare_op=ALU.is_gt,
                              fill=0.0, base=0, pattern=[[-1, 128]],
                              channel_multiplier=1))
        gstep(G.memset(id_bf[:, :], 0.0))
        gstep(G.affine_select(out=id_bf[:, :], in_=id_bf[:, :], compare_op=ALU.not_equal,
                              fill=1.0, base=0, pattern=[[-1, 128]],
                              channel_multiplier=1))
        gstep(G.memset(ones_bf[0:1, :], 1.0))
        gstep(G.memset(eps_col[:, :], 1e-5))
        gstep(G.memset(eps24_col[:, :], 1e-24))
        gstep(G.memset(vj1[:, :], 0.0))
        gstep(G.memset(vj2[:, :], 0.0))

        s_st.inc(Y.dma_start(out=lnw_sb[:, :],
                             in_=AP(lnw_in.ap().tensor, 0, [[1, 128], [128, 8]]),
                             allow_slow_non_contiguous=True), 16)
        with contextlib.ExitStack() as p0:
            SB0 = lambda *a: p0.enter_context(nc.sbuf_tensor(*a))
            wnat = SB0("wnat", [128, 8 * D], BF16)
            Xb = [SB0(f"xld{i}", [128, D], F32) for i in range(4)]
            XO = [SB0(f"xnt{i}", [128, D], BF16) for i in range(2)]
            musum = [SB0(f"musum{i}", [128, 1], F32) for i in range(2)]
            ssc = [SB0(f"ssc{i}", [128, 1], F32) for i in range(2)]
            ln_t1 = [SB0(f"ln_t1{i}", [128, 1], F32) for i in range(2)]
            ln_t2 = [SB0(f"ln_t2{i}", [128, 1], F32) for i in range(2)]
            ln_r = [SB0(f"ln_r{i}", [128, 1], F32) for i in range(2)]
            ln_nm = [SB0(f"ln_nm{i}", [128, 1], F32) for i in range(2)]
            ln_v1 = [SB0(f"ln_v1{i}", [128, 1], F32) for i in range(2)]
            ln_v2 = [SB0(f"ln_v2{i}", [128, 1], F32) for i in range(2)]
            ln_v3 = SB0("ln_v30", [128, 1], F32)
            sq_junk = [SB0(f"sq_junk{i}", [128, D], BF16) for i in range(2)]
            phi_sb = SB0("phi_sb", [128, 8 * NP], BF16)
            e_sb = [SB0("e_sb", [128, NP], BF16), SB0("e_sb2", [128, NP], BF16)]
            pl = [p0.enter_context(nc.psum_tensor("pl", [128, NP], F32)),
                  p0.enter_context(nc.psum_tensor("pl2", [128, NP], F32))]

            s_ld.inc(G.dma_start(
                out=phi_sb[:, :].rearrange("p (ic n) -> p ic n", ic=8),
                in_=AP(phi_in.ap().tensor, 0, [[NP, 128], [128 * NP, 8], [1, NP]])), 16)
            cap_phi = s_ld.n

            lnw_row = SB0("lnw_row", [128, D], F32)
            wosc = SB0("wosc", [128, D], BF16)
            s_ld.inc(G.dma_start(out=lnw_row[:, :],
                                 in_=AP(lnw_in.ap().tensor, 0, [[0, 128], [1, D]])), 16)
            for wi, (src, hb) in enumerate([(wo_in, h_wob), (wq_in, h_wqb),
                                            (wk_in, h_wkb), (wv_in, h_wvb)]):
                s_ld.inc(G.dma_start(
                    out=wnat[:, :].rearrange("p (jc i) -> p jc i", jc=8),
                    in_=AP(src.ap().tensor, 0, [[D, 128], [128 * D, 8], [1, D]])), 16)
                Y.wait_ge(s_ld.h, s_ld.n)
                if wi == 0:
                    V.wait_ge(s_ld.h, s_ld.n)
                    for jc in range(8):
                        if jc > 0:
                            V.wait_ge(s_st.h, s_st.n)
                        s_ve.inc(V.tensor_mul(wosc[:, :], wnat[:, jc * D:(jc + 1) * D],
                                              lnw_row[:, :]))
                        Y.wait_ge(s_ve.h, s_ve.n)
                        s_st.inc(Y.dma_start(out=hb[jc * 128:(jc + 1) * 128, :],
                                             in_=wosc[:, :]), 16)
                else:
                    for jc in range(8):
                        s_st.inc(Y.dma_start(out=hb[jc * 128:(jc + 1) * 128, :],
                                             in_=wnat[:, jc * D:(jc + 1) * D]), 16)
                Y.wait_ge(s_st.h, s_st.n)
                for ic in range(8):
                    dst = [woT, wqT, wkT, wvT][wi](ic)
                    s_tr.inc(Y.dma_start(out=dst, in_=hb[:, ic * 128:(ic + 1) * 128],
                                         transpose=True), 16)
                G.wait_ge(s_st.h, s_st.n)
            s_ld.inc(G.dma_start(out=wnat[0:H, 0:D], in_=wf_in[:, :]), 16)
            Y.wait_ge(s_ld.h, s_ld.n)
            s_st.inc(Y.dma_start(out=h_wfb[:, :], in_=wnat[0:H, 0:D]), 16)
            Y.wait_ge(s_st.h, s_st.n)
            for ic in range(8):
                s_tr.inc(Y.dma_start(out=wfT[:, ic * H:(ic + 1) * H],
                                     in_=h_wfb[:, ic * 128:(ic + 1) * 128],
                                     transpose=True), 16)
            G.wait_ge(s_st.h, s_st.n)
            s_ld.inc(G.dma_start(out=wnat[0:64, 0:D], in_=wg1_in[:, :]), 16)
            Y.wait_ge(s_ld.h, s_ld.n)
            s_st.inc(Y.dma_start(out=h_wg1b[:, :], in_=wnat[0:64, 0:D]), 16)
            Y.wait_ge(s_st.h, s_st.n)
            for ic in range(8):
                s_tr.inc(Y.dma_start(out=wg1T[:, ic * 64:(ic + 1) * 64],
                                     in_=h_wg1b[:, ic * 128:(ic + 1) * 128],
                                     transpose=True), 16)
            G.wait_ge(s_st.h, s_st.n)
            wg2pad = SB0("wg2pad", [128, 8 * 128], BF16)
            gstep(G.memset(wg2pad[:, :], 0.0))
            s_ld.inc(G.dma_start(
                out=wg2pad[:, :].rearrange("p (jc i) -> p jc i", jc=8)[:, :, 0:64],
                in_=AP(wg2_in.ap().tensor, 0, [[64, 128], [128 * 64, 8], [1, 64]])), 16)
            Y.wait_ge(s_ld.h, s_ld.n)
            Y.wait_ge(s_g.h, s_g.n)
            for jc in range(8):
                s_st.inc(Y.dma_start(out=h_wg2b[jc * 128:(jc + 1) * 128, :],
                                     in_=wg2pad[:, jc * 128:(jc + 1) * 128]), 16)
            Y.wait_ge(s_st.h, s_st.n)
            s_tr.inc(Y.dma_start(out=wg2T[:, :], in_=h_wg2b[:, :], transpose=True), 16)

            cap_x = {}; cap_sq = {}; cap_sscv = {}; cap_chain = {}; cap_xo = {}
            cap_xnst = {}; cap_Est = {}; cap_rs = {}; cap_f1pe = {}
            sc_marks = {}
            cap_xnf = {}
            cap_f1mm = {}

            def f1_evac(m):
                pp = pl[m % 2]
                ee = e_sb[m % 2]
                S.wait_ge(s_pe.h, cap_f1pe[m])
                if m >= 2:
                    S.wait_ge(s_st.h, cap_Est[m - 2])
                s_sc.inc(S.activation(ee[:, :], pp[:, :], AF.Exp,
                                      accum_out=rs_all[:, m:m + 1]))
                sc_marks[m] = s_sc.n
                Y.wait_ge(s_sc.h, sc_marks[m])
                s_st.inc(Y.dma_start(out=h_E[m * 128:(m + 1) * 128, :], in_=ee[:, :]), 16)
                cap_Est[m] = s_st.n

            for mt in range(NT):
                sl, r = mt // 4, mt % 4
                p = mt % 2
                xt = Xb[mt % 4]
                xo = XO[p]
                if mt >= 4:
                    G.wait_ge(s_ve.h, cap_xo[mt - 4])
                s_x.inc(G.dma_start(out=xt[:, :], in_=x_in[mt * 128:(mt + 1) * 128, :]), 16)
                cap_x[mt] = s_x.n
                V.wait_ge(s_x.h, cap_x[mt])
                if mt >= 2:
                    V.wait_ge(s_sc.h, cap_chain[mt - 2])
                s_ve.inc(V.tensor_reduce(musum[p][:, :], xt[:, :],
                                         mybir.AxisListType.X, ALU.add))
                cap_mu = s_ve.n
                S.wait_ge(s_x.h, cap_x[mt])
                if mt >= 2:
                    S.wait_ge(s_ve.h, cap_sscv[mt - 2])
                s_sc.inc(S.activation(sq_junk[p][:, :], xt[:, :], AF.Square))
                cap_sq[mt] = s_sc.n
                V.wait_ge(s_sc.h, cap_sq[mt])
                s_ve.inc(V.tensor_reduce(ssc[p][:, :], sq_junk[p][:, :],
                                         mybir.AxisListType.X, ALU.add))
                cap_sscv[mt] = s_ve.n
                S.wait_ge(s_ve.h, cap_sscv[mt])
                if mt >= 2:
                    S.wait_ge(s_ve.h, cap_xo[mt - 2])
                S.activation(ln_t1[p][:, :], musum[p][:, :], AF.Identity, scale=1.0 / D)
                S.activation(ln_t2[p][:, :], ln_t1[p][:, :], AF.Square)
                S.activation(ln_v2[p][:, :], ln_t2[p][:, :], AF.Identity, scale=-1.0)
                S.activation(ln_nm[p][:, :], ln_t1[p][:, :], AF.Identity, scale=-1.0)
                S.activation(ln_v3[:, :], eps_col[:, :], AF.Identity)
                S.activation(ln_v1[p][:, :], ssc[p][:, :], AF.Identity, scale=1.0 / D,
                             bias=ln_v2[p][:, 0:1])
                S.activation(ln_t2[p][:, :], ln_v1[p][:, :], AF.Ln, bias=eps_col[:, 0:1])
                s_sc.inc(S.activation(ln_r[p][:, :], ln_t2[p][:, :], AF.Exp, scale=-0.5))
                cap_chain[mt] = s_sc.n
                V.wait_ge(s_sc.h, cap_chain[mt])
                if mt >= 2:
                    V.wait_ge(s_st.h, cap_xnst[mt - 2])
                s_ve.inc(V.tensor_scalar(xo[:, :], xt[:, :], ln_nm[p][:, 0:1],
                                         ln_r[p][:, 0:1], ALU.add, ALU.mult))
                cap_xo[mt] = s_ve.n
                Y.wait_ge(s_ve.h, cap_xo[mt])
                s_st.inc(Y.dma_start(out=h_xn[mt * 128:(mt + 1) * 128, :], in_=xo[:, :]), 16)
                cap_xnst[mt] = s_st.n

                if r == 3:
                    xnb = xnf if sl % 2 == 0 else xnf2
                    Y.wait_ge(s_st.h, cap_xnst[mt])
                    if sl > 1:
                        Y.wait_ge(s_pe.h, cap_f1mm[sl - 2])
                    for ic in range(8):
                        s_tr.inc(Y.dma_start(
                            out=xnb[:, ic * SLAB:(ic + 1) * SLAB],
                            in_=h_xn[sl * SLAB:(sl + 1) * SLAB, ic * 128:(ic + 1) * 128],
                            transpose=True), 16)
                    cap_xnf[sl] = s_tr.n
                    for r4 in range(4):
                        m2 = sl * 4 + r4
                        pp = pl[m2 % 2]
                        if r4 >= 2:
                            f1_evac(m2 - 2)
                        T.wait_ge(s_tr.h, cap_xnf[sl])
                        T.wait_ge(s_ld.h, cap_phi)
                        if m2 >= 2:
                            T.wait_ge(s_sc.h, sc_marks[m2 - 2])
                        for ic in range(8):
                            lhs = xnb[:, ic * SLAB + r4 * 128: ic * SLAB + (r4 + 1) * 128]
                            for nb in range(4):
                                T.matmul(pp[:, nb * 512:(nb + 1) * 512], lhs,
                                         phi_sb[:, ic * NP + nb * 512: ic * NP + (nb + 1) * 512],
                                         start=(ic == 0), stop=(ic == 7))
                        s_pe.inc(T.last)
                        cap_f1pe[m2] = s_pe.n
                    cap_f1mm[sl] = s_pe.n
                    f1_evac(sl * 4 + 2)
                    f1_evac(sl * 4 + 3)

            S.activation(ln_v3[:, :], eps_col[:, :], AF.Identity)
            S.activation(ln_v3[:, :], eps_col[:, :], AF.Identity)
            S.activation(rc_tmp[:, :], rs_all[:, :], AF.Ln)
            s_sc.inc(S.activation(rc_all[:, :], rc_tmp[:, :], AF.Exp, scale=-1.0))
            G.wait_ge(s_st.h, cap_Est[NT - 1])
            s_cc.inc(G.collective_compute(
                "AllReduce", ALU.add, replica_groups=[list(range(NCORES))],
                ins=[h_E.ap()], outs=[h_Z.ap()]))

        cap_f1_sc = s_sc.n
        cap_f1_pe = s_pe.n
        cap_f1_st = s_st.n
        cap_f1_ve = s_ve.n

        with contextlib.ExitStack() as aa:
            SBA = lambda *a: aa.enter_context(nc.sbuf_tensor(*a))
            k_bf = [SBA("k_bf0", [128, 4 * D], BF16), SBA("k_bf1", [128, 4 * D], BF16)]
            v_sl = [SBA("v_sl0", [128, 4 * D], BF16), SBA("v_sl1", [128, 4 * D], BF16)]
            gate_sl = SBA("gate_sl", [128, 4 * D], BF16)
            q_fm = SBA("q_fm", [128, H * SLAB], BF16)
            k_fm = SBA("k_fm", [128, H * SLAB], BF16)
            g1f = SBA("g1f", [128, SLAB], BF16)
            q_bf = SBA("q_bf", [128, H * 128], BF16)
            k_pad = SBA("k_pad", [128, H * 128], BF16)
            g1_bf = SBA("g1_bf", [128, 128], BF16)
            gt_raw = SBA("gt_raw", [128, 512], BF16)
            B1 = SBA("B1", [128, 2048], BF16)
            B2 = SBA("B2", [128, 2048], BF16)
            B3 = SBA("B3", [128, 2048], BF16)
            e1f = SBA("e1f", [128, 64], F32)
            e1g = SBA("e1g", [128, 64], F32)
            phif = SBA("phif", [128, 64], F32)
            phih = SBA("phih", [128, 64], BF16)
            phil = SBA("phil", [128, 64], BF16)
            ssn = SBA("ssn", [128, 64], F32)
            ssnb = SBA("ssnb", [128, 64], F32)
            rn_sl = SBA("rn_sl", [128, 64], F32)
            expF = SBA("expF", [128, H], F32)
            expK = SBA("expK", [128, H], F32)
            ekrn = SBA("ekrn", [128, H], F32)
            vpp = SBA("vpp", [128, D], BF16)
            dec_col = SBA("dec_col", [128, H], F32)
            SA_ = SBA("SA_", [64, H * HD], F32)
            SBf = SBA("SBf", [64, H * HD], F32)
            S_bf = SBA("S_bf", [64, H * HD], BF16)
            og = SBA("og", [128, D], BF16)
            ogg = SBA("ogg", [128, D], BF16)
            oln_bf = SBA("oln_bf", [128, D], BF16)
            olnf = SBA("olnf", [128, 8 * SLAB], BF16)
            musum = SBA("musumA", [128, 1], F32)
            ssc = SBA("sscA", [128, 1], F32)
            ln_t1 = SBA("ln_t1A", [128, 1], F32)
            ln_t2 = SBA("ln_t2A", [128, 1], F32)
            ln_r = SBA("ln_rA", [128, 1], F32)
            ln_nm = SBA("ln_nmA", [128, 1], F32)
            ln_v1 = SBA("ln_v1A", [128, 1], F32)
            ln_v2 = SBA("ln_v2A", [128, 1], F32)
            ln_v3 = SBA("ln_v3A", [128, 1], F32)
            pS = [aa.enter_context(nc.psum_tensor(f"pS{i}", [128, 512], F32))
                  for i in range(4)]
            pBig = aa.enter_context(nc.psum_tensor("pBigA", [128, 2048], F32))

            G.wait_ge(s_st.h, cap_f1_st)
            G.wait_ge(s_pe.h, cap_f1_pe)
            G.wait_ge(s_ve.h, cap_f1_ve)
            G.wait_ge(s_sc.h, cap_f1_sc)
            gstep(G.memset(SA_[:, :], 0.0))
            gstep(G.memset(S_bf[:, :], 0.0))
            gstep(G.memset(q_bf[:, :], 0.0))
            gstep(G.memset(k_pad[:, :], 0.0))
            gstep(G.memset(g1_bf[:, :], 0.0))
            T.wait_ge(s_sc.h, cap_f1_sc)
            S.wait_ge(s_g.h, s_g.n)
            V.wait_ge(s_g.h, s_g.n)

            bank_free = [cap_f1_sc] * 4
            cap_state = 0
            cap_expdv = 0
            cap_pmul = 0
            cap_combh0 = 0
            cap_vpp = 0
            cap_oh1 = 0
            cap_kv = 0
            cap_phiev = 0
            cap_flphi = 0
            cap_phimm = 0
            cap_e1g = 0
            cap_rn = 0
            cap_phcp = 0
            cap_s3ev2 = 0
            cap_s3st_last = 0
            cap_s3mm_last = 0
            cap_olnst = 0
            cap_gla_last = 0
            cap_gatemm = 0
            cap_trqk = 0
            cap_trg1 = 0
            qst = {}
            mmcap = {}
            s1last = {}
            evlast = {}
            xnfcap = {}
            gidx = [0]

            def issue_xnf(sl2):
                Y.wait_ge(s_pe.h, s1last.get(sl2 - 1, cap_f1_pe))
                for ic in range(8):
                    s_tr.inc(Y.dma_start(
                        out=xnf[:, ic * SLAB:(ic + 1) * SLAB],
                        in_=h_xn[sl2 * SLAB:(sl2 + 1) * SLAB, ic * 128:(ic + 1) * 128],
                        transpose=True), 16)
                xnfcap[sl2] = s_tr.n

            def s1_mm(sl2, c, kind):
                g = gidx[0]
                gidx[0] += 1
                bank = pS[g % 4]
                T.wait_ge(s_tr.h, xnfcap[sl2])
                T.wait_ge(s_sc.h, bank_free[g % 4])
                for ic in range(8):
                    lhs = xnf[:, ic * SLAB + c * 128: ic * SLAB + (c + 1) * 128]
                    st, sp = (ic == 0), (ic == 7)
                    if kind == "qlo":
                        T.matmul(bank[:, 0:512], lhs, wqT(ic)[:, 0:512], start=st, stop=sp)
                    elif kind == "qhi":
                        T.matmul(bank[:, 0:512], lhs, wqT(ic)[:, 512:1024], start=st, stop=sp)
                    elif kind == "klo":
                        T.matmul(bank[:, 0:512], lhs, wkT(ic)[:, 0:512], start=st, stop=sp)
                    elif kind == "khi":
                        T.matmul(bank[:, 0:512], lhs, wkT(ic)[:, 512:1024], start=st, stop=sp)
                    elif kind == "vlo":
                        T.matmul(bank[:, 0:512], lhs, wvT(ic)[:, 0:512], start=st, stop=sp)
                    elif kind == "vhi":
                        T.matmul(bank[:, 0:512], lhs, wvT(ic)[:, 512:1024], start=st, stop=sp)
                    elif kind == "f":
                        T.matmul(bank[:, 0:16], lhs, wfT[:, ic * H:(ic + 1) * H],
                                 start=st, stop=sp)
                    else:
                        T.matmul(bank[:, 0:64], lhs, wg1T[:, ic * 64:(ic + 1) * 64],
                                 start=st, stop=sp)
                s_pe.inc(T.last)
                mmcap[(sl2, c, kind)] = s_pe.n
                s1last[sl2] = s_pe.n
                return bank

            def s1_evac(sl2, c, kind, bank):
                par = sl2 % 2
                S.wait_ge(s_pe.h, mmcap[(sl2, c, kind)])
                if kind == "qlo" and (sl2, c - 1) in qst:
                    S.wait_ge(s_st.h, qst[(sl2, c - 1)])
                if kind == "qlo":
                    S.activation(_rep(q_bf[:, :], 0, [[128, 8], [1, 64]]),
                                 bank[:, 0:512], AF.Silu)
                elif kind == "qhi":
                    S.activation(_rep(q_bf[:, :], 8 * 128, [[128, 8], [1, 64]]),
                                 bank[:, 0:512], AF.Silu)
                elif kind == "klo":
                    S.activation(_rep(k_pad[:, :], 0, [[128, 8], [1, 64]]),
                                 bank[:, 0:512], AF.Silu)
                elif kind == "khi":
                    S.activation(_rep(k_pad[:, :], 8 * 128, [[128, 8], [1, 64]]),
                                 bank[:, 0:512], AF.Silu)
                    S.activation(k_bf[par][:, c * D:(c + 1) * D],
                                 _rep(k_pad[:, :], 0, [[128, H], [1, 64]]), AF.Copy)
                elif kind == "vlo":
                    S.activation(v_sl[par][:, c * D: c * D + 512],
                                 bank[:, 0:512], AF.Silu)
                elif kind == "vhi":
                    S.activation(v_sl[par][:, c * D + 512:(c + 1) * D],
                                 bank[:, 0:512], AF.Silu)
                elif kind == "f":
                    S.wait_ge(s_ve.h, cap_e1g)
                    S.activation(e1f[:, c * 16:(c + 1) * 16], bank[:, 0:16],
                                 AF.Exp, scale=-1.0)
                else:
                    S.copy(g1_bf[:, 0:64], bank[:, 0:64])
                s_sc.inc(S.last)
                b = None
                for i in range(4):
                    if pS[i] is bank:
                        b = i
                bank_free[b] = s_sc.n
                evlast[sl2] = s_sc.n

            def s1_stores(sl2, c):
                tt = sl2 * SLAB + c * 128
                G.wait_ge(s_sc.h, evlast[sl2])
                s_st.inc(G.dma_start(out=h_q[tt:tt + 128, :], in_=q_bf[:, :]), 16)
                s_st.inc(G.dma_start(out=h_k[tt:tt + 128, :], in_=k_pad[:, :]), 16)
                s_st.inc(G.dma_start(out=h_g1[tt:tt + 128, :], in_=g1_bf[:, :]), 16)
                qst[(sl2, c)] = s_st.n

            KA = ["qlo", "qhi", "klo", "khi"]
            KB = ["vlo", "vhi", "f", "g1"]

            def s1_batchA(sl2, c):
                banks = [s1_mm(sl2, c, k) for k in KA]
                for k, b in zip(KA, banks):
                    s1_evac(sl2, c, k, b)

            def s1_batchB(sl2, c):
                banks = [s1_mm(sl2, c, k) for k in KB]
                for k, b in zip(KB, banks):
                    s1_evac(sl2, c, k, b)
                s1_stores(sl2, c)

            def issue_trans(sl2):
                Y.wait_ge(s_st.h, qst[(sl2, 3)])
                Y.wait_ge(s_pe.h, cap_gla_last)
                Y.wait_ge(s_pe.h, cap_gatemm)
                s_tr.inc(Y.dma_start(out=g1f[:, :], in_=h_g1[sl2 * SLAB:(sl2 + 1) * SLAB, :],
                                     transpose=True), 16)
                trg1 = s_tr.n
                for h in range(H):
                    s_tr.inc(Y.dma_start(out=q_fm[:, h * SLAB:(h + 1) * SLAB],
                                         in_=h_q[sl2 * SLAB:(sl2 + 1) * SLAB,
                                                 h * 128:(h + 1) * 128],
                                         transpose=True), 16)
                    s_tr.inc(Y.dma_start(out=k_fm[:, h * SLAB:(h + 1) * SLAB],
                                         in_=h_k[sl2 * SLAB:(sl2 + 1) * SLAB,
                                                 h * 128:(h + 1) * 128],
                                         transpose=True), 16)
                return trg1, s_tr.n

            def issue_gate(sl2):
                nonlocal cap_gatemm
                T.wait_ge(s_tr.h, cap_trg1)
                T.wait_ge(s_sc.h, evlast[sl2])
                for r in range(4):
                    lhs = g1f[0:64, r * 128:(r + 1) * 128]
                    T.matmul(pS[r][:, 0:512], lhs, wg2T[0:64, 0:512],
                             start=True, stop=True)
                s_pe.inc(T.last)
                S.wait_ge(s_pe.h, s_pe.n)
                cap_gts = s_ve.n
                for r in range(4):
                    if r > 0:
                        S.wait_ge(s_ve.h, cap_gts)
                    s_sc.inc(S.activation(gt_raw[:, 0:512], pS[r][:, 0:512],
                                          AF.Tanh, scale=0.5))
                    V.wait_ge(s_sc.h, s_sc.n)
                    s_ve.inc(V.tensor_scalar(gate_sl[:, r * D:r * D + 512],
                                             gt_raw[:, 0:512],
                                             0.5, 0.5, ALU.mult, ALU.add))
                    cap_gts = s_ve.n
                T.wait_ge(s_sc.h, s_sc.n)
                T.wait_ge(s_ve.h, cap_gts)
                for r in range(4):
                    lhs = g1f[0:64, r * 128:(r + 1) * 128]
                    T.matmul(pS[r][:, 0:512], lhs, wg2T[0:64, 512:1024],
                             start=True, stop=True)
                s_pe.inc(T.last)
                cap_gatemm = s_pe.n
                S.wait_ge(s_pe.h, s_pe.n)
                for r in range(4):
                    if r > 0:
                        S.wait_ge(s_ve.h, cap_gts)
                    s_sc.inc(S.activation(gt_raw[:, 0:512], pS[r][:, 0:512],
                                          AF.Tanh, scale=0.5))
                    V.wait_ge(s_sc.h, s_sc.n)
                    s_ve.inc(V.tensor_scalar(gate_sl[:, r * D + 512:(r + 1) * D],
                                             gt_raw[:, 0:512],
                                             0.5, 0.5, ALU.mult, ALU.add))
                    cap_gts = s_ve.n
                for i in range(4):
                    bank_free[i] = s_sc.n

            def issue_fknorm(sl2):
                nonlocal cap_e1g, cap_rn, cap_phcp
                par = sl2 % 2
                V.wait_ge(s_sc.h, evlast[sl2])
                s_ve.inc(V.tensor_scalar_add(e1g[:, :], e1f[:, :], 1.0))
                cap_e1g = s_ve.n
                S.wait_ge(s_ve.h, cap_e1g)
                S.wait_ge(s_ve.h, cap_flphi)
                s_sc.inc(S.activation(phif[:, :], e1g[:, :], AF.Ln))
                cap_phif = s_sc.n
                V.wait_ge(s_sc.h, cap_rn)
                for c in range(4):
                    V.tensor_mul(B1[:, 0:1024], k_bf[par][:, c * D:(c + 1) * D],
                                 k_bf[par][:, c * D:(c + 1) * D])
                    V.tensor_reduce(ssn[:, c * 16:(c + 1) * 16],
                                    B1[:, 0:1024].rearrange("p (h w) -> p h w", h=H),
                                    mybir.AxisListType.X, ALU.add)
                s_ve.inc(V.last)
                cap_ssn = s_ve.n
                S.wait_ge(s_ve.h, cap_ssn)
                S.wait_ge(s_ve.h, cap_pmul)
                S.activation(ssnb[:, :], ssn[:, :], AF.Ln, bias=eps24_col[:, 0:1])
                s_sc.inc(S.activation(rn_sl[:, :], ssnb[:, :], AF.Exp, scale=-0.5))
                cap_rn = s_sc.n
                V.wait_ge(s_sc.h, cap_rn)
                V.wait_ge(s_pe.h, cap_phimm)
                V.tensor_copy(phih[:, :], phif[:, :])
                V.tensor_copy(vj1[:, :], vj2[:, :])
                s_ve.inc(V.tensor_sub(phil[:, :], phif[:, :], phih[:, :]))
                cap_phcp = s_ve.n

            trqk_caps = {}
            issue_xnf(0)
            for c in range(4):
                s1_batchA(0, c)
                s1_batchB(0, c)
            cap_trg1, cap_trqk = issue_trans(0)
            for c in range(4):
                trqk_caps[(0, c)] = cap_trqk
            issue_gate(0)
            issue_fknorm(0)

            for sl in range(NSLAB):
                t0s = sl * SLAB
                par = sl % 2
                fill = sl + 1 < NSLAB
                if fill:
                    issue_xnf(sl + 1)
                for c in range(4):
                    if c == 0:
                        T.wait_ge(s_ve.h, cap_s3ev2)
                        T.wait_ge(s_ve.h, cap_phcp)
                    T.wait_ge(s_ve.h, cap_state)
                    T.wait_ge(s_sc.h, cap_expdv)
                    T.matmul(pBig[:, 0:16], u_bf[:, :], phih[:, c * 16:(c + 1) * 16],
                             start=True, stop=False)
                    T.matmul(pBig[:, 0:16], u_bf[:, :], phil[:, c * 16:(c + 1) * 16],
                             start=False, stop=True)
                    T.matmul(pBig[:, 16:32], lst_bf[:, :], phih[:, c * 16:(c + 1) * 16],
                             start=True, stop=False)
                    T.matmul(pBig[:, 16:32], lst_bf[:, :], phil[:, c * 16:(c + 1) * 16],
                             start=False, stop=True)
                    s_pe.inc(T.last)
                    cap_phimm = s_pe.n
                    S.wait_ge(s_pe.h, cap_phimm)
                    S.activation(expF[:, :], pBig[:, 0:16], AF.Exp, scale=-1.0)
                    s_sc.inc(S.activation(expK[:, :], pBig[:, 16:32], AF.Exp,
                                          scale=-1.0))
                    cap_phiev = s_sc.n
                    if fill:
                        banksA = [s1_mm(sl + 1, c, k) for k in KA]
                    V.wait_ge(s_sc.h, cap_phiev)
                    V.tensor_mul(dec_col[:, :], expF[:, :], expK[:, :])
                    V.tensor_mul(ekrn[:, :], expK[:, :], rn_sl[:, c * 16:(c + 1) * 16])
                    V.tensor_copy(vj1[:, :], vj2[:, :])
                    s_ve.inc(V.tensor_tensor(
                        vpp[:, :].rearrange("p (h w) -> p h w", h=H),
                        v_sl[par][:, c * D:(c + 1) * D].rearrange("p (h w) -> p h w", h=H),
                        _rep(ekrn[:, :], 0, [[1, H], [0, HD]]), ALU.mult))
                    cap_vpp = s_ve.n
                    V.wait_ge(s_pe.h, cap_oh1)
                    for h in range(H):
                        V.tensor_scalar_mul(B3[:, h * 128:(h + 1) * 128], lst_bf[:, :],
                                            phif[:, c * 16 + h:c * 16 + h + 1])
                    s_ve.inc(V.last)
                    cap_flphi = s_ve.n
                    if fill:
                        for k, b in zip(KA, banksA):
                            s1_evac(sl + 1, c, k, b)
                        banksB = [s1_mm(sl + 1, c, k) for k in KB]
                    T.wait_ge(s_ve.h, cap_flphi)
                    T.wait_ge(s_sc.h, cap_phiev)
                    for h in range(H):
                        T.matmul(pBig[:, h * 128:(h + 1) * 128],
                                 B3[:, h * 128:(h + 1) * 128], u_bf[:, :],
                                 start=True, stop=False)
                        T.matmul(pBig[:, h * 128:(h + 1) * 128],
                                 id_bf[:, :], lstpos[:, :], start=False, stop=True)
                    s_pe.inc(T.last)
                    cap_dmm = s_pe.n
                    if fill:
                        for k, b in zip(KB, banksB):
                            s1_evac(sl + 1, c, k, b)
                        s1_stores(sl + 1, c)
                    S.wait_ge(s_pe.h, cap_dmm)
                    S.wait_ge(s_ve.h, cap_pmul)
                    s_sc.inc(S.activation(B2[:, :], pBig[:, :], AF.Exp, scale=-1.0))
                    cap_expdv = s_sc.n
                    T.wait_ge(s_sc.h, cap_expdv)
                    T.wait_ge(s_tr.h, trqk_caps[(sl, c)])
                    for h in range(H):
                        T.matmul(pBig[:, h * 128:(h + 1) * 128],
                                 k_fm[0:64, h * SLAB + c * 128: h * SLAB + (c + 1) * 128],
                                 q_fm[0:64, h * SLAB + c * 128: h * SLAB + (c + 1) * 128],
                                 start=True, stop=True)
                    s_pe.inc(T.last)
                    cap_gmm = s_pe.n
                    V.wait_ge(s_pe.h, cap_gmm)
                    V.tensor_tensor(
                        B1[:, :].rearrange("p (h w) -> p h w", h=H),
                        pBig[:, :].rearrange("p (h w) -> p h w", h=H),
                        _rep(rn_sl[:, :], c * 16, [[1, H], [0, 128]]), ALU.mult)
                    s_ve.inc(V.tensor_mul(B3[:, :], B1[:, :], B2[:, :]))
                    cap_pmul = s_ve.n
                    T.wait_ge(s_ve.h, cap_pmul)
                    for hp in range(8):
                        T.matmul(pBig[:, hp * 64:(hp + 1) * 64],
                                 B3[:, hp * 128:(hp + 1) * 128],
                                 v_sl[par][:, c * D + hp * 64: c * D + (hp + 1) * 64],
                                 start=True, stop=True)
                        T.matmul(pBig[:, 512 + hp * 64: 512 + (hp + 1) * 64],
                                 q_fm[0:64, hp * SLAB + c * 128: hp * SLAB + (c + 1) * 128],
                                 S_bf[0:64, hp * 64:(hp + 1) * 64],
                                 start=True, stop=True)
                    s_pe.inc(T.last)
                    cap_oh0 = s_pe.n
                    V.wait_ge(s_pe.h, cap_oh0)
                    V.wait_ge(s_sc.h, cap_expdv)
                    if c == 0:
                        V.wait_ge(s_st.h, cap_s3st_last)
                    V.tensor_tensor(
                        B1[:, 0:512].rearrange("p (h w) -> p h w", h=8),
                        pBig[:, 512:1024].rearrange("p (h w) -> p h w", h=8),
                        _rep(expF[:, :], 0, [[1, 8], [0, 64]]), ALU.mult)
                    V.tensor_add(og[:, 0:512], B1[:, 0:512], pBig[:, 0:512])
                    s_ve.inc(V.tensor_mul(ogg[:, 0:512], og[:, 0:512],
                                          gate_sl[:, c * D: c * D + 512]))
                    cap_combh0 = s_ve.n
                    for hp in range(8):
                        h = 8 + hp
                        T.matmul(pBig[:, 1024 + hp * 64: 1024 + (hp + 1) * 64],
                                 B3[:, h * 128:(h + 1) * 128],
                                 v_sl[par][:, c * D + h * 64: c * D + (h + 1) * 64],
                                 start=True, stop=True)
                        T.matmul(pBig[:, 1536 + hp * 64: 1536 + (hp + 1) * 64],
                                 q_fm[0:64, h * SLAB + c * 128: h * SLAB + (c + 1) * 128],
                                 S_bf[0:64, h * 64:(h + 1) * 64],
                                 start=True, stop=True)
                    s_pe.inc(T.last)
                    cap_oh1 = s_pe.n
                    cap_gla_last = s_pe.n
                    T.wait_ge(s_ve.h, cap_combh0)
                    T.wait_ge(s_ve.h, cap_vpp)
                    for h in range(H):
                        T.matmul(pBig[0:64, h * 64:(h + 1) * 64],
                                 k_bf[par][:, c * D + h * 64: c * D + (h + 1) * 64],
                                 vpp[:, h * 64:(h + 1) * 64], start=True, stop=True)
                    s_pe.inc(T.last)
                    cap_kv = s_pe.n
                    V.wait_ge(s_pe.h, cap_oh1)
                    V.tensor_tensor(
                        B1[:, 0:512].rearrange("p (h w) -> p h w", h=8),
                        pBig[:, 1536:2048].rearrange("p (h w) -> p h w", h=8),
                        _rep(expF[:, :], 8, [[1, 8], [0, 64]]), ALU.mult)
                    V.tensor_add(og[:, 512:1024], B1[:, 0:512], pBig[:, 1024:1536])
                    s_ve.inc(V.tensor_mul(ogg[:, 512:1024], og[:, 512:1024],
                                          gate_sl[:, c * D + 512:(c + 1) * D]))
                    s_ve.inc(V.tensor_reduce(musum[:, :], ogg[:, :],
                                             mybir.AxisListType.X, ALU.add))
                    cap_mu = s_ve.n
                    S.wait_ge(s_ve.h, cap_mu)
                    s_sc.inc(S.activation(B2[:, 0:1024], ogg[:, :], AF.Square))
                    cap_sq = s_sc.n
                    V.wait_ge(s_sc.h, cap_sq)
                    s_ve.inc(V.tensor_reduce(ssc[:, :], B2[:, 0:1024],
                                             mybir.AxisListType.X, ALU.add))
                    cap_ssc = s_ve.n
                    S.wait_ge(s_ve.h, cap_ssc)
                    S.activation(ln_t1[:, :], musum[:, :], AF.Identity, scale=1.0 / D)
                    S.activation(ln_t2[:, :], ln_t1[:, :], AF.Square)
                    S.activation(ln_v2[:, :], ln_t2[:, :], AF.Identity, scale=-1.0)
                    S.activation(ln_nm[:, :], ln_t1[:, :], AF.Identity, scale=-1.0)
                    S.activation(ln_v3[:, :], eps_col[:, :], AF.Identity)
                    S.activation(ln_v1[:, :], ssc[:, :], AF.Identity, scale=1.0 / D,
                                 bias=ln_v2[:, 0:1])
                    S.activation(ln_t2[:, :], ln_v1[:, :], AF.Ln, bias=eps_col[:, 0:1])
                    s_sc.inc(S.activation(ln_r[:, :], ln_t2[:, :], AF.Exp, scale=-0.5))
                    cap_lnr = s_sc.n
                    V.wait_ge(s_sc.h, cap_lnr)
                    V.wait_ge(s_st.h, cap_olnst)
                    s_ve.inc(V.tensor_scalar(oln_bf[:, :], ogg[:, :], ln_nm[:, 0:1],
                                             ln_r[:, 0:1], ALU.add, ALU.mult))
                    cap_oln = s_ve.n
                    G.wait_ge(s_ve.h, cap_oln)
                    s_st.inc(G.dma_start(
                        out=h_oln[t0s + c * 128: t0s + (c + 1) * 128, :],
                        in_=oln_bf[:, :]), 16)
                    cap_olnst = s_st.n
                    V.wait_ge(s_pe.h, cap_kv)
                    V.tensor_tensor(
                        SBf[0:64, :].rearrange("p (h w) -> p h w", h=H),
                        SA_[0:64, :].rearrange("p (h w) -> p h w", h=H),
                        _rep(dec_col[:, :], 0, [[1, H], [0, HD]], pcount=64),
                        ALU.mult)
                    V.tensor_add(SA_[0:64, :], SBf[0:64, :], pBig[0:64, 0:1024])
                    s_ve.inc(V.tensor_copy(S_bf[0:64, :], SA_[0:64, :]))
                    cap_state = s_ve.n

                Y.wait_ge(s_st.h, cap_olnst)
                Y.wait_ge(s_pe.h, cap_s3mm_last)
                for ic in range(8):
                    s_tr.inc(Y.dma_start(
                        out=olnf[:, ic * SLAB:(ic + 1) * SLAB],
                        in_=h_oln[t0s:t0s + SLAB, ic * 128:(ic + 1) * 128],
                        transpose=True), 16)
                cap_trolnf = s_tr.n
                if fill:
                    cap_trg1, cap_trqk = issue_trans(sl + 1)
                    for c in range(4):
                        trqk_caps[(sl + 1, c)] = cap_trqk

                s3ev = {}
                s3st = {}
                for r in range(4):
                    off = (r % 2) * 1024
                    T.wait_ge(s_tr.h, cap_trolnf)
                    if r >= 2:
                        T.wait_ge(s_ve.h, s3ev[r - 2])
                    else:
                        T.wait_ge(s_ve.h, cap_state)
                    for ic in range(8):
                        lhs = olnf[:, ic * SLAB + r * 128: ic * SLAB + (r + 1) * 128]
                        T.matmul(pBig[:, off:off + 512], lhs, woT(ic)[:, 0:512],
                                 start=(ic == 0), stop=(ic == 7))
                        T.matmul(pBig[:, off + 512:off + 1024], lhs,
                                 woT(ic)[:, 512:1024], start=(ic == 0), stop=(ic == 7))
                    s_pe.inc(T.last)
                    cap_s3mm = s_pe.n
                    V.wait_ge(s_pe.h, cap_s3mm)
                    if r >= 1:
                        V.wait_ge(s_st.h, s3st[r - 1])
                    s_ve.inc(V.tensor_copy(og[:, :], pBig[:, off:off + 1024]))
                    s3ev[r] = s_ve.n
                    G.wait_ge(s_ve.h, s3ev[r])
                    s_st.inc(G.dma_start(
                        out=h_s[t0s + r * 128: t0s + (r + 1) * 128, :],
                        in_=og[:, :]), 16)
                    s3st[r] = s_st.n
                cap_s3mm_last = s_pe.n
                cap_s3ev2 = s3ev[2]
                cap_s3st_last = s3st[3]

                if fill:
                    issue_gate(sl + 1)
                    issue_fknorm(sl + 1)
        nc.all_engine_barrier()

        G.wait_ge(s_cc.h, s_cc.n)
        nc.all_engine_barrier()
        with contextlib.ExitStack() as f2:
            SB2 = lambda *a: f2.enter_context(nc.sbuf_tensor(*a))
            e_stream = SB2("e_stream", [128, NT * 512], BF16)
            z_stream = SB2("z_stream", [128, NT * 512], BF16)
            xn_t = [SB2("xn_t1", [128, D], BF16), SB2("xn_t2", [128, D], BF16)]
            rz_f = SB2("rz_f", [128, 512], F32)
            rz_g = [SB2("rz_g1", [128, 512], F32), SB2("rz_g2", [128, 512], F32)]
            disp = [SB2("disp1", [128, 512], BF16), SB2("disp2", [128, 512], BF16)]
            slots_fm = SB2("slots_fm", [128, 8 * NP], BF16)
            psl = [f2.enter_context(nc.psum_tensor(f"psl{i}", [128, 512], F32))
                   for i in range(8)]
            cap_rz = {}
            cap_dp = {}
            cap_xt = {}
            cap_mm = {}
            cap_ev = {}
            it = 0
            for sb in range(4):
                if sb > 0:
                    G.wait_ge(s_ve.h, cap_dp[it - 1])
                    G.wait_ge(s_sc.h, cap_rz[it - 1])
                s_ld.inc(G.dma_start(
                    out=z_stream[:, :].rearrange("p (m w) -> p m w", m=NT),
                    in_=AP(h_Z.ap().tensor, sb * 512,
                           [[NP, 128], [128 * NP, NT], [1, 512]])), 16)
                cap_zld = s_ld.n
                s_ld.inc(G.dma_start(
                    out=e_stream[:, :].rearrange("p (m w) -> p m w", m=NT),
                    in_=AP(h_E.ap().tensor, sb * 512,
                           [[NP, 128], [128 * NP, NT], [1, 512]])), 16)
                cap_eld = s_ld.n
                for mt in range(NT):
                    par = it % 2
                    xt = xn_t[par]
                    dp = disp[par]
                    rzg = rz_g[par]
                    if it >= 2:
                        Y.wait_ge(s_pe.h, cap_mm[it - 2])
                    s_st.inc(Y.dma_start(out=xt[:, :],
                                         in_=h_xn[mt * 128:(mt + 1) * 128, :]), 16)
                    cap_xt[it] = s_st.n
                    if it % 2 == 0:
                        V.wait_ge(s_ld.h, cap_zld)
                        V.reciprocal(rzg[:, :], z_stream[:, mt * 512:(mt + 1) * 512])
                        s_ve.inc(V.last)
                    else:
                        S.wait_ge(s_ld.h, cap_zld)
                        if it >= 2:
                            S.wait_ge(s_ve.h, cap_dp[it - 2])
                        S.activation(rz_f[:, :], z_stream[:, mt * 512:(mt + 1) * 512],
                                     AF.Ln)
                        s_sc.inc(S.activation(rzg[:, :], rz_f[:, :], AF.Exp, scale=-1.0))
                        cap_rz[it] = s_sc.n
                        V.wait_ge(s_sc.h, cap_rz[it])
                    V.wait_ge(s_ld.h, cap_eld)
                    if it >= 2:
                        V.wait_ge(s_pe.h, cap_mm[it - 2])
                    s_ve.inc(V.tensor_mul(dp[:, :],
                                          e_stream[:, mt * 512:(mt + 1) * 512], rzg[:, :]))
                    cap_dp[it] = s_ve.n
                    T.wait_ge(s_ve.h, cap_dp[it])
                    T.wait_ge(s_st.h, cap_xt[it])
                    if mt == 0 and sb > 0:
                        T.wait_ge(s_ve.h, cap_ev[sb - 1])
                    for dc in range(8):
                        T.matmul(psl[dc][:, :], xt[:, dc * 128:(dc + 1) * 128], dp[:, :],
                                 start=(mt == 0), stop=(mt == NT - 1))
                    s_pe.inc(T.last)
                    cap_mm[it] = s_pe.n
                    it += 1
                V.wait_ge(s_pe.h, cap_mm[it - 1])
                for dc in range(8):
                    V.tensor_copy(slots_fm[:, dc * NP + sb * 512: dc * NP + (sb + 1) * 512],
                                  psl[dc][:, :])
                s_ve.inc(V.last)
                cap_ev[sb] = s_ve.n
            Y.wait_ge(s_ve.h, s_ve.n)
            for dc in range(8):
                s_st.inc(Y.dma_start(out=h_slots[dc * 128:(dc + 1) * 128, :],
                                     in_=slots_fm[:, dc * NP:(dc + 1) * NP]), 16)
            G.wait_ge(s_st.h, s_st.n)
            s_cc.inc(G.collective_compute(
                "AllGather", ALU.bypass, replica_groups=[list(range(NCORES))],
                ins=[h_slots.ap()], outs=[h_slots_all.ap()]))

        nc.all_engine_barrier()

        with contextlib.ExitStack() as f3:
            SB3 = lambda *a: f3.enter_context(nc.sbuf_tensor(*a))
            sa = SB3("sa", [128, 8 * 2 * NL * 128], BF16)
            b_rows = SB3("b_rows", [1, NL * D], BF16)
            s_ld.inc(G.dma_start(out=b_rows[0:1, :],
                                 in_=AP(bffn_in.ap().tensor, 0,
                                        [[NL * D, 1], [1, NL * D]])), 16)
            wfa = SB3("wfa", [128, 8 * D], BF16)
            wfb = SB3("wfb", [128, 8 * D], BF16)
            y_ev = SB3("y_ev", [128, D], BF16)
            py1 = f3.enter_context(nc.psum_tensor("py1", [128, D], F32))
            py2 = f3.enter_context(nc.psum_tensor("py2", [128, D], F32))

            def load_wffn(nl):
                wb = wfa if nl % 2 == 0 else wfb
                s_ld.inc(G.dma_start(
                    out=wb[:, :].rearrange("p (dc e) -> p dc e", dc=8),
                    in_=AP(wffn_in.ap().tensor, nl * D * D,
                           [[D, 128], [128 * D, 8], [1, D]])), 16)
                ld_marks.append(s_ld.n)

            ld_marks = []
            load_wffn(0)
            load_wffn(1)
            G.wait_ge(s_cc.h, s_cc.n)
            with G.register("sar") as sareg:
                G.reg_load(sareg, dynoffs_in[0:1, 0:1])
                off = G.snap(sareg)
                s_ld.inc(G.dma_start(
                    out=h_smine.ap(),
                    in_=AP(h_slots_all.ap().tensor, off,
                           [[NP, NCORES * D], [1, 256]]),
                    bounds_check=None), 16)
            G.wait_ge(s_ld.h, s_ld.n)
            for b in range(NCORES):
                mh, rb = b // 4, b % 4
                for dc in range(8):
                    src_rows = h_smine[b * D + dc * 128: b * D + (dc + 1) * 128, :]
                    src3 = src_rows.rearrange("p (n c) -> p n c", c=32)
                    s_ld.inc(G.dma_start(
                        out=AP(sa[:, :].tensor,
                               (dc * 2 + mh) * NL * 128 + rb * 32,
                               [[8 * 2 * NL * 128, 128], [128, NL], [1, 32]]),
                        in_=src3), 16)
            cap_sa = s_ld.n
            T.wait_ge(s_ld.h, cap_sa)
            pe_f3 = s_pe.n
            for nl in range(NL):
                wb = wfa if nl % 2 == 0 else wfb
                if nl + 2 < NL:
                    G.wait_ge(s_pe.h, pe_f3 + 2 * nl + 2)
                    load_wffn(nl + 2)
                for mh in range(2):
                    py = py1 if mh == 0 else py2
                    T.wait_ge(s_ld.h, ld_marks[nl])
                    T.wait_ge(s_st.h, s_st.n)
                    if not (nl == 0 and mh == 0):
                        T.wait_ge(s_ve.h, s_ve.n)
                    lbase = (0 * 2 + mh) * NL + nl
                    for dc in range(8):
                        lhs = sa[:, ((dc * 2 + mh) * NL + nl) * 128:
                                 ((dc * 2 + mh) * NL + nl) * 128 + 128]
                        T.matmul(py[:, 0:512], lhs, wb[:, dc * D:dc * D + 512],
                                 start=(dc == 0), stop=False)
                        T.matmul(py[:, 512:1024], lhs, wb[:, dc * D + 512:(dc + 1) * D],
                                 start=(dc == 0), stop=False)
                    T.matmul(py[:, 0:512], ones_bf[0:1, :],
                             b_rows[0:1, nl * D:nl * D + 512], start=False, stop=True)
                    T.matmul(py[:, 512:1024], ones_bf[0:1, :],
                             b_rows[0:1, nl * D + 512:(nl + 1) * D], start=False, stop=True)
                    s_pe.inc(T.last)
                    V.wait_ge(s_pe.h, s_pe.n)
                    if not (nl == 0 and mh == 0):
                        V.wait_ge(s_st.h, s_st.n)
                    V.tensor_copy(y_ev[:, :], py[:, :])
                    s_ve.inc(V.last)
                    Y.wait_ge(s_ve.h, s_ve.n)
                    s_st.inc(Y.dma_start(
                        out=h_y[nl * 256 + mh * 128: nl * 256 + (mh + 1) * 128, :],
                        in_=y_ev[:, :]), 16)
            G.wait_ge(s_st.h, s_st.n)
            s_cc.inc(G.collective_compute(
                "AllGather", ALU.bypass, replica_groups=[list(range(NCORES))],
                ins=[h_y.ap()], outs=[h_y_all.ap()]))

        nc.all_engine_barrier()

        with contextlib.ExitStack() as f4:
            SB4 = lambda *a: f4.enter_context(nc.sbuf_tensor(*a))
            y_b = SB4("y_b", [128, 16 * D], BF16)
            et = SB4("et", [128, 16 * 1024], BF16)
            g_bf = SB4("g_bf", [128, D], BF16)
            s_t = SB4("s_t", [128, D], BF16)
            x_t = SB4("x_t", [128, D], F32)
            t2 = SB4("t2", [128, D], BF16)
            outf = SB4("outf", [128, D], F32)
            po1 = f4.enter_context(nc.psum_tensor("po1", [128, D], F32))
            po2 = f4.enter_context(nc.psum_tensor("po2", [128, D], F32))
            MS = min(1024, N)
            for npc in range(16):
                s_tr.inc(Y.dma_start(
                    out=et[:, npc * 1024:npc * 1024 + MS],
                    in_=h_E[0:MS, npc * 128:(npc + 1) * 128],
                    transpose=True), 16)
            G.wait_ge(s_cc.h, s_cc.n)
            with G.register("ybr") as ybreg:
                G.reg_load(ybreg, dynoffs_in[0:1, 1:2])
                bboff = G.snap(ybreg)
                for i in range(NCORES):
                    view = h_y_all[i * 2048:(i + 1) * 2048, :].rearrange(
                        "(nl bb p) e -> nl bb (p e)", nl=NL, bb=NCORES)
                    s_ld.inc(G.dma_start(
                        out=h_ymine[i * 256:(i + 1) * 256, :].rearrange(
                            "(nl p) e -> nl (p e)", nl=NL).unsqueeze(1),
                        in_=view[:, bass.ds(bboff, 1), :],
                        bounds_check=None), 16)
            G.wait_ge(s_ld.h, s_ld.n)
            for npc in range(16):
                s_ld.inc(G.dma_start(
                    out=y_b[:, npc * D:(npc + 1) * D],
                    in_=h_ymine[npc * 128:(npc + 1) * 128, :]), 16)
            for ms in range(N // MS):
                if ms > 0:
                    Y.wait_ge(s_pe.h, s_pe.n)
                    for npc in range(16):
                        s_tr.inc(Y.dma_start(
                            out=et[:, npc * 1024:npc * 1024 + MS],
                            in_=h_E[ms * MS:(ms + 1) * MS, npc * 128:(npc + 1) * 128],
                            transpose=True), 16)
                for rr in range(MS // 128):
                    mt = ms * (MS // 128) + rr
                    po = po1 if mt % 2 == 0 else po2
                    T.wait_ge(s_tr.h, s_tr.n)
                    T.wait_ge(s_ld.h, s_ld.n)
                    if mt >= 2:
                        T.wait_ge(s_sc.h, s_sc.n)
                    for npc in range(16):
                        lhs = et[:, npc * 1024 + rr * 128: npc * 1024 + (rr + 1) * 128]
                        T.matmul(po[:, 0:512], lhs, y_b[:, npc * D:npc * D + 512],
                                 start=(npc == 0), stop=(npc == 15))
                        T.matmul(po[:, 512:1024], lhs, y_b[:, npc * D + 512:(npc + 1) * D],
                                 start=(npc == 0), stop=(npc == 15))
                    s_pe.inc(T.last)
                    S.wait_ge(s_pe.h, s_pe.n)
                    if mt >= 1:
                        S.wait_ge(s_ve.h, s_ve.n)
                    s_sc.inc(S.activation(g_bf[:, :], po[:, :], AF.Silu,
                                          scale=rc_all[:, mt:mt + 1]))
                    s_st.inc(Y.dma_start(out=s_t[:, :],
                                         in_=h_s[mt * 128:(mt + 1) * 128, :]), 16)
                    s_st.inc(Y.dma_start(out=x_t[:, :],
                                         in_=x_in[mt * 128:(mt + 1) * 128, :]), 16)
                    V.wait_ge(s_sc.h, s_sc.n)
                    V.wait_ge(s_st.h, s_st.n)
                    V.tensor_mul(t2[:, :], s_t[:, :], g_bf[:, :])
                    s_ve.inc(V.tensor_add(outf[:, :], x_t[:, :], t2[:, :]))
                    Y.wait_ge(s_ve.h, s_ve.n)
                    s_out.inc(Y.dma_start(out=out_t[mt * 128:(mt + 1) * 128, :],
                                          in_=outf[:, :]), 16)
            Y.wait_ge(s_out.h, s_out.n)

        if dbg:
            nc.all_engine_barrier()
            with contextlib.ExitStack() as dd:
                cp = dd.enter_context(nc.sbuf_tensor("dbgcp", [128, NP], BF16))
                cpf = dd.enter_context(nc.sbuf_tensor("dbgcf", [128, NP], F32))
                for name, dst in dbg.items():
                    hs = dbg_src[name]
                    wid = int(hs.shape[1])
                    for mt in range(int(hs.shape[0]) // 128):
                        Y.wait_ge(s_ve.h, s_ve.n)
                        s_st.inc(Y.dma_start(out=cp[:, 0:wid],
                                             in_=hs[mt * 128:(mt + 1) * 128, :]), 16)
                        V.wait_ge(s_st.h, s_st.n)
                        s_ve.inc(V.tensor_copy(cpf[:, 0:wid], cp[:, 0:wid]))
                        Y.wait_ge(s_ve.h, s_ve.n)
                        s_out.inc(Y.dma_start(out=dst[mt * 128:(mt + 1) * 128, :],
                                              in_=cpf[:, 0:wid]), 16)
                Y.wait_ge(s_out.h, s_out.n)

        nc.all_engine_barrier()
        Y.wait_ge(s_out.h, s_out.n)

    return nc


_NC_CACHE = {}


def kernel(L0_attn_x, L0_attn_Wq, L0_attn_Wk, L0_attn_Wv, L0_attn_Wf,
           L0_attn_Wg1, L0_attn_Wg2, L0_attn_ln_w, L0_attn_Wo,
           L0_ffn_phi, L0_ffn_W, L0_ffn_b):
    N = int(np.asarray(L0_attn_x).shape[1])
    key = (N, tuple(DEBUG_OUTS))
    if key not in _NC_CACHE:
        _NC_CACHE[key] = build_nc(N=N, debug=tuple(DEBUG_OUTS))
    nc = _NC_CACHE[key]
    x = np.ascontiguousarray(np.asarray(L0_attn_x, dtype=np.float32))
    com = {
        "wq": np.ascontiguousarray(np.asarray(L0_attn_Wq, np.float32)),
        "wk": np.ascontiguousarray(np.asarray(L0_attn_Wk, np.float32)),
        "wv": np.ascontiguousarray(np.asarray(L0_attn_Wv, np.float32)),
        "wo": np.ascontiguousarray(np.asarray(L0_attn_Wo, np.float32)),
        "wf": np.ascontiguousarray(np.asarray(L0_attn_Wf, np.float32)),
        "wg1": np.ascontiguousarray(np.asarray(L0_attn_Wg1, np.float32)),
        "wg2": np.ascontiguousarray(np.asarray(L0_attn_Wg2, np.float32)),
        "lnw": np.ascontiguousarray(np.asarray(L0_attn_ln_w, np.float32)),
        "phi": np.ascontiguousarray(np.asarray(L0_ffn_phi, np.float32).reshape(D, NP)),
    }
    W = np.asarray(L0_ffn_W, np.float32)
    B = np.asarray(L0_ffn_b, np.float32)
    in_maps = []
    for c in range(NCORES):
        m = dict(com)
        m["x"] = np.ascontiguousarray(x[c])
        m["wffn"] = np.ascontiguousarray(W[c * NL:(c + 1) * NL])
        m["bffn"] = np.ascontiguousarray(B[c * NL:(c + 1) * NL])
        m["dynoffs"] = np.array([[256 * c, c]], dtype=np.int32)
        in_maps.append(m)
    res = run_bass_kernel_spmd(nc, in_maps, core_ids=list(range(NCORES)))
    out = np.stack([np.asarray(res.results[c]["out"]) for c in range(NCORES)], axis=0)
    return out
